# version 22
# baseline (speedup 1.0000x reference)
import contextlib
import numpy as np
import ml_dtypes
import concourse.bass as bass
import concourse.mybir as mybir
from concourse.bass_utils import run_bass_kernel_spmd

F32 = mybir.dt.float32
F32R = mybir.dt.float32r
BF16 = mybir.dt.bfloat16
I32 = mybir.dt.int32
AF = mybir.ActivationFunctionType
ALU = mybir.AluOpType
AX = mybir.AxisListType
BF16NP = ml_dtypes.bfloat16

N_CORES = 8
SEQ = 2048
DM = 1024
NG = DM // 128
NCH = SEQ // 512
HEADS = 16
DH = 64
WIN = 256
DI = 2730
HID_L1 = 3072
INV127 = 1.0 / 127.0
DEPTH_FFN = 6


class R:
    __slots__ = ("name", "birth")
    def __init__(self, name, birth=None):
        self.name = name
        self.birth = birth
    def __repr__(self):
        return f"R({self.name})"

class _Op:
    __slots__ = ("idx", "engine", "fn", "is_dma", "dma_idx", "deps",
                 "milestone", "mnum", "tag", "sync_all", "small_out")

COMPUTE_ENGINES = ("tensor", "vector", "scalar", "gpsimd")
ALL_ENGINES = ("sync", "tensor", "vector", "scalar", "gpsimd")

SERIAL_DEBUG = False
SIM_SAFE = False

class Prog:
    def __init__(self, nc):
        self.nc = nc
        self.ops = []
        self._lw = {}
        self._rd = {}
        self._dma_count = dict.fromkeys(ALL_ENGINES, 0)

    def add(self, engine, fn, reads=(), writes=(), dma=False, tag="",
            extra_deps=(), sync_all=False, small_out=False):
        idx = len(self.ops)
        deps = set(extra_deps)
        if SERIAL_DEBUG and idx > 0:
            deps.add(idx - 1)
        for r in tuple(reads) + tuple(writes):
            if getattr(r, "birth", None):
                for br in r.birth:
                    w = self._lw.get(br)
                    if w is not None:
                        deps.add(w)
                    for rr in self._rd.get(br, ()):
                        deps.add(rr)
                r.birth = None
        for r in reads:
            w = self._lw.get(r)
            if w is not None:
                deps.add(w)
            self._rd.setdefault(r, []).append(idx)
        for r in writes:
            w = self._lw.get(r)
            if w is not None:
                deps.add(w)
            for rr in self._rd.get(r, ()):
                deps.add(rr)
            self._lw[r] = idx
            self._rd[r] = []
        deps.discard(idx)
        op = _Op()
        op.idx, op.engine, op.fn, op.is_dma = idx, engine, fn, dma
        op.dma_idx = None
        if dma:
            op.dma_idx = self._dma_count[engine]
            self._dma_count[engine] += 1
        op.deps, op.milestone, op.mnum, op.tag = deps, False, 0, tag
        op.sync_all = sync_all
        op.small_out = small_out
        self.ops.append(op)
        return idx

    def dma(self, queue, out, in_, reads=(), writes=(), tag="", **kw):
        return self.add(queue,
                        lambda e, o=out, i=in_, kw=kw: e.dma_start(out=o, in_=i, **kw),
                        reads=reads, writes=writes, dma=True, tag=tag)

    def mm(self, ps, lhsT, rhs, start, stop, reads=(), writes=(), tag=""):
        return self.add("tensor",
                        lambda e, a=ps, b=lhsT, c=rhs, s=start, t=stop:
                            e.matmul(a, b, c, start=s, stop=t),
                        reads=reads, writes=writes, tag=tag)

    def trans(self, ps, in_, ident, reads=(), writes=(), tag=""):
        return self.add("tensor",
                        lambda e, a=ps, b=in_, c=ident: e.transpose(a, b, c),
                        reads=reads, writes=writes, tag=tag)

    def act(self, out, in_, func, bias=None, scale=None, accum=None,
            reads=(), writes=(), tag=""):
        kw = {}
        if bias is not None:
            kw["bias"] = bias
        if scale is not None:
            kw["scale"] = scale
        if accum is not None:
            kw["accum_out"] = accum
        sa = self._isap(bias) or self._isap(scale)
        return self.add("scalar",
                        lambda e, o=out, i=in_, f=func, kw=kw:
                            e.activation(out=o, in_=i, func=f, **kw),
                        reads=reads, writes=writes, tag=tag, sync_all=sa,
                        small_out=self._small(out) or accum is not None)

    def tt(self, out, in0, in1, op, engine="vector", reads=(), writes=(), tag=""):
        return self.add(engine,
                        lambda e, o=out, a=in0, b=in1, p=op:
                            e.tensor_tensor(out=o, in0=a, in1=b, op=p),
                        reads=reads, writes=writes, tag=tag,
                        small_out=self._small(out))

    @staticmethod
    def _isap(v):
        return v is not None and not isinstance(v, (int, float))

    @staticmethod
    def _small(out):
        try:
            return out.free_size() < 128
        except Exception:
            return True

    def ts(self, out, in0, s1, s2=None, op0=ALU.mult, op1=ALU.add,
           engine="vector", reads=(), writes=(), tag=""):
        sa = self._isap(s1) or self._isap(s2)
        if s2 is None:
            return self.add(engine,
                            lambda e, o=out, a=in0, x=s1, p0=op0:
                                e.tensor_single_scalar(out=o, in_=a, scalar=x,
                                                       op=p0),
                            reads=reads, writes=writes, tag=tag, sync_all=sa,
                            small_out=self._small(out))
        return self.add(engine,
                        lambda e, o=out, a=in0, x=s1, y=s2, p0=op0, p1=op1:
                            e.tensor_scalar(out=o, in0=a, scalar1=x, scalar2=y,
                                            op0=p0, op1=p1),
                        reads=reads, writes=writes, tag=tag, sync_all=sa,
                        small_out=self._small(out))

    def stt(self, out, in0, scalar, in1, op0, op1, accum=None,
            engine="vector", reads=(), writes=(), tag=""):
        return self.add(engine,
                        lambda e, o=out, a=in0, s=scalar, b=in1, p0=op0, p1=op1,
                               ac=accum:
                            e.scalar_tensor_tensor(out=o, in0=a, scalar=s, in1=b,
                                                   op0=p0, op1=p1, accum_out=ac),
                        reads=reads, writes=writes, tag=tag,
                        sync_all=self._isap(scalar),
                        small_out=self._small(out) or accum is not None)

    def red(self, out, in_, op=ALU.add, negate=False, reads=(), writes=(), tag=""):
        return self.add("vector",
                        lambda e, o=out, i=in_, p=op, n=negate:
                            e.tensor_reduce(out=o, in_=i, axis=AX.X, op=p, negate=n),
                        reads=reads, writes=writes, tag=tag, small_out=True)

    def copy(self, out, in_, engine="vector", reads=(), writes=(), tag=""):
        return self.add(engine,
                        lambda e, o=out, i=in_: e.tensor_copy(out=o, in_=i),
                        reads=reads, writes=writes, tag=tag,
                        small_out=self._small(out))

    def recip(self, out, in_, reads=(), writes=(), tag=""):
        return self.add("vector",
                        lambda e, o=out, i=in_: e.reciprocal(out=o, in_=i),
                        reads=reads, writes=writes, tag=tag, small_out=True)

    def scan(self, out, d0, d1, initial, reads=(), writes=(), tag=""):
        return self.add("vector",
                        lambda e, o=out, a=d0, b=d1, it=initial:
                            e.tensor_tensor_scan(out=o, data0=a, data1=b,
                                                 initial=it, op0=ALU.mult,
                                                 op1=ALU.add),
                        reads=reads, writes=writes, tag=tag,
                        sync_all=self._isap(initial),
                        small_out=self._small(out))

    def final_fence(self):
        last = {}
        for op in self.ops:
            if op.is_dma:
                last[op.engine] = op.idx
        self.add("sync", lambda e: e.nop(), extra_deps=set(last.values()),
                 tag="final_fence")

    def emit(self, st):
        nc = self.nc
        for op in self.ops:
            for d in op.deps:
                dep = self.ops[d]
                if (not dep.is_dma) and (SIM_SAFE or op.sync_all
                                         or dep.small_out
                                         or dep.engine != op.engine):
                    dep.milestone = True
        counts = dict.fromkeys(ALL_ENGINES, 0)
        for op in self.ops:
            if op.milestone and not op.is_dma:
                counts[op.engine] += 1
                op.mnum = counts[op.engine]
        engines_used = []
        for e in ALL_ENGINES:
            if any(op.engine == e for op in self.ops):
                engines_used.append(e)
        NPOOL = 12
        esem, qpool = {}, {}
        for e in engines_used:
            if any((not o.is_dma) and o.milestone and o.engine == e
                   for o in self.ops):
                esem[e] = st.enter_context(nc.semaphore("es_" + e))
            ndma = self._dma_count[e]
            if ndma:
                qpool[e] = [st.enter_context(nc.semaphore(f"qs_{e}{i}"))
                            for i in range(min(NPOOL, ndma))]
        def dma_sem_val(dep):
            pool = qpool[dep.engine]
            k = len(pool)
            return pool[dep.dma_idx % k], 16 * (dep.dma_idx // k + 1)
        sched = {e: [] for e in engines_used}
        hw = {}
        for op in self.ops:
            waits = {}
            if op.is_dma:
                pool = qpool[op.engine]
                k = len(pool)
                cyc = op.dma_idx // k
                if cyc > 0:
                    waits[pool[op.dma_idx % k]] = 16 * cyc
            for d in sorted(op.deps):
                dep = self.ops[d]
                if dep.is_dma:
                    sem, val = dma_sem_val(dep)
                elif (dep.engine == op.engine and not SIM_SAFE
                      and not op.sync_all and not dep.small_out):
                    continue
                else:
                    sem, val = esem[dep.engine], dep.mnum
                if waits.get(sem, -1) < val:
                    waits[sem] = val
            final = []
            for sem, val in waits.items():
                key = (op.engine, sem)
                if hw.get(key, 0) >= val:
                    continue
                hw[key] = val
                final.append((sem, val))
            inc = None
            if op.is_dma:
                sem, val = dma_sem_val(op)
                inc = (sem, 16)
            elif op.milestone:
                inc = (esem[op.engine], 1)
            sched[op.engine].append((final, op.fn, inc, op.tag))
        block = st.enter_context(nc.Block())
        def runner(entries):
            def run(eng):
                for waits, fn, inc, tag in entries:
                    for sem, val in waits:
                        eng.wait_ge(sem, val)
                    ins = fn(eng)
                    if inc is not None:
                        if ins is None:
                            ins = eng.nop()
                        ins.then_inc(inc[0], inc[1])
            return run
        for e in engines_used:
            getattr(block, e)(runner(sched[e]))


class Region:
    __slots__ = ("arena", "off", "n", "name", "dt", "ratio", "res_list", "R")
    def __init__(self, arena, off, n, name, dt, ratio, birth):
        self.arena, self.off, self.n, self.name = arena, off, n, name
        self.dt, self.ratio = dt, ratio
        self.R = R(name, birth=tuple(birth))
        self.res_list = [self.R]
    def sub(self, name):
        r = R(name, birth=self.R.birth)
        self.res_list.append(r)
        return r
    def subs(self, name, n):
        return [self.sub(f"{name}{i}") for i in range(n)]
    def v(self, p0=0, np_=128, c0=0, nc_=None):
        r = self.ratio
        if nc_ is None:
            nc_ = self.n // r - c0
        assert 0 <= c0 and (c0 + nc_) * r <= self.n, \
            (self.name, c0, nc_, self.n, r)
        ap = self.arena.ap[p0:p0 + np_,
                           self.off + c0 * r:self.off + (c0 + nc_) * r]
        if self.dt is not None:
            ap = ap.bitcast(self.dt)
        return ap
    def free(self):
        self.arena.free_region(self)

_DT_BYTES = {F32: 4, F32R: 4, BF16: 2, I32: 4}

class Arena:
    def __init__(self, st, nc, name, dtype, n_elems):
        self.h = st.enter_context(nc.sbuf_tensor(name, [128, n_elems], dtype))
        self.ap = self.h.ap()
        self.name = name
        self.dtype = dtype
        self.n = n_elems
        self._free = [(0, n_elems)]
        self._tombs = []
        self.hiwater = 0

    def alloc(self, n, name="", dt=None):
        ratio = 1
        if dt is not None and dt != self.dtype:
            ratio = _DT_BYTES[dt] // _DT_BYTES[self.dtype]
            assert ratio >= 1
        else:
            dt = None
        nn = (n * ratio + 15) & ~15
        for i, (o, sz) in enumerate(self._free):
            if sz >= nn:
                if sz == nn:
                    self._free.pop(i)
                else:
                    self._free[i] = (o + nn, sz - nn)
                self.hiwater = max(self.hiwater, o + nn)
                birth = []
                keep = []
                for (to, tn, tres) in self._tombs:
                    if to < o + nn and o < to + tn:
                        birth.extend(tres)
                        if to < o:
                            keep.append((to, o - to, tres))
                        if to + tn > o + nn:
                            keep.append((o + nn, to + tn - (o + nn), tres))
                    else:
                        keep.append((to, tn, tres))
                self._tombs = keep
                return Region(self, o, nn, name, dt, ratio, birth)
        raise MemoryError(f"{self.name}: cannot alloc {nn} cols for {name} "
                          f"(free={self._free})")

    def free_region(self, region):
        self._tombs.append((region.off, region.n, tuple(region.res_list)))
        self._free.append((region.off, region.n))
        self._free.sort()
        merged = []
        for seg in self._free:
            if merged and merged[-1][0] + merged[-1][1] == seg[0]:
                merged[-1] = (merged[-1][0], merged[-1][1] + seg[1])
            else:
                merged.append(list(seg))
        self._free = [tuple(s) for s in merged]


def _t(a):
    return np.ascontiguousarray(np.asarray(a).T)

def _rs_feat(a):
    a = np.asarray(a)
    f = a.shape[-1]
    ncol = -(-f // 128)
    lead = a.shape[:-1]
    ar = a.reshape(-1, f)
    out = np.zeros((ar.shape[0], 128, ncol), a.dtype)
    for i in range(ar.shape[0]):
        pad = np.zeros(128 * ncol, a.dtype)
        pad[:f] = ar[i]
        out[i] = pad.reshape(ncol, 128).T
    return out.reshape(lead + (128, ncol))

def host_prepare(inputs):
    g = {}
    g["l0a_wqkvT"] = _t(inputs["L0_attn_W_qkv"]).astype(np.float32)
    g["l0a_woutT"] = _t(inputs["L0_attn_W_out"]).astype(np.float32)
    g["wqkv"] = np.ascontiguousarray(inputs["L0_ffn_Wqkv"], np.float32)
    g["wo"] = np.ascontiguousarray(inputs["L0_ffn_Wo"], np.float32)
    g["w1"] = np.ascontiguousarray(inputs["L0_ffn_W1"], np.float32)
    w2 = np.asarray(inputs["L0_ffn_W2"], np.float32)
    w2p = np.zeros((6, 2816, 1024), np.float32)
    w2p[:, :DI, :] = w2
    g["w2"] = w2p
    g["qs"] = np.ascontiguousarray(inputs["L0_ffn_q_scale"], np.float32).reshape(6, 1, 64)
    g["ks"] = np.ascontiguousarray(inputs["L0_ffn_k_scale"], np.float32).reshape(6, 1, 64)
    g["ln_g"] = _rs_feat(np.asarray(inputs["L0_ffn_ln_g"], np.float32))
    g["ln_b"] = _rs_feat(np.asarray(inputs["L0_ffn_ln_b"], np.float32))
    b1 = np.asarray(inputs["L0_ffn_b1"], np.float32)
    g["b1a"] = _rs_feat(b1[:, :DI])
    g["b1g"] = _rs_feat(b1[:, DI:])
    g["b2"] = _rs_feat(np.asarray(inputs["L0_ffn_b2"], np.float32))
    g["l1_ap"] = np.asarray(inputs["L1_attn_a_params"], np.float32).reshape(1, 256)
    g["l1_bwT"] = _t(inputs["L1_attn_B_w"]).astype(np.float32)
    g["l1_cwT"] = _t(inputs["L1_attn_C_w"]).astype(np.float32)
    g["l1_D"] = _rs_feat(np.asarray(inputs["L1_attn_D"], np.float32))
    g["l1_wgT"] = _t(inputs["L1_ffn_w_gate"]).astype(np.int32)
    g["l1_wuT"] = _t(inputs["L1_ffn_w_up"]).astype(np.int32)
    g["l1_wdT"] = _t(inputs["L1_ffn_w_down"]).astype(np.int32)
    x = np.asarray(inputs["L0_attn_x"], np.float32)
    xs = [np.ascontiguousarray(x[b].T) for b in range(N_CORES)]
    return g, xs


def make_consts():
    c = {}
    c["ident_bf"] = np.eye(128, dtype=np.float32)
    c["ident_f32"] = np.eye(128, dtype=np.float32)
    c["ones_r_col"] = np.ones((128, 1), np.float32)
    c["ones_r_row"] = np.ones((1, 128), np.float32)
    c["ones_bf_col"] = np.ones((128, 1), np.float32)
    c["ones_bf_row"] = np.ones((1, 128), np.float32)
    c["ones_f32_row64"] = np.ones((1, 64), np.float32)
    c["ones_f32_row"] = np.ones((1, 128), np.float32)
    c["ones_f32_big"] = np.ones((128, 512), np.float32)
    c["zeros_r"] = np.zeros((128, 16), np.float32)
    c["eps_col"] = np.full((128, 1), 1e-5, np.float32)
    half = np.arange(0, DH, 2, dtype=np.float64)
    inv_freq = 1.0 / 10000.0 ** (half / DH)
    t = np.arange(SEQ, dtype=np.float64)
    freqs = np.concatenate([t[:, None] * inv_freq[None, :]] * 2, -1)
    base_scale = (half + 0.4 * DH) / (1.4 * DH)
    power = (t - SEQ // 2) / (WIN // 2)
    scale = np.concatenate([base_scale[None, :] ** power[:, None]] * 2, -1)
    cos, sin = np.cos(freqs), np.sin(freqs)
    sgn = np.concatenate([-np.ones(32), np.ones(32)])[None, :]
    def tile_major(tab):
        return np.ascontiguousarray(
            tab.reshape(16, 128, 64).transpose(1, 0, 2).reshape(128, 16 * 64)
        ).astype(np.float32)
    c["rot_t1q"] = tile_major(cos * scale)
    c["rot_t2q"] = tile_major(sin * scale * sgn)
    c["rot_t1k"] = tile_major(cos / scale)
    c["rot_t2k"] = tile_major(sin / scale * sgn)
    jj = np.arange(512)[None, :]
    j2 = np.arange(256)[None, :]
    for s in range(2):
        ii = (np.arange(128) + 128 * s)[:, None]
        c[f"mask_s{s}"] = np.where((jj >= ii) & (jj <= ii + 256), 0.0,
                                   -1e30).astype(np.float32)
        c[f"mask0_s{s}"] = np.where(j2 <= ii, 0.0, -1e30).astype(np.float32)
    qq = np.arange(256)[None, :]
    for ch in range(4):
        j = (np.arange(128) + 128 * ch)[:, None]
        c[f"bmask_c{ch}"] = ((qq >= j - 256) & (qq <= j)).astype(np.float32)
    for ch in range(2):
        j = (np.arange(128) + 128 * ch)[:, None]
        c[f"bmask0_c{ch}"] = (qq >= j).astype(np.float32)
    cc = np.arange(256)[None, :]
    for s in range(2):
        ii = (np.arange(128) + 128 * s)[:, None]
        c[f"cmask_s{s}"] = np.where(cc <= ii, 0.125, 0.0).astype(np.float32)
    M = np.zeros((512, 256), np.float64)
    jv = np.arange(512)
    for m in range(1, 257):
        M[:, m - 1] += -np.sin(2 * np.pi * jv * m / 512)
    for m in range(257, 512):
        M[:, 512 - m] += np.sin(2 * np.pi * jv * m / 512)
    c["l1_M"] = np.ascontiguousarray(
        M.reshape(4, 128, 256).transpose(1, 0, 2).reshape(128, 4 * 256)
    ).astype(np.float32)
    return c

_CONSTS = make_consts()

_CONST_DT = {
    "ident_bf": BF16, "ident_f32": F32,
    "ones_r_col": F32R, "ones_r_row": F32R,
    "ones_bf_col": BF16, "ones_bf_row": BF16,
    "ones_f32_row64": F32, "ones_f32_row": F32, "ones_f32_big": F32,
    "zeros_r": F32R, "eps_col": F32,
    "rot_t1q": BF16, "rot_t2q": BF16, "rot_t1k": BF16, "rot_t2k": BF16,
    "mask_s0": BF16, "mask_s1": BF16, "mask0_s0": F32, "mask0_s1": F32,
    "bmask_c0": BF16, "bmask_c1": BF16, "bmask_c2": BF16, "bmask_c3": BF16,
    "bmask0_c0": BF16, "bmask0_c1": BF16,
    "cmask_s0": BF16, "cmask_s1": BF16,
    "l1_M": F32,
}


ARENA_R_ELEMS = 25700
ARENA_B_ELEMS = 36300

class Build:
    def __init__(self, phases):
        self.phases = phases
        self.nc = bass.Bass("TRN2", target_bir_lowering=False, debug=False)
        self.st = contextlib.ExitStack()
        self.P = Prog(self.nc)
        self.shared_inputs = {}
        self._uid = 0

    def uid(self):
        self._uid += 1
        return self._uid

    def din(self, name, shape, dt):
        return self.nc.dram_tensor(name, list(shape), dt, kind="ExternalInput").ap()

    def dout(self, name, shape, dt):
        return self.nc.dram_tensor(name, list(shape), dt, kind="ExternalOutput").ap()

    def dint(self, name, shape, dt):
        return self.nc.dram_tensor(name, list(shape), dt, kind="Internal").ap()

    def win(self, name, arr, dt=None):
        if dt is None:
            dt = {np.dtype(np.float32): F32, np.dtype(np.int32): I32}[arr.dtype]
        self.shared_inputs[name] = arr
        return self.din(name, arr.shape, dt)

    def setup(self):
        nc, st = self.nc, self.st
        self.ar = Arena(st, nc, "arena_r", F32R, ARENA_R_ELEMS)
        self.ab = Arena(st, nc, "arena_b", BF16, ARENA_B_ELEMS)
        self.sf = st.enter_context(nc.sbuf_tensor("sf_small", [128, 1024], F32))
        self.sfR = R("sf_small")
        self.sfRp = [R(f"sfp{j}") for j in range(4)]
        if "l1attn" in self.phases or "test" in self.phases:
            self.tabf = st.enter_context(
                nc.sbuf_tensor("tabf", [128, 2560], F32))
        self.pb = [st.enter_context(nc.psum_tensor(f"pb{i}", [128, 512], F32))
                   for i in range(6)]
        self.tb = [st.enter_context(nc.psum_tensor(f"tb{i}", [128, 1024], BF16))
                   for i in range(2)]
        self.pbR = [R(f"pb{i}") for i in range(6)]
        self.tbR = [R(f"tb{i}") for i in range(2)]
        self.c = {}
        self.cR = R("consts")
        for name in self.const_names():
            arr = _CONSTS[name]
            dt = _CONST_DT[name]
            npdt = {F32: np.float32, F32R: np.float32, BF16: BF16NP}[dt]
            self.shared_inputs["c_" + name] = np.ascontiguousarray(arr.astype(npdt))
            hin = self.din("c_" + name, arr.shape,
                           F32 if dt == F32R else dt)
            t = st.enter_context(nc.sbuf_tensor("sc_" + name, list(arr.shape), dt))
            src = hin.bitcast(F32R) if dt == F32R else hin
            self.P.dma("sync", t.ap()[:, :], src, writes=[self.cR],
                       tag="c_" + name)
            self.c[name] = t.ap()
        return self

    def const_names(self):
        names = ["ident_bf", "ident_f32", "ones_r_col", "ones_r_row",
                 "ones_bf_col", "ones_bf_row", "ones_f32_row64",
                 "ones_f32_row", "ones_f32_big", "zeros_r", "eps_col"]
        if "l0ffn" in self.phases:
            names += ["rot_t1q", "rot_t2q", "rot_t1k", "rot_t2k",
                      "mask_s0", "mask_s1",
                      "bmask_c0", "bmask_c1", "bmask_c2", "bmask_c3"]
        if "l0head" in self.phases:
            names += ["cmask_s0", "cmask_s1"]
        if "l1attn" in self.phases:
            names += ["l1_M"]
        return names

    def alloc_x(self):
        self.x = self.ar.alloc(NG * SEQ, "x")
        self.xR = self.x.subs("x", NG)

    def x_view(self, g, c0=0, n=SEQ, p0=0, np_=128):
        if getattr(self, "x_alias_xi", False):
            return self.xi_data(g, c0, n, p0, np_)
        return self.x.v(p0, np_, g * SEQ + c0, n)

    def load_x(self, src_ap):
        s = src_ap.rearrange("(g p) t -> p g t", p=128)
        self.P.dma("sync",
                   self.x.v(0, 128, 0, NG * SEQ).rearrange(
                       "p (g t) -> p g t", t=SEQ),
                   s.bitcast(F32R), writes=list(self.xR), tag="load_x")

    def store_x(self, dst_ap, tag="store_x"):
        if getattr(self, "x_alias_xi", False):
            return self.store_xi_as(dst_ap, tag=tag)
        d = dst_ap.rearrange("(g p) t -> p g t", p=128)
        return self.P.dma(
            "sync", d,
            self.x.v().rearrange("p (g t) -> p g t", t=SEQ).bitcast(F32),
            reads=list(self.xR), tag=tag)

    def phase_combine_split(self, x_h, a_h, f_h):
        P = self.P
        srcs = [h.rearrange("(g p) t -> p g t", p=128)
                for h in (x_h, a_h, f_h)]
        bufs = [self.ar.alloc(512, f"cmb{i}") for i in range(4)]
        for g_ in range(NG):
            for ch in range(NCH):
                i2 = ((g_ * NCH + ch) % 2) * 2
                bA, bB = bufs[i2], bufs[i2 + 1]
                sl = (slice(None), g_, slice(ch * 512, (ch + 1) * 512))
                P.dma("sync", bA.v(), srcs[0][sl].bitcast(F32R),
                      writes=[bA.R], tag=f"cmbx{g_}_{ch}")
                P.dma("sync", bB.v(), srcs[1][sl].bitcast(F32R),
                      writes=[bB.R], tag=f"cmba{g_}_{ch}")
                P.tt(self.x_view(g_, ch * 512, 512), bA.v().bitcast(F32),
                     bB.v().bitcast(F32), ALU.add,
                     reads=[bA.R, bB.R], writes=[self.xR[g_]],
                     tag=f"cmb1{g_}_{ch}")
                P.dma("sync", bA.v(), srcs[2][sl].bitcast(F32R),
                      writes=[bA.R], tag=f"cmbf{g_}_{ch}")
                P.tt(self.x_view(g_, ch * 512, 512),
                     self.x_view(g_, ch * 512, 512).bitcast(F32),
                     bA.v().bitcast(F32), ALU.add,
                     reads=[bA.R, self.xR[g_]], writes=[self.xR[g_]],
                     tag=f"cmb2{g_}_{ch}")
        for b_ in bufs:
            b_.free()

    def phase_combine_fused(self, x_h, aC3, aCR):
        P = self.P
        xsrc = x_h.rearrange("(g p) t -> p g t", p=128)
        bufs = [self.ar.alloc(512, f"cmf{i}") for i in range(4)]
        for g_ in range(NG):
            for ch in range(NCH):
                i2 = ((g_ * NCH + ch) % 2) * 2
                bA, bB = bufs[i2], bufs[i2 + 1]
                sl = (slice(None), g_, slice(ch * 512, (ch + 1) * 512))
                P.dma("sync", bA.v(), xsrc[sl].bitcast(F32R),
                      writes=[bA.R], tag=f"cmfx{g_}_{ch}")
                P.dma("sync", bB.v(), aC3[sl].bitcast(F32R),
                      reads=[aCR], writes=[bB.R], tag=f"cmfa{g_}_{ch}")
                P.tt(bA.v(), bA.v().bitcast(F32), bB.v().bitcast(F32),
                     ALU.add, reads=[bA.R, bB.R], writes=[bA.R],
                     tag=f"cmf1{g_}_{ch}")
                P.tt(self.xi_data(g_, ch * 512, 512),
                     self.xi_data(g_, ch * 512, 512).bitcast(F32),
                     bA.v().bitcast(F32), ALU.add,
                     reads=[bA.R, self.xiR[g_]], writes=[self.xiR[g_]],
                     tag=f"cmf2{g_}_{ch}")
        for b_ in bufs:
            b_.free()
        self.x_alias_xi = True
        self.xR = self.xiR

    def alloc_xi(self):
        self.xi = self.ar.alloc(NG * (SEQ + 1), "xi")
        self.xiR = self.xi.subs("xi", NG)
        z = self.c["zeros_r"]
        for g in range(NG):
            self.P.dma("sync", self.xi.v(0, 128, g * (SEQ + 1), 1), z[:, 0:1],
                       reads=[self.cR], writes=[self.xiR[g]], tag=f"xi_z{g}")

    def xi_data(self, g, c0=0, n=SEQ, p0=0, np_=128):
        return self.xi.v(p0, np_, g * (SEQ + 1) + 1 + c0, n)

    def xi_shift(self, g, c0=0, n=SEQ, p0=0, np_=128):
        return self.xi.v(p0, np_, g * (SEQ + 1) + c0, n)

    class ResidentSrc:
        def __init__(self, view_fn, res_list):
            self.view_fn, self.res = view_fn, res_list
        def chunk(self, b, ch):
            return [(self.view_fn(g, ch * 512, 512), self.res[g])
                    for g in range(NG)]
        def release(self):
            pass

    class HbmSrc:
        def __init__(self, b, dram_ap, name):
            self.ap3 = dram_ap.rearrange("(g p) t -> p g t", p=128)
            self.name = name
            self.bufs = [b.ar.alloc(512, f"{name}{g}") for g in range(NG)]
            self.res = [buf.R for buf in self.bufs]
        def chunk(self, b, ch):
            out = []
            for g in range(NG):
                b.P.dma("sync", self.bufs[g].v(),
                        self.ap3[:, g, ch * 512:(ch + 1) * 512].bitcast(F32R),
                        writes=[self.res[g]], tag=f"{self.name}{ch}_{g}")
                out.append((self.bufs[g].v(), self.res[g]))
            return out
        def release(self):
            for r in self.bufs:
                r.free()

    def emit_ln(self, src, cb, tag):
        P, c = self.P, self.c
        u = self.uid()
        sq = [self.ar.alloc(512, f"lnq{u}_{i}") for i in range(2)]
        sqR = [b.R for b in sq]
        rows = self.ab.alloc(5 * 512, f"lnrows{u}", dt=F32)
        rowR = rows.R
        SX, SQM, MU, T1, T2 = range(5)
        rv = lambda i: rows.v(0, 1, i * 512, 512)
        rstd = self.ar.alloc(512, f"lnrstd{u}")
        bprime = self.ar.alloc(512, f"lnbp{u}")
        rstdR = rstd.R
        bprime.res_list.append(rstdR)
        a_b = self.ab.alloc(512, f"lnab{u}", dt=F32)
        b_b = self.ab.alloc(512, f"lnbb{u}", dt=F32)
        t1b = [self.ab.alloc(512, f"lnt1{u}_{i}", dt=F32) for i in range(2)]
        abR, bbR = a_b.R, b_b.R
        t1R = [b.R for b in t1b]
        for ch in range(NCH):
            tiles = src.chunk(self, ch)
            psS, psSR = self.pb[0].ap()[0:1, :], self.pbR[0]
            psQ, psQR = self.pb[1].ap()[0:1, :], self.pbR[1]
            for g in range(NG):
                ap_, res_ = tiles[g]
                P.mm(psS, c["ones_r_col"][:, :], ap_, g == 0, g == NG - 1,
                     reads=[res_, self.cR], writes=[psSR],
                     tag=f"{tag}sx{ch}_{g}")
            for g in range(NG):
                ap_, res_ = tiles[g]
                b = sq[g % 2]
                P.act(b.v(), ap_, AF.Square, reads=[res_],
                      writes=[sqR[g % 2]], tag=f"{tag}q{ch}_{g}")
                P.mm(psQ, c["ones_r_col"][:, :], b.v(), g == 0, g == NG - 1,
                     reads=[sqR[g % 2], self.cR], writes=[psQR],
                     tag=f"{tag}qm{ch}_{g}")
            P.act(rv(SX), psS, AF.Copy, reads=[psSR], writes=[rowR],
                  tag=f"{tag}evS{ch}")
            P.act(rv(SQM), psQ, AF.Copy, reads=[psQR], writes=[rowR],
                  tag=f"{tag}evQ{ch}")
            P.ts(rv(MU), rv(SX), 1.0 / DM, None, reads=[rowR], writes=[rowR],
                 tag=f"{tag}mu{ch}")
            P.stt(rv(T1), rv(MU), -1.0, rv(MU), ALU.mult, ALU.mult,
                  reads=[rowR], writes=[rowR], tag=f"{tag}mu2{ch}")
            P.stt(rv(T2), rv(SQM), 1.0 / DM, rv(T1), ALU.mult, ALU.add,
                  reads=[rowR], writes=[rowR], tag=f"{tag}var{ch}")
            P.act(rv(T1), rv(T2), AF.Ln, bias=c["eps_col"][0:1, :],
                  reads=[rowR, self.cR], writes=[rowR], tag=f"{tag}ln{ch}")
            P.act(rstd.v(0, 1), rv(T1), AF.Exp, scale=-0.5,
                  reads=[rowR], writes=[rstdR], tag=f"{tag}rstd{ch}")
            P.stt(bprime.v(0, 1), rv(MU), -1.0, rstd.v(0, 1).bitcast(F32),
                  ALU.mult, ALU.mult,
                  reads=[rowR, rstdR], writes=[rstdR], tag=f"{tag}bp{ch}")
            psA, psAR = self.pb[2].ap()[:, :], self.pbR[2]
            psB, psBR = self.pb[3].ap()[:, :], self.pbR[3]
            P.mm(psA, c["ones_r_row"][:, :], rstd.v(0, 1), True, True,
                 reads=[rstdR, self.cR], writes=[psAR], tag=f"{tag}bcA{ch}")
            P.mm(psB, c["ones_r_row"][:, :], bprime.v(0, 1), True, True,
                 reads=[rstdR, self.cR], writes=[psBR], tag=f"{tag}bcB{ch}")
            P.act(a_b.v(), psA, AF.Copy, reads=[psAR], writes=[abR],
                  tag=f"{tag}abe{ch}")
            P.act(b_b.v(), psB, AF.Copy, reads=[psBR], writes=[bbR],
                  tag=f"{tag}bbe{ch}")
            for g in range(NG):
                ap_, res_ = tiles[g]
                t1 = t1b[g % 2]
                P.tt(t1.v(), ap_.bitcast(F32), a_b.v(), ALU.mult,
                     reads=[res_, abR], writes=[t1R[g % 2]],
                     tag=f"{tag}t1_{ch}_{g}")
                cb(g, ch, ap_, res_, t1.v(), t1R[g % 2], b_b.v(), bbR)
        for reg in (rows, rstd, bprime, a_b, b_b, t1b[0], t1b[1],
                    sq[0], sq[1]):
            reg.free()
        src.release()

    def phase_ln0(self, x_dram):
        src = self.HbmSrc(self, x_dram, "x0s")
        def cb(g, ch, tile, tileR, t1, t1R, b_b, bbR):
            self.P.tt(self.xi_data(g, ch * 512, 512), t1, b_b, ALU.add,
                      reads=[t1R, bbR], writes=[self.xiR[g]],
                      tag=f"ln0o{ch}_{g}")
        self.emit_ln(src, cb, "ln0")

    def phase_ln1(self, d_rs_ap, dR):
        self.xnb = self.ab.alloc(NG * SEQ, "xnb")
        self.xnbR = self.xnb.subs("xnb", NG)
        t2 = [self.ab.alloc(512, f"ln1t2_{i}", dt=F32) for i in range(2)]
        t2R = [b.R for b in t2]
        def cb(g, ch, tile, tileR, t1, t1R, b_b, bbR):
            P = self.P
            t = t2[g % 2]
            P.tt(t.v(), t1, b_b, ALU.add, reads=[t1R, bbR],
                 writes=[t2R[g % 2]], tag=f"ln1t2_{ch}_{g}")
            P.copy(self.xnb_view(g, ch * 512, 512), t.v(),
                   reads=[t2R[g % 2]], writes=[self.xnbR[g]],
                   tag=f"ln1xn_{ch}_{g}")
            P.stt(self.x_view(g, ch * 512, 512), t.v(), d_rs_ap[:, g:g + 1],
                  self.x_view(g, ch * 512, 512).bitcast(F32),
                  ALU.mult, ALU.add,
                  reads=[t2R[g % 2], dR, self.xR[g]], writes=[self.xR[g]],
                  tag=f"ln1D_{ch}_{g}")
        self.emit_ln(self.ResidentSrc(self.x_view, self.xR), cb, "ln1")

    def xnb_view(self, g, c0=0, n=SEQ, p0=0, np_=128):
        return self.xnb.v(p0, np_, g * SEQ + c0, n)

    def phase_l0ffn(self, hs):
        glu_st = self.dint("glu_st", (2816, SEQ), BF16)
        self._glu3 = glu_st.rearrange("(g p) t -> p g t", p=128)
        self._gluR = R("glu_st")
        w2st = self.dint("w2bf_st", (NG, 128, 2816), BF16)
        self._w2st = w2st
        self._w2stR = R("w2bf_st")
        for i in range(DEPTH_FFN):
            pl, qsb, ftab = self._l0f_params(i, hs)
            self._l0f_attn(i, hs, ftab)
            ftab.free()
            qsb.free()
            self._l0f_ffn(i, hs, pl)
            pl.free()

    def _l0f_params(self, i, hs):
        P, c = self.P, self.c
        pl = self.ab.alloc(68, f"pl{i}", dt=F32)
        for name, c0, n in (("lng", 0, 8), ("lnb", 8, 16), ("b1a", 16, 38),
                            ("b1g", 38, 60), ("b2", 60, 68)):
            P.dma("sync", pl.v(0, 128, c0, n - c0), hs[name][i],
                  writes=[pl.R], tag=f"pl_{name}{i}")
        sfa = self.sf.ap()
        P.dma("sync", sfa[0:1, 576:640], hs["qs"][i], writes=[self.sfR],
              tag=f"pl_qs{i}")
        P.dma("sync", sfa[0:1, 640:704], hs["ks"][i], writes=[self.sfR],
              tag=f"pl_ks{i}")
        qsb = self.ab.alloc(256, f"qsb{i}")
        for idx, (src0, dst0) in enumerate(((576, 0), (640, 128))):
            psq, psqR = self.pb[0].ap()[:, 0:64], self.pbR[0]
            P.mm(psq, c["ones_f32_row"][:, :], sfa[0:1, src0:src0 + 64],
                 True, True, reads=[self.sfR, self.cR], writes=[psqR],
                 tag=f"qsbc{i}_{idx}")
            P.act(qsb.v(0, 128, dst0, 64), psq, AF.Copy,
                  reads=[psqR], writes=[qsb.R], tag=f"qsbe{i}_{idx}")
            P.act(qsb.v(0, 128, dst0 + 64, 32), self.pb[0].ap()[:, 32:64],
                  AF.Copy, reads=[psqR], writes=[qsb.R], tag=f"qsw1{i}_{idx}")
            P.act(qsb.v(0, 128, dst0 + 96, 32), self.pb[0].ap()[:, 0:32],
                  AF.Copy, reads=[psqR], writes=[qsb.R], tag=f"qsw2{i}_{idx}")
        ftab = self.ab.alloc(4 * 1024, f"ftab{i}")
        for o, (tab, qcol) in enumerate((("rot_t1q", 0), ("rot_t2q", 64),
                                         ("rot_t1k", 128), ("rot_t2k", 192))):
            P.tt(ftab.v(0, 128, o * 1024, 1024).rearrange(
                     "p (a c) -> p a c", c=64),
                 c[tab][:, :].rearrange("p (a c) -> p a c", c=64),
                 qsb.v(0, 128, qcol, 64).rearrange(
                     "p (a c) -> p a c", a=1).broadcast_to((128, 16, 64)),
                 ALU.mult, reads=[qsb.R, self.cR], writes=[ftab.R],
                 tag=f"ftab{i}_{o}")
        return pl, qsb, ftab

    def _l0f_attn(self, i, hs, ftab):
        P, c = self.P, self.c
        vtm = self.ab.alloc(6 * 1024, f"vtm{i}")
        qfm = self.ab.alloc(NG * 512, f"qfm{i}")
        kfm = self.ab.alloc(NG * 768, f"kfm{i}")
        wst = [self.ar.alloc(NG * 512, f"l0fw{i}_{j}") for j in range(2)]
        rA = [self.ab.alloc(1024, f"rA{i}_{j}") for j in range(2)]
        rB = [self.ab.alloc(1024, f"rB{i}_{j}") for j in range(2)]
        tbank = self.tb[0].ap()
        sfa = self.sf.ap()
        rdn = sfa[0:1, 64:320]
        rdn2 = sfa[0:1, 320:576]
        for q4 in range(4):
            t0 = q4 * 512
            qtm = self.ab.alloc(4 * 1024, f"qtm{i}_{q4}")
            ktm = self.ab.alloc(4 * 1024, f"ktm{i}_{q4}")
            if q4 > 0:
                P.copy(vtm.v(0, 128, 0, 2048), vtm.v(0, 128, 4096, 2048),
                       reads=[vtm.R], writes=[vtm.R], tag=f"vh{i}_{q4}")
                for f in range(NG):
                    P.copy(kfm.v(0, 128, f * 768, 256),
                           kfm.v(0, 128, f * 768 + 512, 256),
                           reads=[kfm.R], writes=[kfm.R],
                           tag=f"kh{i}_{q4}_{f}")
            for wc in range(6):
                w = wst[wc % 2]
                P.dma("sync",
                      w.v(0, 128, 0, NG * 512).rearrange(
                          "p (g f) -> p g f", f=512),
                      hs["wqkv"][i][:, wc * 512:(wc + 1) * 512].rearrange(
                          "(g p) f -> p g f", p=128).bitcast(F32R),
                      writes=[w.R], tag=f"fw{i}_{q4}_{wc}")
                for tt in range(4):
                    ps, psR = self.pb[tt % 2].ap()[:, :], self.pbR[tt % 2]
                    for k in range(NG):
                        lhsT = (self.xi_data(k, t0 + tt * 128, 128) if k < 4
                                else self.xi_shift(k, t0 + tt * 128, 128))
                        P.mm(ps, lhsT, w.v(0, 128, k * 512, 512),
                             k == 0, k == NG - 1,
                             reads=[self.xiR[k], w.R], writes=[psR],
                             tag=f"fqkv{i}_{q4}_{wc}_{tt}_{k}")
                    dst = (qtm, ktm, vtm)[wc // 2]
                    base = (tt * 1024 if wc < 4 else (2 + tt) * 1024)
                    dv = dst.v(0, 128, base + (wc % 2) * 512, 512)
                    P.act(dv, ps, AF.Copy, reads=[psR], writes=[dst.R],
                          tag=f"fev{i}_{q4}_{wc}_{tt}")
            for qk in range(2):
                src = qtm if qk == 0 else ktm
                dstfm = qfm if qk == 0 else kfm
                fb = ftab.v
                f1c = 0 if qk == 0 else 2048
                f2c = 1024 if qk == 0 else 3072
                for tt in range(4):
                    tg = q4 * 4 + tt
                    tpar = (qk * 4 + tt) % 2
                    teng = "vector" if tpar == 0 else "gpsimd"
                    rAv, rBv = rA[tpar], rB[tpar]
                    sb0 = 704 + 96 * tpar
                    t3 = lambda ap: ap.rearrange("p (a c) -> p a c", c=64)
                    sv = src.v(0, 128, tt * 1024, 1024)
                    P.act(rAv.v(), sv, AF.Square, reads=[src.R],
                          writes=[rAv.R], tag=f"rsq{i}_{q4}_{qk}_{tt}")
                    P.red(sfa[:, sb0:sb0 + 16].rearrange("p (a c) -> p a c", c=1),
                          t3(rAv.v()), ALU.add, reads=[rAv.R],
                          writes=[self.sfRp[tpar]],
                          tag=f"rss{i}_{q4}_{qk}_{tt}")
                    P.act(sfa[:, sb0 + 16:sb0 + 32], sfa[:, sb0:sb0 + 16],
                          AF.Sqrt,
                          reads=[self.sfRp[tpar]], writes=[self.sfRp[tpar]],
                          tag=f"rst{i}_{q4}_{qk}_{tt}")
                    P.ts(sfa[:, sb0 + 32:sb0 + 48],
                         sfa[:, sb0 + 16:sb0 + 32], 1e-12, None,
                         op0=ALU.max, reads=[self.sfRp[tpar]],
                         writes=[self.sfRp[tpar]],
                         tag=f"rmx{i}_{q4}_{qk}_{tt}")
                    P.recip(sfa[:, sb0 + 48:sb0 + 64],
                            sfa[:, sb0 + 32:sb0 + 48],
                            reads=[self.sfRp[tpar]],
                            writes=[self.sfRp[tpar]],
                            tag=f"rrc{i}_{q4}_{qk}_{tt}")
                    if qk == 0:
                        P.ts(sfa[:, sb0 + 64:sb0 + 80],
                             sfa[:, sb0 + 48:sb0 + 64], 8.0, None,
                             reads=[self.sfRp[tpar]],
                             writes=[self.sfRp[tpar]],
                             tag=f"rq8{i}_{q4}_{tt}")
                    P.tt(rAv.v().rearrange("p (a c) -> p a c", c=64),
                         t3(sv),
                         ftab.v(0, 128, f1c + tg * 64, 64).rearrange(
                             "p (a c) -> p a c", a=1).broadcast_to(
                                 (128, 16, 64)),
                         ALU.mult, engine=teng,
                         reads=[src.R, ftab.R], writes=[rAv.R],
                         tag=f"rA{i}_{q4}_{qk}_{tt}")
                    sv3 = src.v(0, 128, tt * 1024, 1024).rearrange(
                        "p (a h c) -> p a h c", h=2, c=32)
                    rB3 = rBv.v().rearrange("p (a h c) -> p a h c", h=2, c=32)
                    P.tt(rB3[:, :, 0, :], sv3[:, :, 1, :],
                         ftab.v(0, 128, f2c + tg * 64, 32).rearrange(
                             "p (a c) -> p a c", a=1).broadcast_to(
                                 (128, 16, 32)),
                         ALU.mult, engine=teng,
                         reads=[src.R, ftab.R], writes=[rBv.R],
                         tag=f"rB1{i}_{q4}_{qk}_{tt}")
                    P.tt(rB3[:, :, 1, :], sv3[:, :, 0, :],
                         ftab.v(0, 128, f2c + tg * 64 + 32, 32).rearrange(
                             "p (a c) -> p a c", a=1).broadcast_to(
                                 (128, 16, 32)),
                         ALU.mult, engine=teng,
                         reads=[src.R, ftab.R], writes=[rBv.R],
                         tag=f"rB2{i}_{q4}_{qk}_{tt}")
                    P.tt(rAv.v(), rAv.v(), rBv.v(), ALU.add, engine=teng,
                         reads=[rAv.R, rBv.R], writes=[rAv.R],
                         tag=f"rC{i}_{q4}_{qk}_{tt}")
                    rncol = (sb0 + 64) if qk == 0 else (sb0 + 48)
                    P.tt(rBv.v().rearrange("p (a c) -> p a c", c=64),
                         t3(rAv.v()),
                         sfa[:, rncol:rncol + 16].rearrange(
                             "p (a c) -> p a c", c=1).broadcast_to(
                                 (128, 16, 64)),
                         ALU.mult, engine=teng,
                         reads=[rAv.R, self.sfRp[tpar]],
                         writes=[rBv.R],
                         tag=f"rD{i}_{q4}_{qk}_{tt}")
                    tbankp = self.tb[tpar].ap()
                    for f in range(NG):
                        tb_ap = tbankp[:, (f % 8) * 128:(f % 8) * 128 + 128]
                        P.trans(tb_ap, rBv.v(0, 128, f * 128, 128),
                                c["ident_bf"][:, :],
                                reads=[rBv.R, self.cR],
                                writes=[self.tbR[tpar]],
                                tag=f"rT{i}_{q4}_{qk}_{tt}_{f}")
                        dcol = (f * 512 + tt * 128 if qk == 0
                                else f * 768 + 256 + tt * 128)
                        if tpar == 0:
                            P.copy(dstfm.v(0, 128, dcol, 128), tb_ap,
                                   reads=[self.tbR[tpar]], writes=[dstfm.R],
                                   tag=f"rTe{i}_{q4}_{qk}_{tt}_{f}")
                        else:
                            P.act(dstfm.v(0, 128, dcol, 128), tb_ap,
                                  AF.Copy,
                                  reads=[self.tbR[tpar]], writes=[dstfm.R],
                                  tag=f"rTe{i}_{q4}_{qk}_{tt}_{f}")
            qtm.free()
            ktm.free()
            atout = self.ab.alloc(NG * 512, f"ato{i}_{q4}")
            pn = [self.ab.alloc(1024, f"pn{i}_{q4}_{j}") for j in range(2)]
            pT = [self.ab.alloc(4 * 256, f"pT{i}_{q4}_{j}") for j in range(2)]
            insts = [(h, wl) for h in range(HEADS) for wl in range(2)]

            def emit_A(j):
                h, wl = insts[j]
                f = h // 2
                pb_ = (h % 2) * 64
                w = q4 * 2 + wl
                W2_ = j % 2
                pnW = pn[W2_]
                bk1a, bk1b = 2 * W2_, 2 * W2_ + 1
                kw = 256 if w == 0 else 512
                kcol = f * 768 + (256 if w == 0 else wl * 256)
                qcol = f * 512 + wl * 256
                nmc = 880 + 4 * W2_
                rdc = 888 + 2 * W2_
                aeng = "vector" if W2_ == 0 else "gpsimd"
                for s in range(2):
                    bk = bk1a if s == 0 else bk1b
                    P.mm(self.pb[bk].ap()[:, 0:kw],
                         qfm.v(pb_, 64, qcol + s * 128, 128),
                         kfm.v(pb_, 64, kcol, kw), True, True,
                         reads=[qfm.R, kfm.R],
                         writes=[self.pbR[bk]],
                         tag=f"m1{i}_{w}_{h}_{s}")
                    mask = (c[f"mask_s{s}"][:, 256:512] if w == 0
                            else c[f"mask_s{s}"][:, 0:kw])
                    P.tt(self.pb[bk].ap()[:, 0:kw],
                         self.pb[bk].ap()[:, 0:kw],
                         mask, ALU.add,
                         reads=[self.pbR[bk], self.cR],
                         writes=[self.pbR[bk]], tag=f"mk{i}_{w}_{h}_{s}")
                    P.red(sfa[:, nmc + s:nmc + s + 1],
                          self.pb[bk].ap()[:, 0:kw],
                          ALU.max, negate=True,
                          reads=[self.pbR[bk]],
                          writes=[self.sfRp[2 + W2_]],
                          tag=f"nm{i}_{w}_{h}_{s}")
                    P.act(pnW.v(0, 128, s * 512, kw),
                          self.pb[bk].ap()[:, 0:kw], AF.Exp,
                          bias=sfa[:, nmc + s:nmc + s + 1],
                          accum=sfa[:, nmc + 2 + s:nmc + 3 + s],
                          reads=[self.pbR[bk], self.sfRp[2 + W2_]],
                          writes=[pnW.R, self.sfRp[2 + W2_]],
                          tag=f"pe{i}_{w}_{h}_{s}")
                    P.recip(sfa[:, rdc + s:rdc + s + 1],
                            sfa[:, nmc + 2 + s:nmc + 3 + s],
                            reads=[self.sfRp[2 + W2_]],
                            writes=[self.sfRp[2 + W2_]],
                            tag=f"rd{i}_{w}_{h}_{s}")
                    P.tt(pnW.v(0, 128, s * 512, kw),
                         pnW.v(0, 128, s * 512, kw),
                         sfa[:, rdc + s:rdc + s + 1].broadcast_to(
                             (128, kw)),
                         ALU.mult, engine=aeng,
                         reads=[pnW.R, self.sfRp[2 + W2_]],
                         writes=[pnW.R], tag=f"pw{i}_{w}_{h}_{s}")

            def emit_B1(j):
                h, wl = insts[j]
                w = q4 * 2 + wl
                W2_ = j % 2
                pnW, pTW = pn[W2_], pT[W2_]
                nk = 2 if w == 0 else 4
                tbk = self.tb[W2_].ap()
                for s in range(2):
                    for ck in range(nk):
                        P.trans(tbk[:, ck * 256 + s * 128:
                                    ck * 256 + s * 128 + 128],
                                pnW.v(0, 128, s * 512 + ck * 128, 128),
                                c["ident_bf"][:, :],
                                reads=[pnW.R, self.cR],
                                writes=[self.tbR[W2_]],
                                tag=f"pt{i}_{w}_{h}_{s}_{ck}")
                P.act(pTW.v(0, 128, 0, nk * 256),
                      tbk[:, 0:nk * 256], AF.Copy,
                      reads=[self.tbR[W2_]], writes=[pTW.R],
                      tag=f"pv{i}_{w}_{h}")

            def emit_B2(j):
                h, wl = insts[j]
                f = h // 2
                pb_ = (h % 2) * 64
                w = q4 * 2 + wl
                W2_ = j % 2
                pTW = pT[W2_]
                bkv = 4 + W2_
                nk = 2 if w == 0 else 4
                for ck in range(nk):
                    P.mm(self.pb[bkv].ap()[pb_:pb_ + 64, 0:256],
                         vtm.v(0, 128, (wl * 2 + ck + (2 if w == 0 else 0))
                               * 1024 + h * 64, 64),
                         pTW.v(0, 128, ck * 256, 256),
                         ck == 0, ck == nk - 1,
                         reads=[vtm.R, pTW.R], writes=[self.pbR[bkv]],
                         tag=f"av{i}_{w}_{h}_{ck}")
                P.act(atout.v(pb_, 64, f * 512 + wl * 256, 256),
                      self.pb[bkv].ap()[pb_:pb_ + 64, 0:256],
                      AF.Copy, reads=[self.pbR[bkv]], writes=[atout.R],
                      tag=f"ao{i}_{w}_{h}")

            emit_A(0)
            emit_A(1)
            for j in range(len(insts)):
                emit_B1(j)
                if j + 2 < len(insts):
                    emit_A(j + 2)
                emit_B2(j)
            for reg in pn + pT:
                reg.free()
            wstg = self.ar.alloc(NG * 128, f"wos{i}_{q4}")
            wob = self.ab.alloc(NG * 128, f"wobb{i}_{q4}")
            for mo in range(NG):
                P.dma("sync",
                      wstg.v(0, 128, 0, 1024).rearrange(
                          "p (g m) -> p g m", m=128),
                      hs["wo"][i][:, mo * 128:(mo + 1) * 128].rearrange(
                          "(g p) m -> p g m", p=128).bitcast(F32R),
                      writes=[wstg.R], tag=f"wo{i}_{q4}_{mo}")
                P.copy(wob.v(), wstg.v().bitcast(F32), reads=[wstg.R],
                       writes=[wob.R], tag=f"woc{i}_{q4}_{mo}")
                ps, psR = self.pb[0].ap()[:, :], self.pbR[0]
                for k in range(NG):
                    P.mm(ps, wob.v(0, 128, k * 128, 128),
                         atout.v(0, 128, k * 512, 512), k == 0, k == NG - 1,
                         reads=[wob.R, atout.R], writes=[psR],
                         tag=f"woy{i}_{q4}_{mo}_{k}")
                P.tt(self.xi_data(mo, t0, 512), ps,
                     self.xi_data(mo, t0, 512).bitcast(F32), ALU.add,
                     reads=[psR, self.xiR[mo]], writes=[self.xiR[mo]],
                     tag=f"wor{i}_{q4}_{mo}")
            for reg in (atout, wstg, wob):
                reg.free()
        for reg in [vtm, qfm, kfm] + rA + rB + list(wst):
            reg.free()

    def _l0f_ffn(self, i, hs, pl):
        P, c = self.P, self.c
        NM1 = 22
        w1s = [self.ar.alloc(NG * 128, f"w1s{i}_{j}") for j in range(4)]

        def w1_load(m):
            wdt = 128 if m < NM1 - 1 else DI - 128 * (NM1 - 1)
            sA, sG = w1s[2 * (m % 2)], w1s[2 * (m % 2) + 1]
            for (st_, cofs) in ((sA, m * 128), (sG, DI + m * 128)):
                P.dma("sync",
                      st_.v(0, 128, 0, NG * 128).rearrange(
                          "p (g f) -> p g f", f=128)[:, :, 0:wdt],
                      hs["w1"][i][:, cofs:cofs + wdt].rearrange(
                          "(g p) f -> p g f", p=128).bitcast(F32R),
                      writes=[st_.R], tag=f"w1l{i}_{m}")
        w1_load(0)
        w1_load(1)
        w2s = self.ar.alloc(2816, f"w2s{i}")
        w2bb = self.ab.alloc(2816, f"w2bb{i}")

        def w2_stage(mo):
            P.dma("gpsimd",
                  w2s.v(0, 128, 0, 2816).rearrange("p (g f) -> p g f", f=128),
                  hs["w2"][i][:, mo * 128:(mo + 1) * 128].rearrange(
                      "(g p) f -> p g f", p=128).bitcast(F32R),
                  writes=[w2s.R], tag=f"w2l{i}_{mo}")
            P.copy(w2bb.v(), w2s.v().bitcast(F32),
                   reads=[w2s.R], writes=[w2bb.R], engine="gpsimd",
                   tag=f"w2c{i}_{mo}")
            P.dma("gpsimd", self._w2st[mo], w2bb.v(),
                  reads=[w2bb.R], writes=[self._w2stR], tag=f"w2o{i}_{mo}")
        self.xnb = self.ab.alloc(NG * SEQ, f"xnb{i}")
        self.xnbR = self.xnb.subs(f"xnb{i}_", NG)
        tf2 = [self.ab.alloc(512, f"ft2{i}_{j}", dt=F32) for j in range(2)]
        def srcv(g, c0, n):
            return (self.xi_data(g, c0, n) if g < 4
                    else self.xi_shift(g, c0, n))
        def cb(g, ch, tile, tileR, t1, t1R, b_b, bbR):
            t = tf2[g % 2]
            P.tt(t.v(), t1, b_b, ALU.add, reads=[t1R, bbR],
                 writes=[t.R], tag=f"fln2{i}_{ch}_{g}")
            P.ts(self.xnb_view(g, ch * 512, 512), t.v(),
                 pl.v(0, 128, g, 1), pl.v(0, 128, 8 + g, 1),
                 op0=ALU.mult, op1=ALU.add,
                 reads=[t.R, pl.R], writes=[self.xnbR[g]],
                 tag=f"flnx{i}_{ch}_{g}")
        self.emit_ln(self.ResidentSrc(srcv, self.xiR), cb, f"fln{i}")
        for t in tf2:
            t.free()
        w1b = [self.ab.alloc(NG * 128, f"w1b{i}_{j}") for j in range(4)]
        gsc = self.ab.alloc(512, f"gsc{i}", dt=F32)
        glt = [self.ab.alloc(512, f"glt{i}_{j}") for j in range(2)]
        for m in range(NM1):
            wdt = 128 if m < NM1 - 1 else DI - 128 * (NM1 - 1)
            sA, sG = w1s[2 * (m % 2)], w1s[2 * (m % 2) + 1]
            bA, bG = w1b[2 * (m % 2)], w1b[2 * (m % 2) + 1]
            P.copy(bA.v(), sA.v().bitcast(F32), reads=[sA.R], writes=[bA.R],
                   engine="gpsimd", tag=f"w1cA{i}_{m}")
            P.copy(bG.v(), sG.v().bitcast(F32), reads=[sG.R], writes=[bG.R],
                   engine="gpsimd", tag=f"w1cG{i}_{m}")
            if m + 2 < NM1:
                w1_load(m + 2)
            if m % 2 == 1 and 3 <= m < 19:
                w2_stage((m - 3) // 2)
            for ch in range(NCH):
                psA, psAR = self.pb[ch % 2].ap(), self.pbR[ch % 2]
                psG, psGR = self.pb[2 + ch % 2].ap(), self.pbR[2 + ch % 2]
                for k in range(NG):
                    P.mm(psA[0:wdt, :], bA.v(0, 128, k * 128, wdt),
                         self.xnb_view(k, ch * 512, 512), k == 0, k == NG - 1,
                         reads=[bA.R, self.xnbR[k]], writes=[psAR],
                         tag=f"w1A{i}_{m}_{ch}_{k}")
                for k in range(NG):
                    P.mm(psG[0:wdt, :], bG.v(0, 128, k * 128, wdt),
                         self.xnb_view(k, ch * 512, 512), k == 0, k == NG - 1,
                         reads=[bG.R, self.xnbR[k]], writes=[psGR],
                         tag=f"w1G{i}_{m}_{ch}_{k}")
                P.act(gsc.v(0, wdt), psG[0:wdt, :], AF.Gelu,
                      bias=pl.v(0, wdt, 38 + m, 1),
                      reads=[psGR, pl.R], writes=[gsc.R],
                      tag=f"gel{i}_{m}_{ch}")
                g2 = glt[ch % 2]
                P.stt(g2.v(0, wdt), psA[0:wdt, :], pl.v(0, wdt, 16 + m, 1),
                      gsc.v(0, wdt), ALU.add, ALU.mult,
                      reads=[psAR, gsc.R, pl.R], writes=[g2.R],
                      tag=f"glu{i}_{m}_{ch}")
                P.dma("gpsimd",
                      self._glu3[0:wdt, m, ch * 512:(ch + 1) * 512],
                      g2.v(0, wdt), reads=[g2.R], writes=[self._gluR],
                      tag=f"gls{i}_{m}_{ch}")
        for reg in w1s + w1b + [gsc] + glt + [w2s, w2bb]:
            reg.free()
        self.xnb.free()
        w2f = [self.ab.alloc(2816, f"w2f{i}_{j}") for j in range(2)]
        gch = [self.ab.alloc(NM1 * 512, f"gch{i}_{j}") for j in range(2)]

        def gch_load(ch):
            g = gch[ch % 2]
            P.dma("gpsimd",
                  g.v(0, 128, 0, NM1 * 512).rearrange(
                      "p (g t) -> p g t", t=512),
                  self._glu3[:, 0:NM1, ch * 512:(ch + 1) * 512],
                  reads=[self._gluR], writes=[g.R], tag=f"w2g{i}_{ch}")

        def w2f_load(ch, mo):
            f = w2f[mo % 2]
            P.dma("scalar", f.v(), self._w2st[mo],
                  reads=[self._w2stR], writes=[f.R], tag=f"w2fl{i}_{ch}_{mo}")
        gch_load(0)
        gch_load(1)
        w2f_load(0, 0)
        for ch in range(NCH):
            g = gch[ch % 2]
            for mo in range(NG):
                f = w2f[mo % 2]
                if mo + 1 < NG:
                    w2f_load(ch, mo + 1)
                elif ch + 1 < NCH:
                    w2f_load(ch + 1, 0)
                ps, psR = self.pb[4 + mo % 2].ap(), self.pbR[4 + mo % 2]
                for k in range(NM1):
                    kw = 128 if k < NM1 - 1 else DI - 128 * (NM1 - 1)
                    P.mm(ps[:, :], f.v(0, kw, k * 128, 128),
                         g.v(0, kw, k * 512, 512), k == 0, k == NM1 - 1,
                         reads=[f.R, g.R], writes=[psR],
                         tag=f"w2m{i}_{ch}_{mo}_{k}")
                P.stt(self.xi_data(mo, ch * 512, 512), ps[:, :],
                      pl.v(0, 128, 60 + mo, 1),
                      self.xi_data(mo, ch * 512, 512).bitcast(F32),
                      ALU.add, ALU.add,
                      reads=[psR, pl.R, self.xiR[mo]], writes=[self.xiR[mo]],
                      tag=f"w2r{i}_{ch}_{mo}")
            if ch + 2 < NCH:
                gch_load(ch + 2)
        for reg in w2f + gch:
            reg.free()

    def phase_l0attn(self, wqkvT_h, woutT_h, attnC_h):
        P, c = self.P, self.c
        NQ = 4
        aC3 = attnC_h.rearrange("(g p) t -> p g t", p=128)
        aCR = R("attnC")
        Sp = [self.ab.alloc(8 * 64, f"l0S{i}", dt=F32) for i in range(2)]
        Zp = [self.ab.alloc(8, f"l0Z{i}", dt=F32) for i in range(2)]
        Sbf = self.ab.alloc(8 * 64, "l0Sbf")
        Zbf = self.ab.alloc(8, "l0Zbf")
        wst = [self.ar.alloc(NG * 256, f"l0wst{i}") for i in range(2)]
        sc_f = [self.ab.alloc(512, f"l0el{i}", dt=F32) for i in range(2)]
        for q4 in range(NQ):
            t0 = q4 * 512
            qtm = self.ab.alloc(4 * 1024, "l0qtm")
            ktm = self.ab.alloc(4 * 1024, "l0ktm")
            vtm = self.ab.alloc(4 * 1024, "l0vtm")
            for wc in range(12):
                w = wst[wc % 2]
                P.dma("sync",
                      w.v(0, 128, 0, NG * 256).rearrange(
                          "p (g f) -> p g f", f=256),
                      wqkvT_h[:, wc * 256:(wc + 1) * 256].rearrange(
                          "(g p) f -> p g f", p=128).bitcast(F32R),
                      writes=[w.R], tag=f"l0w{q4}_{wc}")
                for tt in range(4):
                    ps0 = self.pb[tt % 2].ap()[:, (wc % 2) * 256:
                                               (wc % 2) * 256 + 256]
                    psR = self.pbR[tt % 2]
                    for k in range(NG):
                        P.mm(ps0, self.xi_data(k, t0 + tt * 128, 128),
                             w.v(0, 128, k * 256, 256), k == 0, k == NG - 1,
                             reads=[self.xiR[k], w.R], writes=[psR],
                             tag=f"l0qkv{q4}_{wc}_{tt}_{k}")
                    dst = (qtm, ktm, vtm)[wc // 4]
                    dv = dst.v(0, 128, tt * 1024 + (wc % 4) * 256, 256)
                    if wc < 8:
                        s1, s2 = sc_f[0], sc_f[1]
                        P.ts(s1.v(0, 128, 0, 256), ps0, 0.0, None,
                             op0=ALU.min,
                             reads=[psR], writes=[s1.R],
                             tag=f"l0e1{q4}_{wc}_{tt}")
                        P.act(s2.v(0, 128, 0, 256), s1.v(0, 128, 0, 256),
                              AF.Exp, reads=[s1.R],
                              writes=[s2.R], tag=f"l0e2{q4}_{wc}_{tt}")
                        P.stt(dv, ps0, 0.0, s2.v(0, 128, 0, 256),
                              ALU.max, ALU.add,
                              reads=[psR, s2.R], writes=[dst.R],
                              tag=f"l0e3{q4}_{wc}_{tt}")
                    else:
                        P.act(dv, ps0, AF.Copy, reads=[psR], writes=[dst.R],
                              tag=f"l0ev{q4}_{wc}_{tt}")
            qfm = self.ab.alloc(NG * 512, "l0qfm")
            kfm = self.ab.alloc(NG * 512, "l0kfm")
            tbank = self.tb[0].ap()
            for src, dst in ((qtm, qfm), (ktm, kfm)):
                for f in range(NG):
                    for tt in range(4):
                        tb_ap = tbank[:, (tt % 4) * 128:(tt % 4) * 128 + 128]
                        P.trans(tb_ap,
                                src.v(0, 128, tt * 1024 + f * 128, 128),
                                c["ident_bf"][:, :],
                                reads=[src.R, self.cR], writes=[self.tbR[0]],
                                tag=f"l0T{q4}_{f}_{tt}")
                        P.copy(dst.v(0, 128, f * 512 + tt * 128, 128), tb_ap,
                               reads=[self.tbR[0]], writes=[dst.R],
                               tag=f"l0Te{q4}_{f}_{tt}")
            qtm.free()
            atn = self.ab.alloc(NG * 512, "l0atn")
            asc = [self.ab.alloc(512, f"l0asc{j}") for j in range(2)]
            atT = [self.ab.alloc(512, f"l0atT{j}") for j in range(2)]
            sfa0 = self.sf.ap()
            rdi = [sfa0[0:1, 0:256], sfa0[0:1, 512:768]]
            rdo = [sfa0[0:1, 256:512], sfa0[0:1, 768:1024]]
            rb = [self.ab.alloc(256, f"l0rb{j}", dt=F32) for j in range(2)]
            for h in range(HEADS):
                f = h // 2
                pb_ = (h % 2) * 64
                scol = (h // 2) * 64
                for cc in range(2):
                    gc = q4 * 2 + cc
                    W2_ = (h * 2 + cc) % 2
                    BSC, BND, BRS = W2_, 2 + W2_, 4 + W2_
                    ascW, atTW, rbW = asc[W2_], atT[W2_], rb[W2_]
                    rdiW, rdoW = rdi[W2_], rdo[W2_]
                    qv = lambda s_, n_: qfm.v(pb_, 64,
                                              f * 512 + cc * 256 + s_, n_)
                    kv = kfm.v(pb_, 64, f * 512 + cc * 256, 256)
                    for s in range(2):
                        P.mm(self.pb[BSC].ap()[:, s * 256:(s + 1) * 256],
                             qv(s * 128, 128), kv, True, True,
                             reads=[qfm.R, kfm.R], writes=[self.pbR[BSC]],
                             tag=f"l0sc{q4}_{h}_{cc}_{s}")
                    for s in range(2):
                        P.tt(ascW.v(0, 128, s * 256, 256),
                             self.pb[BSC].ap()[:, s * 256:(s + 1) * 256],
                             c[f"cmask_s{s}"][:, :], ALU.mult,
                             reads=[self.pbR[BSC], self.cR], writes=[ascW.R],
                             tag=f"l0ms{q4}_{h}_{cc}_{s}")
                    tbankp = self.tb[W2_].ap()
                    for s in range(2):
                        for ks in range(2):
                            tb_ap = tbankp[:, 512 + ks * 256 + s * 128:
                                           512 + ks * 256 + s * 128 + 128]
                            P.trans(tb_ap,
                                    ascW.v(0, 128, s * 256 + ks * 128, 128),
                                    c["ident_bf"][:, :],
                                    reads=[ascW.R, self.cR],
                                    writes=[self.tbR[W2_]],
                                    tag=f"l0aT{q4}_{h}_{cc}_{s}{ks}")
                            P.copy(atTW.v(0, 128, ks * 256 + s * 128, 128),
                                   tb_ap, reads=[self.tbR[W2_]],
                                   writes=[atTW.R],
                                   tag=f"l0aTe{q4}_{h}_{cc}_{s}{ks}")
                    nmm = 3 if gc > 0 else 2
                    for ks in range(2):
                        P.mm(self.pb[BND].ap()[pb_:pb_ + 64, 0:256],
                             vtm.v(0, 128, (2 * cc + ks) * 1024 + h * 64, 64),
                             atTW.v(0, 128, ks * 256, 256),
                             ks == 0, ks == nmm - 1 and gc == 0,
                             reads=[vtm.R, atTW.R], writes=[self.pbR[BND]],
                             tag=f"l0nm{q4}_{h}_{cc}_{ks}")
                    if gc > 0:
                        P.mm(self.pb[BND].ap()[pb_:pb_ + 64, 0:256],
                             Sbf.v(pb_, 64, scol, 64),
                             qv(0, 256), False, True,
                             reads=[Sbf.R, qfm.R], writes=[self.pbR[BND]],
                             tag=f"l0nS{q4}_{h}_{cc}")
                    for ks in range(2):
                        P.mm(self.pb[BND].ap()[0:1, 256:512],
                             c["ones_bf_col"][:, :],
                             atTW.v(0, 128, ks * 256, 256),
                             ks == 0, ks == nmm - 1 and gc == 0,
                             reads=[atTW.R, self.cR], writes=[self.pbR[BND]],
                             tag=f"l0dn{q4}_{h}_{cc}_{ks}")
                    if gc > 0:
                        P.mm(self.pb[BND].ap()[0:1, 256:512],
                             Zbf.v(pb_, 64, h // 2, 1),
                             qv(0, 256), False, True,
                             reads=[Zbf.R, qfm.R], writes=[self.pbR[BND]],
                             tag=f"l0dZ{q4}_{h}_{cc}")
                    P.ts(rdiW, self.pb[BND].ap()[0:1, 256:512], 1e-6, None,
                         op0=ALU.max, reads=[self.pbR[BND]],
                         writes=[self.sfRp[W2_]],
                         tag=f"l0rm{q4}_{h}_{cc}")
                    P.recip(rdoW, rdiW, reads=[self.sfRp[W2_]],
                            writes=[self.sfRp[W2_]],
                            tag=f"l0rr{q4}_{h}_{cc}")
                    P.mm(self.pb[BRS].ap()[pb_:pb_ + 64, 0:256],
                         c["ones_f32_row64"][:, :], rdoW, True, True,
                         reads=[self.sfRp[W2_], self.cR],
                         writes=[self.pbR[BRS]],
                         tag=f"l0rb{q4}_{h}_{cc}")
                    P.act(rbW.v(pb_, 64),
                          self.pb[BRS].ap()[pb_:pb_ + 64, 0:256],
                          AF.Copy,
                          reads=[self.pbR[BRS]], writes=[rbW.R],
                          tag=f"l0rbe{q4}_{h}_{cc}")
                    P.tt(atn.v(pb_, 64, f * 512 + cc * 256, 256),
                         self.pb[BND].ap()[pb_:pb_ + 64, 0:256],
                         rbW.v(pb_, 64), ALU.mult,
                         reads=[self.pbR[BND], rbW.R], writes=[atn.R],
                         tag=f"l0o{q4}_{h}_{cc}")
                    for ks in range(2):
                        P.mm(self.pb[BRS].ap()[pb_:pb_ + 64, 256:320],
                             ktm.v(0, 128, (2 * cc + ks) * 1024 + h * 64, 64),
                             vtm.v(0, 128, (2 * cc + ks) * 1024 + h * 64, 64),
                             ks == 0, ks == 1,
                             reads=[ktm.R, vtm.R], writes=[self.pbR[BRS]],
                             tag=f"l0Su{q4}_{h}_{cc}_{ks}")
                    for ks in range(2):
                        P.mm(self.pb[BRS].ap()[pb_:pb_ + 64, 320:321],
                             ktm.v(0, 128, (2 * cc + ks) * 1024 + h * 64, 64),
                             c["ones_bf_col"][:, :], ks == 0, ks == 1,
                             reads=[ktm.R, self.cR], writes=[self.pbR[BRS]],
                             tag=f"l0Zu{q4}_{h}_{cc}_{ks}")
                    Sr, Sw = Sp[gc % 2], Sp[1 - gc % 2]
                    Zr, Zw = Zp[gc % 2], Zp[1 - gc % 2]
                    if gc == 0:
                        P.copy(Sw.v(pb_, 64, scol, 64),
                               self.pb[BRS].ap()[pb_:pb_ + 64, 256:320],
                               reads=[self.pbR[BRS]], writes=[Sw.R],
                               tag=f"l0Sc{q4}_{h}_{cc}")
                        P.copy(Zw.v(pb_, 64, h // 2, 1),
                               self.pb[BRS].ap()[pb_:pb_ + 64, 320:321],
                               reads=[self.pbR[BRS]], writes=[Zw.R],
                               tag=f"l0Zc{q4}_{h}_{cc}")
                    else:
                        P.tt(Sw.v(pb_, 64, scol, 64),
                             self.pb[BRS].ap()[pb_:pb_ + 64, 256:320],
                             Sr.v(pb_, 64, scol, 64), ALU.add,
                             reads=[self.pbR[BRS], Sr.R], writes=[Sw.R],
                             tag=f"l0Sa{q4}_{h}_{cc}")
                        P.tt(Zw.v(pb_, 64, h // 2, 1),
                             self.pb[BRS].ap()[pb_:pb_ + 64, 320:321],
                             Zr.v(pb_, 64, h // 2, 1), ALU.add,
                             reads=[self.pbR[BRS], Zr.R], writes=[Zw.R],
                             tag=f"l0Za{q4}_{h}_{cc}")
                    P.act(Sbf.v(pb_, 64, scol, 64), Sw.v(pb_, 64, scol, 64),
                          AF.Copy, scale=0.125, reads=[Sw.R], writes=[Sbf.R],
                          tag=f"l0Sb{q4}_{h}_{cc}")
                    P.act(Zbf.v(pb_, 64, h // 2, 1), Zw.v(pb_, 64, h // 2, 1),
                          AF.Copy, scale=0.125, reads=[Zw.R], writes=[Zbf.R],
                          tag=f"l0Zb{q4}_{h}_{cc}")
            wstg = self.ar.alloc(NG * 128, "l0wostg")
            wob = self.ab.alloc(NG * 128, "l0wob")
            wsc = self.ab.alloc(512, "l0wsc", dt=F32)
            for mo in range(NG):
                P.dma("sync",
                      wstg.v(0, 128, 0, 1024).rearrange(
                          "p (g m) -> p g m", m=128),
                      woutT_h[:, mo * 128:(mo + 1) * 128].rearrange(
                          "(g p) m -> p g m", p=128).bitcast(F32R),
                      writes=[wstg.R], tag=f"l0wo{q4}_{mo}")
                P.copy(wob.v(), wstg.v().bitcast(F32), reads=[wstg.R],
                       writes=[wob.R], tag=f"l0woc{q4}_{mo}")
                ps, psR = self.pb[0].ap()[:, :], self.pbR[0]
                for k in range(NG):
                    P.mm(ps, wob.v(0, 128, k * 128, 128),
                         atn.v(0, 128, k * 512, 512), k == 0, k == NG - 1,
                         reads=[wob.R, atn.R], writes=[psR],
                         tag=f"l0y{q4}_{mo}_{k}")
                P.act(wsc.v(), ps, AF.Copy, reads=[psR], writes=[wsc.R],
                      tag=f"l0ye{q4}_{mo}")
                P.dma("sync", aC3[:, mo, t0:t0 + 512], wsc.v(),
                      reads=[wsc.R], writes=[aCR], tag=f"l0ys{q4}_{mo}")
            for reg in [ktm, vtm, qfm, kfm, atn, wstg, wob, wsc] + asc + \
                    atT + rb:
                reg.free()
        for reg in Sp + Zp + [Sbf, Zbf] + wst + sc_f:
            reg.free()
        return aCR

    def phase_l1attn(self, ap_arr, bwT_h, cwT_h):
        P, c = self.P, self.c
        NJ = 4
        stg = self.ar.alloc(NG * 512, "l1bwstg")
        bwb = self.ab.alloc(NG * 512, "l1bwb")
        P.dma("sync", stg.v(0, 128, 0, NG * 512).rearrange(
                  "p (g n) -> p g n", n=512),
              bwT_h.rearrange("(g p) n -> p g n", p=128).bitcast(F32R),
              writes=[stg.R], tag="bwld")
        P.copy(bwb.v(), stg.v().bitcast(F32), reads=[stg.R], writes=[bwb.R],
               tag="bwcv")
        stg.free()
        arow, arowR = self.small("l1_ap", ap_arr, F32)
        abc = self.ab.alloc(256, "l1abc", dt=F32)
        wsc = self.ab.alloc(256, "l1wsc", dt=F32)
        wbuf = self.ab.alloc(32, "l1wbuf", dt=F32)
        psA, psAR = self.pb[0].ap()[:, 0:256], self.pbR[0]
        P.mm(psA, c["ones_f32_row"][:, :], arow[:, :], True, True,
             reads=[arowR, self.cR], writes=[psAR], tag="l1abc")
        P.act(abc.v(), psA, AF.Copy, reads=[psAR], writes=[abc.R], tag="l1abce")
        for jt in range(NJ):
            P.tt(wsc.v(), c["l1_M"][:, jt * 256:(jt + 1) * 256], abc.v(),
                 ALU.mult, reads=[abc.R, self.cR], writes=[wsc.R],
                 tag=f"l1w{jt}")
            P.red(wbuf.v(0, 128, jt, 1), wsc.v(), ALU.add,
                  reads=[wsc.R], writes=[wbuf.R], tag=f"l1wr{jt}")
        DBG = getattr(self, "dbg_l1a", False)
        sfv = lambda c0, n=4: self.sf.ap()[:, c0:c0 + n]
        sfR = self.sfR
        W, T3, RP, RR, OMN, CTH, STH = 0, 4, 8, 12, 16, 20, 24
        CT, ST, A1, B1, C5, S5 = 28, 29, 30, 31, 32, 33
        CARP, CARQ = 34, 38
        P.copy(sfv(W), wbuf.v(0, 128, 0, 4), reads=[wbuf.R], writes=[sfR],
               tag="l1wcp")
        P.tt(sfv(T3), sfv(W), sfv(W), ALU.mult, reads=[sfR], writes=[sfR],
             tag="l1w2")
        P.ts(sfv(RP), sfv(T3), 1.0, None, op0=ALU.add, reads=[sfR],
             writes=[sfR], tag="l1w3")
        P.recip(sfv(RR), sfv(RP), reads=[sfR], writes=[sfR], tag="l1w4")
        P.ts(sfv(OMN), sfv(T3), -1.0, 1.0, op0=ALU.mult, op1=ALU.add,
             reads=[sfR], writes=[sfR], tag="l1w5")
        P.tt(sfv(CTH), sfv(OMN), sfv(RR), ALU.mult, reads=[sfR], writes=[sfR],
             tag="l1w6")
        P.stt(sfv(STH), sfv(W), -2.0, sfv(RR), ALU.mult, ALU.mult,
              reads=[sfR], writes=[sfR], tag="l1w7")
        if DBG:
            self.debug_out("dbg_wbuf", sfv(0, 48), [sfR])
        h_st = self.dint("l1h_st", (512, SEQ), BF16)
        h3 = h_st.rearrange("(g p) t -> p g t", p=128)
        hstR = R("l1h_st")
        tf = self.tabf.ap()
        tabs = [0, 1, 2, 3]
        tabRs = [R(f"tabf{i}") for i in range(4)]
        scrR = [R("tscr0"), R("tscr1")]
        tv = lambda b, c0, n: tf[:, b * 512 + c0:b * 512 + c0 + n]
        sc1 = lambda c0, n: tf[:, 2048 + c0:2048 + c0 + n]
        sc2 = lambda c0, n: tf[:, 2304 + c0:2304 + c0 + n]
        Pf = self.ab.alloc(512, "l1Pf", dt=F32)
        Qf = self.ab.alloc(512, "l1Qf", dt=F32)
        cumP = self.ab.alloc(512, "l1cumP", dt=F32)
        cumQ = self.ab.alloc(512, "l1cumQ", dt=F32)
        hstg = [self.ab.alloc(512, f"l1hs{i}") for i in range(2)]
        for jt in range(NJ):
            cb_, sb_ = tabs[0], tabs[1]
            cR_, sR_ = tabRs[cb_], tabRs[sb_]
            P.ts(tv(cb_, 0, 1), sfv(CTH, 1), 0.0, 1.0, op0=ALU.mult,
                 op1=ALU.add, reads=[sfR], writes=[cR_], tag=f"l1tc0_{jt}")
            P.ts(tv(sb_, 0, 1), sfv(CTH, 1), 0.0, None, op0=ALU.mult,
                 reads=[sfR], writes=[sR_], tag=f"l1ts0_{jt}")
            P.copy(tv(cb_, 1, 1), sfv(CTH + jt, 1), reads=[sfR], writes=[cR_],
                   tag=f"l1tc1_{jt}")
            P.copy(tv(sb_, 1, 1), sfv(STH + jt, 1), reads=[sfR], writes=[sR_],
                   tag=f"l1ts1_{jt}")
            T = 1
            while T < 512:
                if T == 1:
                    P.copy(sfv(CT, 1), tv(cb_, 1, 1), reads=[cR_],
                           writes=[sfR], tag=f"l1cT{jt}_{T}")
                    P.copy(sfv(ST, 1), tv(sb_, 1, 1), reads=[sR_],
                           writes=[sfR], tag=f"l1sT{jt}_{T}")
                else:
                    P.tt(sfv(A1, 1), tv(cb_, T - 1, 1), tv(cb_, 1, 1),
                         ALU.mult, reads=[cR_], writes=[sfR],
                         tag=f"l1cTa{jt}_{T}")
                    P.tt(sfv(B1, 1), tv(sb_, T - 1, 1), tv(sb_, 1, 1),
                         ALU.mult, reads=[sR_], writes=[sfR],
                         tag=f"l1cTb{jt}_{T}")
                    P.tt(sfv(CT, 1), sfv(A1, 1), sfv(B1, 1), ALU.subtract,
                         reads=[sfR], writes=[sfR], tag=f"l1cTc{jt}_{T}")
                    P.tt(sfv(A1, 1), tv(sb_, T - 1, 1), tv(cb_, 1, 1),
                         ALU.mult, reads=[cR_, sR_], writes=[sfR],
                         tag=f"l1sTa{jt}_{T}")
                    P.tt(sfv(B1, 1), tv(cb_, T - 1, 1), tv(sb_, 1, 1),
                         ALU.mult, reads=[cR_, sR_], writes=[sfR],
                         tag=f"l1sTb{jt}_{T}")
                    P.tt(sfv(ST, 1), sfv(A1, 1), sfv(B1, 1), ALU.add,
                         reads=[sfR], writes=[sfR], tag=f"l1sTc{jt}_{T}")
                P.ts(sc1(0, T), tv(cb_, 0, T), sfv(CT, 1), None, op0=ALU.mult,
                     reads=[cR_, sfR], writes=[scrR[0]], tag=f"l1e1_{jt}_{T}")
                P.ts(sc2(0, T), tv(sb_, 0, T), sfv(ST, 1), None, op0=ALU.mult,
                     reads=[sR_, sfR], writes=[scrR[1]], tag=f"l1e2_{jt}_{T}")
                P.tt(tv(cb_, T, T), sc1(0, T), sc2(0, T), ALU.subtract,
                     reads=[scrR[0], scrR[1]], writes=[cR_],
                     tag=f"l1e3_{jt}_{T}")
                P.ts(sc1(0, T), tv(sb_, 0, T), sfv(CT, 1), None, op0=ALU.mult,
                     reads=[sR_, sfR], writes=[scrR[0]], tag=f"l1e4_{jt}_{T}")
                P.ts(sc2(0, T), tv(cb_, 0, T), sfv(ST, 1), None, op0=ALU.mult,
                     reads=[cR_, sfR], writes=[scrR[1]], tag=f"l1e5_{jt}_{T}")
                P.tt(tv(sb_, T, T), sc1(0, T), sc2(0, T), ALU.add,
                     reads=[scrR[0], scrR[1]], writes=[sR_],
                     tag=f"l1e6_{jt}_{T}")
                T *= 2
            P.tt(sfv(A1, 1), tv(cb_, 511, 1), tv(cb_, 1, 1), ALU.mult,
                 reads=[cR_], writes=[sfR], tag=f"l1r1_{jt}")
            P.tt(sfv(B1, 1), tv(sb_, 511, 1), tv(sb_, 1, 1), ALU.mult,
                 reads=[sR_], writes=[sfR], tag=f"l1r2_{jt}")
            P.tt(sfv(C5, 1), sfv(A1, 1), sfv(B1, 1), ALU.subtract,
                 reads=[sfR], writes=[sfR], tag=f"l1r3_{jt}")
            P.tt(sfv(A1, 1), tv(sb_, 511, 1), tv(cb_, 1, 1), ALU.mult,
                 reads=[cR_, sR_], writes=[sfR], tag=f"l1r4_{jt}")
            P.tt(sfv(B1, 1), tv(cb_, 511, 1), tv(sb_, 1, 1), ALU.mult,
                 reads=[cR_, sR_], writes=[sfR], tag=f"l1r5_{jt}")
            P.tt(sfv(S5, 1), sfv(A1, 1), sfv(B1, 1), ALU.add,
                 reads=[sfR], writes=[sfR], tag=f"l1r6_{jt}")
            if DBG and jt == 0:
                self.debug_out("dbg_cos0", tv(cb_, 0, 512), [cR_])
                self.debug_out("dbg_sin0", tv(sb_, 0, 512), [sR_])
            for ch in range(NCH):
                cb_, sb_ = tabs[0], tabs[1]
                cn_, sn_ = tabs[2], tabs[3]
                cR_, sR_ = tabRs[cb_], tabRs[sb_]
                cnR_, snR_ = tabRs[cn_], tabRs[sn_]
                ps, psR = self.pb[2 + ch % 2].ap()[:, :], self.pbR[2 + ch % 2]
                for k in range(NG):
                    P.mm(ps, bwb.v(0, 128, k * 512 + jt * 128, 128),
                         self.xnb_view(k, ch * 512, 512), k == 0, k == NG - 1,
                         reads=[bwb.R, self.xnbR[k]], writes=[psR],
                         tag=f"l1bu{jt}_{ch}_{k}")
                P.tt(Pf.v(), ps, tv(cb_, 0, 512), ALU.mult,
                     reads=[psR, cR_], writes=[Pf.R], tag=f"l1P{jt}_{ch}")
                P.tt(Qf.v(), ps, tv(sb_, 0, 512), ALU.mult,
                     reads=[psR, sR_], writes=[Qf.R], tag=f"l1Q{jt}_{ch}")
                if DBG and jt == 0 and ch == 0:
                    self.debug_out("dbg_Pf", Pf.v(), [Pf.R])
                initP = 0.0 if ch == 0 else sfv(CARP + jt, 1)
                initQ = 0.0 if ch == 0 else sfv(CARQ + jt, 1)
                P.scan(cumP.v(), c["ones_f32_big"][:, :], Pf.v(), initP,
                       reads=[Pf.R, sfR, self.cR], writes=[cumP.R],
                       tag=f"l1sP{jt}_{ch}")
                P.scan(cumQ.v(), c["ones_f32_big"][:, :], Qf.v(), initQ,
                       reads=[Qf.R, sfR, self.cR], writes=[cumQ.R],
                       tag=f"l1sQ{jt}_{ch}")
                P.copy(sfv(CARP + jt, 1), cumP.v(0, 128, 511, 1),
                       reads=[cumP.R], writes=[sfR], tag=f"l1cP{jt}_{ch}")
                P.copy(sfv(CARQ + jt, 1), cumQ.v(0, 128, 511, 1),
                       reads=[cumQ.R], writes=[sfR], tag=f"l1cQ{jt}_{ch}")
                P.tt(Pf.v(), cumP.v(), tv(cb_, 0, 512), ALU.mult,
                     reads=[cumP.R, cR_], writes=[Pf.R], tag=f"l1h1{jt}_{ch}")
                P.tt(Qf.v(), cumQ.v(), tv(sb_, 0, 512), ALU.mult,
                     reads=[cumQ.R, sR_], writes=[Qf.R], tag=f"l1h2{jt}_{ch}")
                hs = hstg[ch % 2]
                P.tt(hs.v(), Pf.v(), Qf.v(), ALU.add,
                     reads=[Pf.R, Qf.R], writes=[hs.R], tag=f"l1h3{jt}_{ch}")
                P.dma("sync", h3[:, jt, ch * 512:(ch + 1) * 512], hs.v(),
                      reads=[hs.R], writes=[hstR], tag=f"l1hst{jt}_{ch}")
                if ch < NCH - 1:
                    for half in range(2):
                        o = half * 256
                        P.ts(sc1(0, 256), tv(cb_, o, 256), sfv(C5, 1), None,
                             op0=ALU.mult, reads=[cR_, sfR], writes=[scrR[0]],
                             tag=f"l1ra{jt}_{ch}_{half}")
                        P.ts(sc2(0, 256), tv(sb_, o, 256), sfv(S5, 1), None,
                             op0=ALU.mult, reads=[sR_, sfR], writes=[scrR[1]],
                             tag=f"l1rb{jt}_{ch}_{half}")
                        P.tt(tv(cn_, o, 256), sc1(0, 256), sc2(0, 256),
                             ALU.subtract, reads=[scrR[0], scrR[1]],
                             writes=[cnR_], tag=f"l1rc{jt}_{ch}_{half}")
                        P.ts(sc1(0, 256), tv(sb_, o, 256), sfv(C5, 1), None,
                             op0=ALU.mult, reads=[sR_, sfR], writes=[scrR[0]],
                             tag=f"l1rd{jt}_{ch}_{half}")
                        P.ts(sc2(0, 256), tv(cb_, o, 256), sfv(S5, 1), None,
                             op0=ALU.mult, reads=[cR_, sfR], writes=[scrR[1]],
                             tag=f"l1re{jt}_{ch}_{half}")
                        P.tt(tv(sn_, o, 256), sc1(0, 256), sc2(0, 256),
                             ALU.add, reads=[scrR[0], scrR[1]], writes=[snR_],
                             tag=f"l1rf{jt}_{ch}_{half}")
                    tabs[0], tabs[1], tabs[2], tabs[3] = \
                        tabs[2], tabs[3], tabs[0], tabs[1]
        for reg in [Pf, Qf, cumP, cumQ, abc, wsc, wbuf, bwb] + hstg:
            reg.free()
        stg2 = self.ar.alloc(4 * 1024, "l1cwstg")
        cwb = self.ab.alloc(4 * 1024, "l1cwb")
        P.dma("sync", stg2.v(0, 128, 0, 4096).rearrange(
                  "p (g n) -> p g n", n=1024),
              cwT_h.rearrange("(g p) n -> p g n", p=128).bitcast(F32R),
              writes=[stg2.R], tag="cwld")
        P.copy(cwb.v(), stg2.v().bitcast(F32), reads=[stg2.R], writes=[cwb.R],
               tag="cwcv")
        stg2.free()
        hch = [self.ab.alloc(4 * 512, f"l1hch{i}") for i in range(2)]
        for ch in range(NCH):
            hc = hch[ch % 2]
            P.dma("sync", hc.v(0, 128, 0, 2048).rearrange(
                      "p (g t) -> p g t", t=512),
                  h3[:, :, ch * 512:(ch + 1) * 512],
                  reads=[hstR], writes=[hc.R], tag=f"l1hld{ch}")
            for mo in range(NG):
                ps, psR = self.pb[4 + mo % 2].ap()[:, :], self.pbR[4 + mo % 2]
                for k in range(NJ):
                    P.mm(ps, cwb.v(0, 128, k * 1024 + mo * 128, 128),
                         hc.v(0, 128, k * 512, 512), k == 0, k == NJ - 1,
                         reads=[cwb.R, hc.R], writes=[psR],
                         tag=f"l1y{ch}_{mo}_{k}")
                P.tt(self.x_view(mo, ch * 512, 512), ps,
                     self.x_view(mo, ch * 512, 512).bitcast(F32), ALU.add,
                     reads=[psR, self.xR[mo]], writes=[self.xR[mo]],
                     tag=f"l1yev{ch}_{mo}")
        for reg in hch + [cwb]:
            reg.free()

    def phase_l1ffn(self, wgT_h, wuT_h, wdT_h):
        P, c = self.P, self.c
        glu_st = self.dint("l1glu_st", (HID_L1, SEQ), BF16)
        glu3 = glu_st.rearrange("(g p) t -> p g t", p=128)
        wd_bf = self.dint("l1wd_bf", (NG, 128, HID_L1), BF16)
        NM = HID_L1 // 128
        ist = [self.ab.alloc(HID_L1, "wdist0", dt=I32)]
        istR = [ist[0].R]
        wcv = [self.ab.alloc(HID_L1, "wdcv0")]
        wcvR = [wcv[0].R]
        for mo in range(NG):
            P.dma("sync", ist[0].v().rearrange("p (g m) -> p g m", m=128),
                  wdT_h[:, mo * 128:(mo + 1) * 128].rearrange(
                      "(g p) m -> p g m", p=128),
                  writes=[istR[0]], tag=f"wdld{mo}")
            P.copy(wcv[0].v(), ist[0].v(), engine="gpsimd",
                   reads=[istR[0]], writes=[wcvR[0]], tag=f"wdcv{mo}")
            P.dma("sync", wd_bf[mo], wcv[0].v(), reads=[wcvR[0]],
                  tag=f"wdst{mo}")
        for r in ist + wcv:
            r.free()
        xg = [self.ab.alloc(NG * 128, f"l1wg{i}", dt=I32) for i in range(2)]
        xgR = [b.R for b in xg]
        wgb = [self.ab.alloc(NG * 128, f"l1wgb{i}") for i in range(2)]
        wgbR = [b.R for b in wgb]
        xu = [self.ab.alloc(NG * 128, f"l1wu{i}", dt=I32) for i in range(2)]
        xuR = [b.R for b in xu]
        wub = [self.ab.alloc(NG * 128, f"l1wub{i}") for i in range(2)]
        wubR = [b.R for b in wub]
        sg = self.ab.alloc(512, "l1sg")
        sgR = sg.R
        glt = [self.ab.alloc(512, f"l1glt{i}") for i in range(2)]
        gltR = [b.R for b in glt]
        for m in range(NM):
            i2 = m % 2
            P.dma("sync", xg[i2].v().rearrange("p (g m) -> p g m", m=128),
                  wgT_h[:, m * 128:(m + 1) * 128].rearrange(
                      "(g p) m -> p g m", p=128),
                  writes=[xgR[i2]], tag=f"wgld{m}")
            P.copy(wgb[i2].v(), xg[i2].v(), engine="gpsimd",
                   reads=[xgR[i2]], writes=[wgbR[i2]], tag=f"wgcv{m}")
            P.dma("sync", xu[i2].v().rearrange("p (g m) -> p g m", m=128),
                  wuT_h[:, m * 128:(m + 1) * 128].rearrange(
                      "(g p) m -> p g m", p=128),
                  writes=[xuR[i2]], tag=f"wuld{m}")
            P.copy(wub[i2].v(), xu[i2].v(), engine="gpsimd",
                   reads=[xuR[i2]], writes=[wubR[i2]], tag=f"wucv{m}")
            for ch in range(NCH):
                ps0, ps0R = self.pb[ch % 2].ap()[:, :], self.pbR[ch % 2]
                ps1, ps1R = self.pb[2 + ch % 2].ap()[:, :], self.pbR[2 + ch % 2]
                for k in range(NG):
                    P.mm(ps0, wgb[i2].v(0, 128, k * 128, 128),
                         self.xnb_view(k, ch * 512, 512), k == 0, k == NG - 1,
                         reads=[wgbR[i2], self.xnbR[k]], writes=[ps0R],
                         tag=f"l1g{m}_{ch}_{k}")
                for k in range(NG):
                    P.mm(ps1, wub[i2].v(0, 128, k * 128, 128),
                         self.xnb_view(k, ch * 512, 512), k == 0, k == NG - 1,
                         reads=[wubR[i2], self.xnbR[k]], writes=[ps1R],
                         tag=f"l1u{m}_{ch}_{k}")
                P.act(sg.v(), ps0, AF.Silu, scale=INV127,
                      reads=[ps0R], writes=[sgR], tag=f"l1sg{m}_{ch}")
                g2 = glt[ch % 2]
                P.stt(g2.v(), ps1, INV127, sg.v(), ALU.mult, ALU.mult,
                      reads=[ps1R, sgR], writes=[gltR[ch % 2]],
                      tag=f"l1glu{m}_{ch}")
                P.dma("sync", glu3[:, m, ch * 512:(ch + 1) * 512], g2.v(),
                      reads=[gltR[ch % 2]], tag=f"l1gst{m}_{ch}")
        for r in xg + wgb + xu + wub + [sg] + glt:
            r.free()
        self.xnb.free()
        gch = self.ab.alloc(NM * 512, "l1gch")
        gchR = gch.R
        wdb = [self.ab.alloc(HID_L1, f"l1wdb{i}") for i in range(2)]
        wdbR = [b.R for b in wdb]
        for ch in range(NCH):
            P.dma("sync", gch.v(0, 128, 0, NM * 512).rearrange(
                      "p (g t) -> p g t", t=512),
                  glu3[:, :, ch * 512:(ch + 1) * 512],
                  writes=[gchR], tag=f"l1gld{ch}")
            for mo in range(NG):
                i2 = mo % 2
                P.dma("sync", wdb[i2].v(), wd_bf[mo], writes=[wdbR[i2]],
                      tag=f"l1wdb{ch}_{mo}")
                ps, psR = self.pb[4 + mo % 2].ap()[:, :], self.pbR[4 + mo % 2]
                for k in range(NM):
                    P.mm(ps, wdb[i2].v(0, 128, k * 128, 128),
                         gch.v(0, 128, k * 512, 512), k == 0, k == NM - 1,
                         reads=[wdbR[i2], gchR], writes=[psR],
                         tag=f"l1d{ch}_{mo}_{k}")
                P.stt(self.x_view(mo, ch * 512, 512), ps, INV127,
                      self.x_view(mo, ch * 512, 512).bitcast(F32),
                      ALU.mult, ALU.add,
                      reads=[psR, self.xR[mo]], writes=[self.xR[mo]],
                      tag=f"l1dev{ch}_{mo}")
        gch.free()
        for r in wdb:
            r.free()

    def debug_out(self, name, ap, reads, shape=None, dt=F32):
        if shape is None:
            shape = ap.shape
        h = self.dout(name, shape, dt)
        self.P.dma("sync", h, ap if dt != F32 or True else ap, reads=reads,
                   tag="dbg_" + name)

    def small(self, name, arr, dt):
        arr = np.ascontiguousarray(arr)
        self.shared_inputs[name] = arr
        hin = self.din(name, arr.shape, F32 if dt == F32R else
                       ({np.dtype(np.float32): F32,
                         np.dtype(np.int32): I32}[arr.dtype] if dt is None else dt))
        t = self.st.enter_context(
            self.nc.sbuf_tensor("s_" + name, list(arr.shape), dt))
        res = R(name)
        src = hin.bitcast(F32R) if dt == F32R else hin
        self.P.dma("sync", t.ap()[:, :], src, writes=[res], tag="ld_" + name)
        return t.ap(), res

    def store_xi_as(self, dst_ap, tag="store_xi"):
        d = dst_ap.rearrange("(g p) t -> p g t", p=128)
        src = self.xi.v(0, 128, 0, NG * (SEQ + 1)).rearrange(
            "p (g t) -> p g t", t=SEQ + 1)
        return self.P.dma("sync", d, src[:, :, 1:SEQ + 1].bitcast(F32),
                          reads=list(self.xiR), tag=tag)

    def run(self, per_core_inputs, out_names):
        self.P.final_fence()
        self.P.emit(self.st)
        self.st.close()
        in_maps = []
        for cidx in range(N_CORES):
            m = dict(self.shared_inputs)
            m.update(per_core_inputs[cidx])
            in_maps.append(m)
        res = run_bass_kernel_spmd(self.nc, in_maps,
                                   core_ids=list(range(N_CORES)))
        return [{k: r[k] for k in out_names} for r in res.results]



def build_test_l1ffn(g):
    b = Build(["test"]).setup()
    xin = b.din("x_in", (DM, SEQ), F32)
    xout = b.dout("x_out", (DM, SEQ), F32)
    wgT_h = b.win("l1_wgT", g["l1_wgT"])
    wuT_h = b.win("l1_wuT", g["l1_wuT"])
    wdT_h = b.win("l1_wdT", g["l1_wdT"])
    d_ap, dR = b.small("l1_D", g["l1_D"], F32)
    b.alloc_x()
    b.load_x(xin)
    b.phase_ln1(d_ap, dR)
    b.phase_l1ffn(wgT_h, wuT_h, wdT_h)
    b.store_x(xout)
    return b

def _l0ffn_handles(b, g):
    return {
        "wqkv": b.win("wqkv", g["wqkv"]),
        "wo": b.win("wo", g["wo"]),
        "w1": b.win("w1", g["w1"]),
        "w2": b.win("w2", g["w2"]),
        "qs": b.win("qs", g["qs"]),
        "ks": b.win("ks", g["ks"]),
        "lng": b.win("lng", g["ln_g"]),
        "lnb": b.win("lnb", g["ln_b"]),
        "b1a": b.win("b1a", g["b1a"]),
        "b1g": b.win("b1g", g["b1g"]),
        "b2": b.win("b2", g["b2"]),
    }

def build_test_l0ffn(g, nlayers=1):
    b = Build(["l0ffn"]).setup()
    xnin = b.din("xn_in", (DM, SEQ), F32)
    xout = b.dout("x_out", (DM, SEQ), F32)
    hs = _l0ffn_handles(b, g)
    b.alloc_xi()
    s = xnin.rearrange("(g p) t -> p g t", p=128)
    b.P.dma("sync",
            b.xi.v(0, 128, 0, NG * (SEQ + 1)).rearrange(
                "p (g t) -> p g t", t=SEQ + 1)[:, :, 1:SEQ + 1],
            s.bitcast(F32R), writes=list(b.xiR), tag="load_xn")
    glu_st = b.dint("glu_st", (2816, SEQ), BF16)
    b._glu3 = glu_st.rearrange("(g p) t -> p g t", p=128)
    b._gluR = R("glu_st")
    b._w2st = b.dint("w2bf_st", (NG, 128, 2816), BF16)
    b._w2stR = R("w2bf_st")
    for i in range(nlayers):
        pl, qsb, ftab = b._l0f_params(i, hs)
        b._l0f_attn(i, hs, ftab)
        ftab.free()
        qsb.free()
        b._l0f_ffn(i, hs, pl)
        pl.free()
    b.store_xi_as(xout)
    return b

def build_test_l0attn(g):
    b = Build(["l0head"]).setup()
    xnin = b.din("xn_in", (DM, SEQ), F32)
    attnC = b.dout("attnC", (DM, SEQ), F32)
    wq = b.win("l0a_wqkvT", g["l0a_wqkvT"])
    wo = b.win("l0a_woutT", g["l0a_woutT"])
    b.alloc_xi()
    s = xnin.rearrange("(g p) t -> p g t", p=128)
    b.P.dma("sync",
            b.xi.v(0, 128, 0, NG * (SEQ + 1)).rearrange(
                "p (g t) -> p g t", t=SEQ + 1)[:, :, 1:SEQ + 1],
            s.bitcast(F32R), writes=list(b.xiR), tag="load_xn")
    b.phase_l0attn(wq, wo, attnC)
    return b

def build_test_l1attn(g):
    b = Build(["l1attn"]).setup()
    xin = b.din("x_in", (DM, SEQ), F32)
    xout = b.dout("x_out", (DM, SEQ), F32)
    bwT_h = b.win("l1_bwT", g["l1_bwT"])
    cwT_h = b.win("l1_cwT", g["l1_cwT"])
    d_ap, dR = b.small("l1_D", g["l1_D"], F32)
    b.alloc_x()
    b.load_x(xin)
    b.phase_ln1(d_ap, dR)
    b.phase_l1attn(g["l1_ap"], bwT_h, cwT_h)
    b.store_x(xout)
    return b

def build_test_ln():
    b = Build(["test"]).setup()
    xin = b.din("x_in", (DM, SEQ), F32)
    xn_out = b.dout("xn_out", (DM, SEQ), F32)
    b.alloc_xi()
    b.phase_ln0(xin)
    b.store_xi_as(xn_out)
    return b


FUSED = True

def _build_l0head(g):
    b = Build(["l0head"]).setup()
    xin = b.din("x_in", (DM, SEQ), F32)
    xn0o = b.dout("xn0", (DM, SEQ), F32)
    attnCo = b.dout("attnC_o", (DM, SEQ), F32)
    wq = b.win("l0a_wqkvT", g["l0a_wqkvT"])
    wo = b.win("l0a_woutT", g["l0a_woutT"])
    b.alloc_xi()
    b.phase_ln0(xin)
    b.phase_l0attn(wq, wo, attnCo)
    b.store_xi_as(xn0o)
    return b

def _build_l0ffn(g):
    b = Build(["l0ffn"]).setup()
    xnin = b.din("xn0", (DM, SEQ), F32)
    xout = b.dout("xiF", (DM, SEQ), F32)
    hs = _l0ffn_handles(b, g)
    b.alloc_xi()
    s = xnin.rearrange("(g p) t -> p g t", p=128)
    b.P.dma("sync",
            b.xi.v(0, 128, 0, NG * (SEQ + 1)).rearrange(
                "p (g t) -> p g t", t=SEQ + 1)[:, :, 1:SEQ + 1],
            s.bitcast(F32R), writes=list(b.xiR), tag="load_xn")
    b.phase_l0ffn(hs)
    b.store_xi_as(xout)
    return b

def _build_l1(g):
    b = Build(["l1attn"]).setup()
    xin = b.din("x_in", (DM, SEQ), F32)
    aC = b.din("attnC_o", (DM, SEQ), F32)
    xiF = b.din("xiF", (DM, SEQ), F32)
    yout = b.dout("y_out", (DM, SEQ), F32)
    bw = b.win("l1_bwT", g["l1_bwT"])
    cw = b.win("l1_cwT", g["l1_cwT"])
    wg = b.win("l1_wgT", g["l1_wgT"])
    wu = b.win("l1_wuT", g["l1_wuT"])
    wd = b.win("l1_wdT", g["l1_wdT"])
    b.alloc_x()
    b.phase_combine_split(xin, aC, xiF)
    d_ap, dR = b.small("l1_D", g["l1_D"], F32)
    b.phase_ln1(d_ap, dR)
    b.phase_l1attn(g["l1_ap"], bw, cw)
    b.phase_l1ffn(wg, wu, wd)
    b.store_x(yout)
    return b

def _build_fused(g):
    b = Build(["l0head", "l0ffn", "l1attn"]).setup()
    xin = b.din("x_in", (DM, SEQ), F32)
    yout = b.dout("y_out", (DM, SEQ), F32)
    wq = b.win("l0a_wqkvT", g["l0a_wqkvT"])
    wo = b.win("l0a_woutT", g["l0a_woutT"])
    hs = _l0ffn_handles(b, g)
    bw = b.win("l1_bwT", g["l1_bwT"])
    cw = b.win("l1_cwT", g["l1_cwT"])
    wg = b.win("l1_wgT", g["l1_wgT"])
    wu = b.win("l1_wuT", g["l1_wuT"])
    wd = b.win("l1_wdT", g["l1_wdT"])
    attnC = b.dint("attnC", (DM, SEQ), F32)
    b.alloc_xi()
    b.phase_ln0(xin)
    aCR = b.phase_l0attn(wq, wo, attnC)
    b.phase_l0ffn(hs)
    b.phase_combine_fused(xin, attnC.rearrange("(g p) t -> p g t", p=128),
                          aCR)
    d_ap, dR = b.small("l1_D", g["l1_D"], F32)
    b.phase_ln1(d_ap, dR)
    b.phase_l1attn(g["l1_ap"], bw, cw)
    b.phase_l1ffn(wg, wu, wd)
    b.store_x(yout)
    return b

def kernel(**inputs):
    g, xs = host_prepare(inputs)
    per_core = [{"x_in": xs[c]} for c in range(N_CORES)]
    if FUSED:
        b = _build_fused(g)
        rs = b.run(per_core, ["y_out"])
        return np.stack([np.ascontiguousarray(r["y_out"].T)
                         for r in rs]).astype(np.float32)
    b1 = _build_l0head(g)
    r1 = b1.run(per_core, ["xn0", "attnC_o"])
    b2 = _build_l0ffn(g)
    r2 = b2.run([{"xn0": r1[c]["xn0"]} for c in range(N_CORES)], ["xiF"])
    b3 = _build_l1(g)
    r3 = b3.run([{"x_in": xs[c], "attnC_o": r1[c]["attnC_o"],
                  "xiF": r2[c]["xiF"]} for c in range(N_CORES)], ["y_out"])
    return np.stack([np.ascontiguousarray(r3[c]["y_out"].T)
                     for c in range(N_CORES)]).astype(np.float32)


# version 23
# speedup vs baseline: 1.0027x; 1.0027x over previous
import contextlib
import numpy as np
import ml_dtypes
import concourse.bass as bass
import concourse.mybir as mybir
from concourse.bass_utils import run_bass_kernel_spmd

F32 = mybir.dt.float32
F32R = mybir.dt.float32r
BF16 = mybir.dt.bfloat16
I32 = mybir.dt.int32
AF = mybir.ActivationFunctionType
ALU = mybir.AluOpType
AX = mybir.AxisListType
BF16NP = ml_dtypes.bfloat16

N_CORES = 8
SEQ = 2048
DM = 1024
NG = DM // 128
NCH = SEQ // 512
HEADS = 16
DH = 64
WIN = 256
DI = 2730
HID_L1 = 3072
INV127 = 1.0 / 127.0
DEPTH_FFN = 6


class R:
    __slots__ = ("name", "birth")
    def __init__(self, name, birth=None):
        self.name = name
        self.birth = birth
    def __repr__(self):
        return f"R({self.name})"

class _Op:
    __slots__ = ("idx", "engine", "fn", "is_dma", "dma_idx", "deps",
                 "milestone", "mnum", "tag", "sync_all", "small_out")

COMPUTE_ENGINES = ("tensor", "vector", "scalar", "gpsimd")
ALL_ENGINES = ("sync", "tensor", "vector", "scalar", "gpsimd")

SERIAL_DEBUG = False
SIM_SAFE = False

class Prog:
    def __init__(self, nc):
        self.nc = nc
        self.ops = []
        self._lw = {}
        self._rd = {}
        self._dma_count = dict.fromkeys(ALL_ENGINES, 0)

    def add(self, engine, fn, reads=(), writes=(), dma=False, tag="",
            extra_deps=(), sync_all=False, small_out=False):
        idx = len(self.ops)
        deps = set(extra_deps)
        if SERIAL_DEBUG and idx > 0:
            deps.add(idx - 1)
        for r in tuple(reads) + tuple(writes):
            if getattr(r, "birth", None):
                for br in r.birth:
                    w = self._lw.get(br)
                    if w is not None:
                        deps.add(w)
                    for rr in self._rd.get(br, ()):
                        deps.add(rr)
                r.birth = None
        for r in reads:
            w = self._lw.get(r)
            if w is not None:
                deps.add(w)
            self._rd.setdefault(r, []).append(idx)
        for r in writes:
            w = self._lw.get(r)
            if w is not None:
                deps.add(w)
            for rr in self._rd.get(r, ()):
                deps.add(rr)
            self._lw[r] = idx
            self._rd[r] = []
        deps.discard(idx)
        op = _Op()
        op.idx, op.engine, op.fn, op.is_dma = idx, engine, fn, dma
        op.dma_idx = None
        if dma:
            op.dma_idx = self._dma_count[engine]
            self._dma_count[engine] += 1
        op.deps, op.milestone, op.mnum, op.tag = deps, False, 0, tag
        op.sync_all = sync_all
        op.small_out = small_out
        self.ops.append(op)
        return idx

    def dma(self, queue, out, in_, reads=(), writes=(), tag="", **kw):
        return self.add(queue,
                        lambda e, o=out, i=in_, kw=kw: e.dma_start(out=o, in_=i, **kw),
                        reads=reads, writes=writes, dma=True, tag=tag)

    def mm(self, ps, lhsT, rhs, start, stop, reads=(), writes=(), tag=""):
        return self.add("tensor",
                        lambda e, a=ps, b=lhsT, c=rhs, s=start, t=stop:
                            e.matmul(a, b, c, start=s, stop=t),
                        reads=reads, writes=writes, tag=tag)

    def trans(self, ps, in_, ident, reads=(), writes=(), tag=""):
        return self.add("tensor",
                        lambda e, a=ps, b=in_, c=ident: e.transpose(a, b, c),
                        reads=reads, writes=writes, tag=tag)

    def act(self, out, in_, func, bias=None, scale=None, accum=None,
            reads=(), writes=(), tag=""):
        kw = {}
        if bias is not None:
            kw["bias"] = bias
        if scale is not None:
            kw["scale"] = scale
        if accum is not None:
            kw["accum_out"] = accum
        sa = self._isap(bias) or self._isap(scale)
        return self.add("scalar",
                        lambda e, o=out, i=in_, f=func, kw=kw:
                            e.activation(out=o, in_=i, func=f, **kw),
                        reads=reads, writes=writes, tag=tag, sync_all=sa,
                        small_out=self._small(out) or accum is not None)

    def tt(self, out, in0, in1, op, engine="vector", reads=(), writes=(), tag=""):
        return self.add(engine,
                        lambda e, o=out, a=in0, b=in1, p=op:
                            e.tensor_tensor(out=o, in0=a, in1=b, op=p),
                        reads=reads, writes=writes, tag=tag,
                        small_out=self._small(out))

    @staticmethod
    def _isap(v):
        return v is not None and not isinstance(v, (int, float))

    @staticmethod
    def _small(out):
        try:
            return out.free_size() < 128
        except Exception:
            return True

    def ts(self, out, in0, s1, s2=None, op0=ALU.mult, op1=ALU.add,
           engine="vector", reads=(), writes=(), tag=""):
        sa = self._isap(s1) or self._isap(s2)
        if s2 is None:
            return self.add(engine,
                            lambda e, o=out, a=in0, x=s1, p0=op0:
                                e.tensor_single_scalar(out=o, in_=a, scalar=x,
                                                       op=p0),
                            reads=reads, writes=writes, tag=tag, sync_all=sa,
                            small_out=self._small(out))
        return self.add(engine,
                        lambda e, o=out, a=in0, x=s1, y=s2, p0=op0, p1=op1:
                            e.tensor_scalar(out=o, in0=a, scalar1=x, scalar2=y,
                                            op0=p0, op1=p1),
                        reads=reads, writes=writes, tag=tag, sync_all=sa,
                        small_out=self._small(out))

    def stt(self, out, in0, scalar, in1, op0, op1, accum=None,
            engine="vector", reads=(), writes=(), tag=""):
        return self.add(engine,
                        lambda e, o=out, a=in0, s=scalar, b=in1, p0=op0, p1=op1,
                               ac=accum:
                            e.scalar_tensor_tensor(out=o, in0=a, scalar=s, in1=b,
                                                   op0=p0, op1=p1, accum_out=ac),
                        reads=reads, writes=writes, tag=tag,
                        sync_all=self._isap(scalar),
                        small_out=self._small(out) or accum is not None)

    def red(self, out, in_, op=ALU.add, negate=False, reads=(), writes=(), tag=""):
        return self.add("vector",
                        lambda e, o=out, i=in_, p=op, n=negate:
                            e.tensor_reduce(out=o, in_=i, axis=AX.X, op=p, negate=n),
                        reads=reads, writes=writes, tag=tag, small_out=True)

    def copy(self, out, in_, engine="vector", reads=(), writes=(), tag=""):
        return self.add(engine,
                        lambda e, o=out, i=in_: e.tensor_copy(out=o, in_=i),
                        reads=reads, writes=writes, tag=tag,
                        small_out=self._small(out))

    def recip(self, out, in_, reads=(), writes=(), tag=""):
        return self.add("vector",
                        lambda e, o=out, i=in_: e.reciprocal(out=o, in_=i),
                        reads=reads, writes=writes, tag=tag, small_out=True)

    def scan(self, out, d0, d1, initial, reads=(), writes=(), tag=""):
        return self.add("vector",
                        lambda e, o=out, a=d0, b=d1, it=initial:
                            e.tensor_tensor_scan(out=o, data0=a, data1=b,
                                                 initial=it, op0=ALU.mult,
                                                 op1=ALU.add),
                        reads=reads, writes=writes, tag=tag,
                        sync_all=self._isap(initial),
                        small_out=self._small(out))

    def final_fence(self):
        last = {}
        for op in self.ops:
            if op.is_dma:
                last[op.engine] = op.idx
        self.add("sync", lambda e: e.nop(), extra_deps=set(last.values()),
                 tag="final_fence")

    def emit(self, st):
        nc = self.nc
        for op in self.ops:
            for d in op.deps:
                dep = self.ops[d]
                if (not dep.is_dma) and (SIM_SAFE or op.sync_all
                                         or dep.small_out
                                         or dep.engine != op.engine):
                    dep.milestone = True
        counts = dict.fromkeys(ALL_ENGINES, 0)
        for op in self.ops:
            if op.milestone and not op.is_dma:
                counts[op.engine] += 1
                op.mnum = counts[op.engine]
        engines_used = []
        for e in ALL_ENGINES:
            if any(op.engine == e for op in self.ops):
                engines_used.append(e)
        NPOOL = 12
        esem, qpool = {}, {}
        for e in engines_used:
            if any((not o.is_dma) and o.milestone and o.engine == e
                   for o in self.ops):
                esem[e] = st.enter_context(nc.semaphore("es_" + e))
            ndma = self._dma_count[e]
            if ndma:
                qpool[e] = [st.enter_context(nc.semaphore(f"qs_{e}{i}"))
                            for i in range(min(NPOOL, ndma))]
        def dma_sem_val(dep):
            pool = qpool[dep.engine]
            k = len(pool)
            return pool[dep.dma_idx % k], 16 * (dep.dma_idx // k + 1)
        sched = {e: [] for e in engines_used}
        hw = {}
        for op in self.ops:
            waits = {}
            if op.is_dma:
                pool = qpool[op.engine]
                k = len(pool)
                cyc = op.dma_idx // k
                if cyc > 0:
                    waits[pool[op.dma_idx % k]] = 16 * cyc
            for d in sorted(op.deps):
                dep = self.ops[d]
                if dep.is_dma:
                    sem, val = dma_sem_val(dep)
                elif (dep.engine == op.engine and not SIM_SAFE
                      and not op.sync_all and not dep.small_out):
                    continue
                else:
                    sem, val = esem[dep.engine], dep.mnum
                if waits.get(sem, -1) < val:
                    waits[sem] = val
            final = []
            for sem, val in waits.items():
                key = (op.engine, sem)
                if hw.get(key, 0) >= val:
                    continue
                hw[key] = val
                final.append((sem, val))
            inc = None
            if op.is_dma:
                sem, val = dma_sem_val(op)
                inc = (sem, 16)
            elif op.milestone:
                inc = (esem[op.engine], 1)
            sched[op.engine].append((final, op.fn, inc, op.tag))
        block = st.enter_context(nc.Block())
        def runner(entries):
            def run(eng):
                for waits, fn, inc, tag in entries:
                    for sem, val in waits:
                        eng.wait_ge(sem, val)
                    ins = fn(eng)
                    if inc is not None:
                        if ins is None:
                            ins = eng.nop()
                        ins.then_inc(inc[0], inc[1])
            return run
        for e in engines_used:
            getattr(block, e)(runner(sched[e]))


class Region:
    __slots__ = ("arena", "off", "n", "name", "dt", "ratio", "res_list", "R")
    def __init__(self, arena, off, n, name, dt, ratio, birth):
        self.arena, self.off, self.n, self.name = arena, off, n, name
        self.dt, self.ratio = dt, ratio
        self.R = R(name, birth=tuple(birth))
        self.res_list = [self.R]
    def sub(self, name):
        r = R(name, birth=self.R.birth)
        self.res_list.append(r)
        return r
    def subs(self, name, n):
        return [self.sub(f"{name}{i}") for i in range(n)]
    def v(self, p0=0, np_=128, c0=0, nc_=None):
        r = self.ratio
        if nc_ is None:
            nc_ = self.n // r - c0
        assert 0 <= c0 and (c0 + nc_) * r <= self.n, \
            (self.name, c0, nc_, self.n, r)
        ap = self.arena.ap[p0:p0 + np_,
                           self.off + c0 * r:self.off + (c0 + nc_) * r]
        if self.dt is not None:
            ap = ap.bitcast(self.dt)
        return ap
    def free(self):
        self.arena.free_region(self)

_DT_BYTES = {F32: 4, F32R: 4, BF16: 2, I32: 4}

class Arena:
    def __init__(self, st, nc, name, dtype, n_elems):
        self.h = st.enter_context(nc.sbuf_tensor(name, [128, n_elems], dtype))
        self.ap = self.h.ap()
        self.name = name
        self.dtype = dtype
        self.n = n_elems
        self._free = [(0, n_elems)]
        self._tombs = []
        self.hiwater = 0

    def alloc(self, n, name="", dt=None):
        ratio = 1
        if dt is not None and dt != self.dtype:
            ratio = _DT_BYTES[dt] // _DT_BYTES[self.dtype]
            assert ratio >= 1
        else:
            dt = None
        nn = (n * ratio + 15) & ~15
        for i, (o, sz) in enumerate(self._free):
            if sz >= nn:
                if sz == nn:
                    self._free.pop(i)
                else:
                    self._free[i] = (o + nn, sz - nn)
                self.hiwater = max(self.hiwater, o + nn)
                birth = []
                keep = []
                for (to, tn, tres) in self._tombs:
                    if to < o + nn and o < to + tn:
                        birth.extend(tres)
                        if to < o:
                            keep.append((to, o - to, tres))
                        if to + tn > o + nn:
                            keep.append((o + nn, to + tn - (o + nn), tres))
                    else:
                        keep.append((to, tn, tres))
                self._tombs = keep
                return Region(self, o, nn, name, dt, ratio, birth)
        raise MemoryError(f"{self.name}: cannot alloc {nn} cols for {name} "
                          f"(free={self._free})")

    def free_region(self, region):
        self._tombs.append((region.off, region.n, tuple(region.res_list)))
        self._free.append((region.off, region.n))
        self._free.sort()
        merged = []
        for seg in self._free:
            if merged and merged[-1][0] + merged[-1][1] == seg[0]:
                merged[-1] = (merged[-1][0], merged[-1][1] + seg[1])
            else:
                merged.append(list(seg))
        self._free = [tuple(s) for s in merged]


def _t(a):
    return np.ascontiguousarray(np.asarray(a).T)

def _rs_feat(a):
    a = np.asarray(a)
    f = a.shape[-1]
    ncol = -(-f // 128)
    lead = a.shape[:-1]
    ar = a.reshape(-1, f)
    out = np.zeros((ar.shape[0], 128, ncol), a.dtype)
    for i in range(ar.shape[0]):
        pad = np.zeros(128 * ncol, a.dtype)
        pad[:f] = ar[i]
        out[i] = pad.reshape(ncol, 128).T
    return out.reshape(lead + (128, ncol))

def host_prepare(inputs):
    g = {}
    g["l0a_wqkvT"] = _t(inputs["L0_attn_W_qkv"]).astype(np.float32)
    g["l0a_woutT"] = _t(inputs["L0_attn_W_out"]).astype(np.float32)
    g["wqkv"] = np.ascontiguousarray(inputs["L0_ffn_Wqkv"], np.float32)
    g["wo"] = np.ascontiguousarray(inputs["L0_ffn_Wo"], np.float32)
    g["w1"] = np.ascontiguousarray(inputs["L0_ffn_W1"], np.float32)
    w2 = np.asarray(inputs["L0_ffn_W2"], np.float32)
    w2p = np.zeros((6, 2816, 1024), np.float32)
    w2p[:, :DI, :] = w2
    g["w2"] = w2p
    g["qs"] = np.ascontiguousarray(inputs["L0_ffn_q_scale"], np.float32).reshape(6, 1, 64)
    g["ks"] = np.ascontiguousarray(inputs["L0_ffn_k_scale"], np.float32).reshape(6, 1, 64)
    g["ln_g"] = _rs_feat(np.asarray(inputs["L0_ffn_ln_g"], np.float32))
    g["ln_b"] = _rs_feat(np.asarray(inputs["L0_ffn_ln_b"], np.float32))
    b1 = np.asarray(inputs["L0_ffn_b1"], np.float32)
    g["b1a"] = _rs_feat(b1[:, :DI])
    g["b1g"] = _rs_feat(b1[:, DI:])
    g["b2"] = _rs_feat(np.asarray(inputs["L0_ffn_b2"], np.float32))
    g["l1_ap"] = np.asarray(inputs["L1_attn_a_params"], np.float32).reshape(1, 256)
    g["l1_bwT"] = _t(inputs["L1_attn_B_w"]).astype(np.float32)
    g["l1_cwT"] = _t(inputs["L1_attn_C_w"]).astype(np.float32)
    g["l1_D"] = _rs_feat(np.asarray(inputs["L1_attn_D"], np.float32))
    g["l1_wgT"] = _t(inputs["L1_ffn_w_gate"]).astype(np.int32)
    g["l1_wuT"] = _t(inputs["L1_ffn_w_up"]).astype(np.int32)
    g["l1_wdT"] = _t(inputs["L1_ffn_w_down"]).astype(np.int32)
    x = np.asarray(inputs["L0_attn_x"], np.float32)
    xs = [np.ascontiguousarray(x[b].T) for b in range(N_CORES)]
    return g, xs


def make_consts():
    c = {}
    c["ident_bf"] = np.eye(128, dtype=np.float32)
    c["ident_f32"] = np.eye(128, dtype=np.float32)
    c["ones_r_col"] = np.ones((128, 1), np.float32)
    c["ones_r_row"] = np.ones((1, 128), np.float32)
    c["ones_bf_col"] = np.ones((128, 1), np.float32)
    c["ones_bf_row"] = np.ones((1, 128), np.float32)
    c["ones_f32_row64"] = np.ones((1, 64), np.float32)
    c["ones_f32_row"] = np.ones((1, 128), np.float32)
    c["ones_f32_big"] = np.ones((128, 512), np.float32)
    c["zeros_r"] = np.zeros((128, 16), np.float32)
    c["eps_col"] = np.full((128, 1), 1e-5, np.float32)
    half = np.arange(0, DH, 2, dtype=np.float64)
    inv_freq = 1.0 / 10000.0 ** (half / DH)
    t = np.arange(SEQ, dtype=np.float64)
    freqs = np.concatenate([t[:, None] * inv_freq[None, :]] * 2, -1)
    base_scale = (half + 0.4 * DH) / (1.4 * DH)
    power = (t - SEQ // 2) / (WIN // 2)
    scale = np.concatenate([base_scale[None, :] ** power[:, None]] * 2, -1)
    cos, sin = np.cos(freqs), np.sin(freqs)
    sgn = np.concatenate([-np.ones(32), np.ones(32)])[None, :]
    def tile_major(tab):
        return np.ascontiguousarray(
            tab.reshape(16, 128, 64).transpose(1, 0, 2).reshape(128, 16 * 64)
        ).astype(np.float32)
    c["rot_t1q"] = tile_major(cos * scale)
    c["rot_t2q"] = tile_major(sin * scale * sgn)
    c["rot_t1k"] = tile_major(cos / scale)
    c["rot_t2k"] = tile_major(sin / scale * sgn)
    jj = np.arange(512)[None, :]
    j2 = np.arange(256)[None, :]
    for s in range(2):
        ii = (np.arange(128) + 128 * s)[:, None]
        c[f"mask_s{s}"] = np.where((jj >= ii) & (jj <= ii + 256), 0.0,
                                   -1e30).astype(np.float32)
        c[f"mask0_s{s}"] = np.where(j2 <= ii, 0.0, -1e30).astype(np.float32)
    qq = np.arange(256)[None, :]
    for ch in range(4):
        j = (np.arange(128) + 128 * ch)[:, None]
        c[f"bmask_c{ch}"] = ((qq >= j - 256) & (qq <= j)).astype(np.float32)
    for ch in range(2):
        j = (np.arange(128) + 128 * ch)[:, None]
        c[f"bmask0_c{ch}"] = (qq >= j).astype(np.float32)
    cc = np.arange(256)[None, :]
    for s in range(2):
        ii = (np.arange(128) + 128 * s)[:, None]
        c[f"cmask_s{s}"] = np.where(cc <= ii, 0.125, 0.0).astype(np.float32)
    M = np.zeros((512, 256), np.float64)
    jv = np.arange(512)
    for m in range(1, 257):
        M[:, m - 1] += -np.sin(2 * np.pi * jv * m / 512)
    for m in range(257, 512):
        M[:, 512 - m] += np.sin(2 * np.pi * jv * m / 512)
    c["l1_M"] = np.ascontiguousarray(
        M.reshape(4, 128, 256).transpose(1, 0, 2).reshape(128, 4 * 256)
    ).astype(np.float32)
    return c

_CONSTS = make_consts()

_CONST_DT = {
    "ident_bf": BF16, "ident_f32": F32,
    "ones_r_col": F32R, "ones_r_row": F32R,
    "ones_bf_col": BF16, "ones_bf_row": BF16,
    "ones_f32_row64": F32, "ones_f32_row": F32, "ones_f32_big": F32,
    "zeros_r": F32R, "eps_col": F32,
    "rot_t1q": BF16, "rot_t2q": BF16, "rot_t1k": BF16, "rot_t2k": BF16,
    "mask_s0": BF16, "mask_s1": BF16, "mask0_s0": F32, "mask0_s1": F32,
    "bmask_c0": BF16, "bmask_c1": BF16, "bmask_c2": BF16, "bmask_c3": BF16,
    "bmask0_c0": BF16, "bmask0_c1": BF16,
    "cmask_s0": BF16, "cmask_s1": BF16,
    "l1_M": F32,
}


ARENA_R_ELEMS = 25700
ARENA_B_ELEMS = 36300

class Build:
    def __init__(self, phases):
        self.phases = phases
        self.nc = bass.Bass("TRN2", target_bir_lowering=False, debug=False)
        self.st = contextlib.ExitStack()
        self.P = Prog(self.nc)
        self.shared_inputs = {}
        self._uid = 0

    def uid(self):
        self._uid += 1
        return self._uid

    def din(self, name, shape, dt):
        return self.nc.dram_tensor(name, list(shape), dt, kind="ExternalInput").ap()

    def dout(self, name, shape, dt):
        return self.nc.dram_tensor(name, list(shape), dt, kind="ExternalOutput").ap()

    def dint(self, name, shape, dt):
        return self.nc.dram_tensor(name, list(shape), dt, kind="Internal").ap()

    def win(self, name, arr, dt=None):
        if dt is None:
            dt = {np.dtype(np.float32): F32, np.dtype(np.int32): I32}[arr.dtype]
        self.shared_inputs[name] = arr
        return self.din(name, arr.shape, dt)

    def setup(self):
        nc, st = self.nc, self.st
        self.ar = Arena(st, nc, "arena_r", F32R, ARENA_R_ELEMS)
        self.ab = Arena(st, nc, "arena_b", BF16, ARENA_B_ELEMS)
        self.sf = st.enter_context(nc.sbuf_tensor("sf_small", [128, 1024], F32))
        self.sfR = R("sf_small")
        self.sfRp = [R(f"sfp{j}") for j in range(4)]
        if "l1attn" in self.phases or "test" in self.phases:
            self.tabf = st.enter_context(
                nc.sbuf_tensor("tabf", [128, 2560], F32))
        self.pb = [st.enter_context(nc.psum_tensor(f"pb{i}", [128, 512], F32))
                   for i in range(6)]
        self.tb = [st.enter_context(nc.psum_tensor(f"tb{i}", [128, 1024], BF16))
                   for i in range(2)]
        self.pbR = [R(f"pb{i}") for i in range(6)]
        self.tbR = [R(f"tb{i}") for i in range(2)]
        self.c = {}
        self.cR = R("consts")
        for name in self.const_names():
            arr = _CONSTS[name]
            dt = _CONST_DT[name]
            npdt = {F32: np.float32, F32R: np.float32, BF16: BF16NP}[dt]
            self.shared_inputs["c_" + name] = np.ascontiguousarray(arr.astype(npdt))
            hin = self.din("c_" + name, arr.shape,
                           F32 if dt == F32R else dt)
            t = st.enter_context(nc.sbuf_tensor("sc_" + name, list(arr.shape), dt))
            src = hin.bitcast(F32R) if dt == F32R else hin
            self.P.dma("sync", t.ap()[:, :], src, writes=[self.cR],
                       tag="c_" + name)
            self.c[name] = t.ap()
        return self

    def const_names(self):
        names = ["ident_bf", "ident_f32", "ones_r_col", "ones_r_row",
                 "ones_bf_col", "ones_bf_row", "ones_f32_row64",
                 "ones_f32_row", "ones_f32_big", "zeros_r", "eps_col"]
        if "l0ffn" in self.phases:
            names += ["rot_t1q", "rot_t2q", "rot_t1k", "rot_t2k",
                      "mask_s0", "mask_s1",
                      "bmask_c0", "bmask_c1", "bmask_c2", "bmask_c3"]
        if "l0head" in self.phases:
            names += ["cmask_s0", "cmask_s1"]
        if "l1attn" in self.phases:
            names += ["l1_M"]
        return names

    def alloc_x(self):
        self.x = self.ar.alloc(NG * SEQ, "x")
        self.xR = self.x.subs("x", NG)

    def x_view(self, g, c0=0, n=SEQ, p0=0, np_=128):
        if getattr(self, "x_alias_xi", False):
            return self.xi_data(g, c0, n, p0, np_)
        return self.x.v(p0, np_, g * SEQ + c0, n)

    def load_x(self, src_ap):
        s = src_ap.rearrange("(g p) t -> p g t", p=128)
        self.P.dma("sync",
                   self.x.v(0, 128, 0, NG * SEQ).rearrange(
                       "p (g t) -> p g t", t=SEQ),
                   s.bitcast(F32R), writes=list(self.xR), tag="load_x")

    def store_x(self, dst_ap, tag="store_x"):
        if getattr(self, "x_alias_xi", False):
            return self.store_xi_as(dst_ap, tag=tag)
        d = dst_ap.rearrange("(g p) t -> p g t", p=128)
        return self.P.dma(
            "sync", d,
            self.x.v().rearrange("p (g t) -> p g t", t=SEQ).bitcast(F32),
            reads=list(self.xR), tag=tag)

    def phase_combine_split(self, x_h, a_h, f_h):
        P = self.P
        srcs = [h.rearrange("(g p) t -> p g t", p=128)
                for h in (x_h, a_h, f_h)]
        bufs = [self.ar.alloc(512, f"cmb{i}") for i in range(4)]
        for g_ in range(NG):
            for ch in range(NCH):
                i2 = ((g_ * NCH + ch) % 2) * 2
                bA, bB = bufs[i2], bufs[i2 + 1]
                sl = (slice(None), g_, slice(ch * 512, (ch + 1) * 512))
                P.dma("sync", bA.v(), srcs[0][sl].bitcast(F32R),
                      writes=[bA.R], tag=f"cmbx{g_}_{ch}")
                P.dma("sync", bB.v(), srcs[1][sl].bitcast(F32R),
                      writes=[bB.R], tag=f"cmba{g_}_{ch}")
                P.tt(self.x_view(g_, ch * 512, 512), bA.v().bitcast(F32),
                     bB.v().bitcast(F32), ALU.add,
                     reads=[bA.R, bB.R], writes=[self.xR[g_]],
                     tag=f"cmb1{g_}_{ch}")
                P.dma("sync", bA.v(), srcs[2][sl].bitcast(F32R),
                      writes=[bA.R], tag=f"cmbf{g_}_{ch}")
                P.tt(self.x_view(g_, ch * 512, 512),
                     self.x_view(g_, ch * 512, 512).bitcast(F32),
                     bA.v().bitcast(F32), ALU.add,
                     reads=[bA.R, self.xR[g_]], writes=[self.xR[g_]],
                     tag=f"cmb2{g_}_{ch}")
        for b_ in bufs:
            b_.free()

    def phase_combine_fused(self, x_h, aC3, aCR):
        P = self.P
        xsrc = x_h.rearrange("(g p) t -> p g t", p=128)
        bufs = [self.ar.alloc(512, f"cmf{i}") for i in range(4)]
        for g_ in range(NG):
            for ch in range(NCH):
                i2 = ((g_ * NCH + ch) % 2) * 2
                bA, bB = bufs[i2], bufs[i2 + 1]
                sl = (slice(None), g_, slice(ch * 512, (ch + 1) * 512))
                P.dma("sync", bA.v(), xsrc[sl].bitcast(F32R),
                      writes=[bA.R], tag=f"cmfx{g_}_{ch}")
                P.dma("sync", bB.v(), aC3[sl].bitcast(F32R),
                      reads=[aCR], writes=[bB.R], tag=f"cmfa{g_}_{ch}")
                P.tt(bA.v(), bA.v().bitcast(F32), bB.v().bitcast(F32),
                     ALU.add, reads=[bA.R, bB.R], writes=[bA.R],
                     tag=f"cmf1{g_}_{ch}")
                P.tt(self.xi_data(g_, ch * 512, 512),
                     self.xi_data(g_, ch * 512, 512).bitcast(F32),
                     bA.v().bitcast(F32), ALU.add,
                     reads=[bA.R, self.xiR[g_]], writes=[self.xiR[g_]],
                     tag=f"cmf2{g_}_{ch}")
        for b_ in bufs:
            b_.free()
        self.x_alias_xi = True
        self.xR = self.xiR

    def alloc_xi(self):
        self.xi = self.ar.alloc(NG * (SEQ + 1), "xi")
        self.xiR = self.xi.subs("xi", NG)
        z = self.c["zeros_r"]
        for g in range(NG):
            self.P.dma("sync", self.xi.v(0, 128, g * (SEQ + 1), 1), z[:, 0:1],
                       reads=[self.cR], writes=[self.xiR[g]], tag=f"xi_z{g}")

    def xi_data(self, g, c0=0, n=SEQ, p0=0, np_=128):
        return self.xi.v(p0, np_, g * (SEQ + 1) + 1 + c0, n)

    def xi_shift(self, g, c0=0, n=SEQ, p0=0, np_=128):
        return self.xi.v(p0, np_, g * (SEQ + 1) + c0, n)

    class ResidentSrc:
        def __init__(self, view_fn, res_list):
            self.view_fn, self.res = view_fn, res_list
        def chunk(self, b, ch):
            return [(self.view_fn(g, ch * 512, 512), self.res[g])
                    for g in range(NG)]
        def release(self):
            pass

    class HbmSrc:
        def __init__(self, b, dram_ap, name):
            self.ap3 = dram_ap.rearrange("(g p) t -> p g t", p=128)
            self.name = name
            self.bufs = [b.ar.alloc(512, f"{name}{g}") for g in range(NG)]
            self.res = [buf.R for buf in self.bufs]
        def chunk(self, b, ch):
            out = []
            for g in range(NG):
                b.P.dma("sync", self.bufs[g].v(),
                        self.ap3[:, g, ch * 512:(ch + 1) * 512].bitcast(F32R),
                        writes=[self.res[g]], tag=f"{self.name}{ch}_{g}")
                out.append((self.bufs[g].v(), self.res[g]))
            return out
        def release(self):
            for r in self.bufs:
                r.free()

    def emit_ln(self, src, cb, tag):
        P, c = self.P, self.c
        u = self.uid()
        sq = [self.ar.alloc(512, f"lnq{u}_{i}") for i in range(2)]
        sqR = [b.R for b in sq]
        rows = self.ab.alloc(5 * 512, f"lnrows{u}", dt=F32)
        rowR = rows.R
        SX, SQM, MU, T1, T2 = range(5)
        rv = lambda i: rows.v(0, 1, i * 512, 512)
        rstd = self.ar.alloc(512, f"lnrstd{u}")
        bprime = self.ar.alloc(512, f"lnbp{u}")
        rstdR = rstd.R
        bprime.res_list.append(rstdR)
        a_b = self.ab.alloc(512, f"lnab{u}", dt=F32)
        b_b = self.ab.alloc(512, f"lnbb{u}", dt=F32)
        t1b = [self.ab.alloc(512, f"lnt1{u}_{i}", dt=F32) for i in range(2)]
        abR, bbR = a_b.R, b_b.R
        t1R = [b.R for b in t1b]
        for ch in range(NCH):
            tiles = src.chunk(self, ch)
            psS, psSR = self.pb[0].ap()[0:1, :], self.pbR[0]
            psQ, psQR = self.pb[1].ap()[0:1, :], self.pbR[1]
            for g in range(NG):
                ap_, res_ = tiles[g]
                P.mm(psS, c["ones_r_col"][:, :], ap_, g == 0, g == NG - 1,
                     reads=[res_, self.cR], writes=[psSR],
                     tag=f"{tag}sx{ch}_{g}")
            for g in range(NG):
                ap_, res_ = tiles[g]
                b = sq[g % 2]
                P.act(b.v(), ap_, AF.Square, reads=[res_],
                      writes=[sqR[g % 2]], tag=f"{tag}q{ch}_{g}")
                P.mm(psQ, c["ones_r_col"][:, :], b.v(), g == 0, g == NG - 1,
                     reads=[sqR[g % 2], self.cR], writes=[psQR],
                     tag=f"{tag}qm{ch}_{g}")
            P.act(rv(SX), psS, AF.Copy, reads=[psSR], writes=[rowR],
                  tag=f"{tag}evS{ch}")
            P.act(rv(SQM), psQ, AF.Copy, reads=[psQR], writes=[rowR],
                  tag=f"{tag}evQ{ch}")
            P.ts(rv(MU), rv(SX), 1.0 / DM, None, reads=[rowR], writes=[rowR],
                 tag=f"{tag}mu{ch}")
            P.stt(rv(T1), rv(MU), -1.0, rv(MU), ALU.mult, ALU.mult,
                  reads=[rowR], writes=[rowR], tag=f"{tag}mu2{ch}")
            P.stt(rv(T2), rv(SQM), 1.0 / DM, rv(T1), ALU.mult, ALU.add,
                  reads=[rowR], writes=[rowR], tag=f"{tag}var{ch}")
            P.act(rv(T1), rv(T2), AF.Ln, bias=c["eps_col"][0:1, :],
                  reads=[rowR, self.cR], writes=[rowR], tag=f"{tag}ln{ch}")
            P.act(rstd.v(0, 1), rv(T1), AF.Exp, scale=-0.5,
                  reads=[rowR], writes=[rstdR], tag=f"{tag}rstd{ch}")
            P.stt(bprime.v(0, 1), rv(MU), -1.0, rstd.v(0, 1).bitcast(F32),
                  ALU.mult, ALU.mult,
                  reads=[rowR, rstdR], writes=[rstdR], tag=f"{tag}bp{ch}")
            psA, psAR = self.pb[2].ap()[:, :], self.pbR[2]
            psB, psBR = self.pb[3].ap()[:, :], self.pbR[3]
            P.mm(psA, c["ones_r_row"][:, :], rstd.v(0, 1), True, True,
                 reads=[rstdR, self.cR], writes=[psAR], tag=f"{tag}bcA{ch}")
            P.mm(psB, c["ones_r_row"][:, :], bprime.v(0, 1), True, True,
                 reads=[rstdR, self.cR], writes=[psBR], tag=f"{tag}bcB{ch}")
            P.act(a_b.v(), psA, AF.Copy, reads=[psAR], writes=[abR],
                  tag=f"{tag}abe{ch}")
            P.act(b_b.v(), psB, AF.Copy, reads=[psBR], writes=[bbR],
                  tag=f"{tag}bbe{ch}")
            for g in range(NG):
                ap_, res_ = tiles[g]
                t1 = t1b[g % 2]
                P.tt(t1.v(), ap_.bitcast(F32), a_b.v(), ALU.mult,
                     reads=[res_, abR], writes=[t1R[g % 2]],
                     tag=f"{tag}t1_{ch}_{g}")
                cb(g, ch, ap_, res_, t1.v(), t1R[g % 2], b_b.v(), bbR)
        for reg in (rows, rstd, bprime, a_b, b_b, t1b[0], t1b[1],
                    sq[0], sq[1]):
            reg.free()
        src.release()

    def phase_ln0(self, x_dram):
        src = self.HbmSrc(self, x_dram, "x0s")
        def cb(g, ch, tile, tileR, t1, t1R, b_b, bbR):
            self.P.tt(self.xi_data(g, ch * 512, 512), t1, b_b, ALU.add,
                      reads=[t1R, bbR], writes=[self.xiR[g]],
                      tag=f"ln0o{ch}_{g}")
        self.emit_ln(src, cb, "ln0")

    def phase_ln1(self, d_rs_ap, dR):
        self.xnb = self.ab.alloc(NG * SEQ, "xnb")
        self.xnbR = self.xnb.subs("xnb", NG)
        t2 = [self.ab.alloc(512, f"ln1t2_{i}", dt=F32) for i in range(2)]
        t2R = [b.R for b in t2]
        def cb(g, ch, tile, tileR, t1, t1R, b_b, bbR):
            P = self.P
            t = t2[g % 2]
            P.tt(t.v(), t1, b_b, ALU.add, reads=[t1R, bbR],
                 writes=[t2R[g % 2]], tag=f"ln1t2_{ch}_{g}")
            P.copy(self.xnb_view(g, ch * 512, 512), t.v(),
                   reads=[t2R[g % 2]], writes=[self.xnbR[g]],
                   tag=f"ln1xn_{ch}_{g}")
            P.stt(self.x_view(g, ch * 512, 512), t.v(), d_rs_ap[:, g:g + 1],
                  self.x_view(g, ch * 512, 512).bitcast(F32),
                  ALU.mult, ALU.add,
                  reads=[t2R[g % 2], dR, self.xR[g]], writes=[self.xR[g]],
                  tag=f"ln1D_{ch}_{g}")
        self.emit_ln(self.ResidentSrc(self.x_view, self.xR), cb, "ln1")

    def xnb_view(self, g, c0=0, n=SEQ, p0=0, np_=128):
        return self.xnb.v(p0, np_, g * SEQ + c0, n)

    def phase_l0ffn(self, hs):
        glu_st = self.dint("glu_st", (2816, SEQ), BF16)
        self._glu3 = glu_st.rearrange("(g p) t -> p g t", p=128)
        self._gluR = R("glu_st")
        w2st = self.dint("w2bf_st", (NG, 128, 2816), BF16)
        self._w2st = w2st
        self._w2stR = R("w2bf_st")
        for i in range(DEPTH_FFN):
            pl, qsb, ftab = self._l0f_params(i, hs)
            self._l0f_attn(i, hs, ftab)
            ftab.free()
            qsb.free()
            self._l0f_ffn(i, hs, pl)
            pl.free()

    def _l0f_params(self, i, hs):
        P, c = self.P, self.c
        pl = self.ab.alloc(68, f"pl{i}", dt=F32)
        for name, c0, n in (("lng", 0, 8), ("lnb", 8, 16), ("b1a", 16, 38),
                            ("b1g", 38, 60), ("b2", 60, 68)):
            P.dma("sync", pl.v(0, 128, c0, n - c0), hs[name][i],
                  writes=[pl.R], tag=f"pl_{name}{i}")
        sfa = self.sf.ap()
        P.dma("sync", sfa[0:1, 576:640], hs["qs"][i], writes=[self.sfR],
              tag=f"pl_qs{i}")
        P.dma("sync", sfa[0:1, 640:704], hs["ks"][i], writes=[self.sfR],
              tag=f"pl_ks{i}")
        qsb = self.ab.alloc(256, f"qsb{i}")
        for idx, (src0, dst0) in enumerate(((576, 0), (640, 128))):
            psq, psqR = self.pb[0].ap()[:, 0:64], self.pbR[0]
            P.mm(psq, c["ones_f32_row"][:, :], sfa[0:1, src0:src0 + 64],
                 True, True, reads=[self.sfR, self.cR], writes=[psqR],
                 tag=f"qsbc{i}_{idx}")
            P.act(qsb.v(0, 128, dst0, 64), psq, AF.Copy,
                  reads=[psqR], writes=[qsb.R], tag=f"qsbe{i}_{idx}")
            P.act(qsb.v(0, 128, dst0 + 64, 32), self.pb[0].ap()[:, 32:64],
                  AF.Copy, reads=[psqR], writes=[qsb.R], tag=f"qsw1{i}_{idx}")
            P.act(qsb.v(0, 128, dst0 + 96, 32), self.pb[0].ap()[:, 0:32],
                  AF.Copy, reads=[psqR], writes=[qsb.R], tag=f"qsw2{i}_{idx}")
        ftab = self.ab.alloc(4 * 1024, f"ftab{i}")
        for o, (tab, qcol) in enumerate((("rot_t1q", 0), ("rot_t2q", 64),
                                         ("rot_t1k", 128), ("rot_t2k", 192))):
            P.tt(ftab.v(0, 128, o * 1024, 1024).rearrange(
                     "p (a c) -> p a c", c=64),
                 c[tab][:, :].rearrange("p (a c) -> p a c", c=64),
                 qsb.v(0, 128, qcol, 64).rearrange(
                     "p (a c) -> p a c", a=1).broadcast_to((128, 16, 64)),
                 ALU.mult, reads=[qsb.R, self.cR], writes=[ftab.R],
                 tag=f"ftab{i}_{o}")
        return pl, qsb, ftab

    def _l0f_attn(self, i, hs, ftab):
        P, c = self.P, self.c
        vtm = self.ab.alloc(6 * 1024, f"vtm{i}")
        qfm = self.ab.alloc(NG * 512, f"qfm{i}")
        kfm = self.ab.alloc(NG * 768, f"kfm{i}")
        wst = [self.ar.alloc(NG * 512, f"l0fw{i}_{j}") for j in range(2)]
        rA = [self.ab.alloc(1024, f"rA{i}_{j}") for j in range(2)]
        rB = [self.ab.alloc(1024, f"rB{i}_{j}") for j in range(2)]
        tbank = self.tb[0].ap()
        sfa = self.sf.ap()
        rdn = sfa[0:1, 64:320]
        rdn2 = sfa[0:1, 320:576]
        for q4 in range(4):
            t0 = q4 * 512
            qtm = self.ab.alloc(4 * 1024, f"qtm{i}_{q4}")
            ktm = self.ab.alloc(4 * 1024, f"ktm{i}_{q4}")
            if q4 > 0:
                P.copy(vtm.v(0, 128, 0, 2048), vtm.v(0, 128, 4096, 2048),
                       reads=[vtm.R], writes=[vtm.R], tag=f"vh{i}_{q4}")
                for f in range(NG):
                    P.copy(kfm.v(0, 128, f * 768, 256),
                           kfm.v(0, 128, f * 768 + 512, 256),
                           reads=[kfm.R], writes=[kfm.R],
                           tag=f"kh{i}_{q4}_{f}")
            for wc in range(6):
                w = wst[wc % 2]
                P.dma("sync",
                      w.v(0, 128, 0, NG * 512).rearrange(
                          "p (g f) -> p g f", f=512),
                      hs["wqkv"][i][:, wc * 512:(wc + 1) * 512].rearrange(
                          "(g p) f -> p g f", p=128).bitcast(F32R),
                      writes=[w.R], tag=f"fw{i}_{q4}_{wc}")
                for tt in range(4):
                    ps, psR = self.pb[tt % 2].ap()[:, :], self.pbR[tt % 2]
                    for k in range(NG):
                        lhsT = (self.xi_data(k, t0 + tt * 128, 128) if k < 4
                                else self.xi_shift(k, t0 + tt * 128, 128))
                        P.mm(ps, lhsT, w.v(0, 128, k * 512, 512),
                             k == 0, k == NG - 1,
                             reads=[self.xiR[k], w.R], writes=[psR],
                             tag=f"fqkv{i}_{q4}_{wc}_{tt}_{k}")
                    dst = (qtm, ktm, vtm)[wc // 2]
                    base = (tt * 1024 if wc < 4 else (2 + tt) * 1024)
                    dv = dst.v(0, 128, base + (wc % 2) * 512, 512)
                    P.act(dv, ps, AF.Copy, reads=[psR], writes=[dst.R],
                          tag=f"fev{i}_{q4}_{wc}_{tt}")
            for qk in range(2):
                src = qtm if qk == 0 else ktm
                dstfm = qfm if qk == 0 else kfm
                fb = ftab.v
                f1c = 0 if qk == 0 else 2048
                f2c = 1024 if qk == 0 else 3072
                for tt in range(4):
                    tg = q4 * 4 + tt
                    tpar = (qk * 4 + tt) % 2
                    teng = "vector" if tpar == 0 else "gpsimd"
                    rAv, rBv = rA[tpar], rB[tpar]
                    sb0 = 704 + 96 * tpar
                    t3 = lambda ap: ap.rearrange("p (a c) -> p a c", c=64)
                    sv = src.v(0, 128, tt * 1024, 1024)
                    P.act(rAv.v(), sv, AF.Square, reads=[src.R],
                          writes=[rAv.R], tag=f"rsq{i}_{q4}_{qk}_{tt}")
                    P.red(sfa[:, sb0:sb0 + 16].rearrange("p (a c) -> p a c", c=1),
                          t3(rAv.v()), ALU.add, reads=[rAv.R],
                          writes=[self.sfRp[tpar]],
                          tag=f"rss{i}_{q4}_{qk}_{tt}")
                    P.act(sfa[:, sb0 + 16:sb0 + 32], sfa[:, sb0:sb0 + 16],
                          AF.Sqrt,
                          reads=[self.sfRp[tpar]], writes=[self.sfRp[tpar]],
                          tag=f"rst{i}_{q4}_{qk}_{tt}")
                    P.ts(sfa[:, sb0 + 32:sb0 + 48],
                         sfa[:, sb0 + 16:sb0 + 32], 1e-12, None,
                         op0=ALU.max, reads=[self.sfRp[tpar]],
                         writes=[self.sfRp[tpar]],
                         tag=f"rmx{i}_{q4}_{qk}_{tt}")
                    P.recip(sfa[:, sb0 + 48:sb0 + 64],
                            sfa[:, sb0 + 32:sb0 + 48],
                            reads=[self.sfRp[tpar]],
                            writes=[self.sfRp[tpar]],
                            tag=f"rrc{i}_{q4}_{qk}_{tt}")
                    if qk == 0:
                        P.ts(sfa[:, sb0 + 64:sb0 + 80],
                             sfa[:, sb0 + 48:sb0 + 64], 8.0, None,
                             reads=[self.sfRp[tpar]],
                             writes=[self.sfRp[tpar]],
                             tag=f"rq8{i}_{q4}_{tt}")
                    P.tt(rAv.v().rearrange("p (a c) -> p a c", c=64),
                         t3(sv),
                         ftab.v(0, 128, f1c + tg * 64, 64).rearrange(
                             "p (a c) -> p a c", a=1).broadcast_to(
                                 (128, 16, 64)),
                         ALU.mult, engine=teng,
                         reads=[src.R, ftab.R], writes=[rAv.R],
                         tag=f"rA{i}_{q4}_{qk}_{tt}")
                    sv3 = src.v(0, 128, tt * 1024, 1024).rearrange(
                        "p (a h c) -> p a h c", h=2, c=32)
                    rB3 = rBv.v().rearrange("p (a h c) -> p a h c", h=2, c=32)
                    P.tt(rB3[:, :, 0, :], sv3[:, :, 1, :],
                         ftab.v(0, 128, f2c + tg * 64, 32).rearrange(
                             "p (a c) -> p a c", a=1).broadcast_to(
                                 (128, 16, 32)),
                         ALU.mult, engine=teng,
                         reads=[src.R, ftab.R], writes=[rBv.R],
                         tag=f"rB1{i}_{q4}_{qk}_{tt}")
                    P.tt(rB3[:, :, 1, :], sv3[:, :, 0, :],
                         ftab.v(0, 128, f2c + tg * 64 + 32, 32).rearrange(
                             "p (a c) -> p a c", a=1).broadcast_to(
                                 (128, 16, 32)),
                         ALU.mult, engine=teng,
                         reads=[src.R, ftab.R], writes=[rBv.R],
                         tag=f"rB2{i}_{q4}_{qk}_{tt}")
                    P.tt(rAv.v(), rAv.v(), rBv.v(), ALU.add, engine=teng,
                         reads=[rAv.R, rBv.R], writes=[rAv.R],
                         tag=f"rC{i}_{q4}_{qk}_{tt}")
                    rncol = (sb0 + 64) if qk == 0 else (sb0 + 48)
                    P.tt(rBv.v().rearrange("p (a c) -> p a c", c=64),
                         t3(rAv.v()),
                         sfa[:, rncol:rncol + 16].rearrange(
                             "p (a c) -> p a c", c=1).broadcast_to(
                                 (128, 16, 64)),
                         ALU.mult, engine=teng,
                         reads=[rAv.R, self.sfRp[tpar]],
                         writes=[rBv.R],
                         tag=f"rD{i}_{q4}_{qk}_{tt}")
                    tbankp = self.tb[tpar].ap()
                    for f in range(NG):
                        tb_ap = tbankp[:, (f % 8) * 128:(f % 8) * 128 + 128]
                        P.trans(tb_ap, rBv.v(0, 128, f * 128, 128),
                                c["ident_bf"][:, :],
                                reads=[rBv.R, self.cR],
                                writes=[self.tbR[tpar]],
                                tag=f"rT{i}_{q4}_{qk}_{tt}_{f}")
                        dcol = (f * 512 + tt * 128 if qk == 0
                                else f * 768 + 256 + tt * 128)
                        P.copy(dstfm.v(0, 128, dcol, 128), tb_ap,
                               reads=[self.tbR[tpar]], writes=[dstfm.R],
                               tag=f"rTe{i}_{q4}_{qk}_{tt}_{f}")
            qtm.free()
            ktm.free()
            atout = self.ab.alloc(NG * 512, f"ato{i}_{q4}")
            pn = [self.ab.alloc(1024, f"pn{i}_{q4}_{j}") for j in range(2)]
            pT = [self.ab.alloc(4 * 256, f"pT{i}_{q4}_{j}") for j in range(2)]
            insts = [(h, wl) for h in range(HEADS) for wl in range(2)]

            def emit_A(j):
                h, wl = insts[j]
                f = h // 2
                pb_ = (h % 2) * 64
                w = q4 * 2 + wl
                W2_ = j % 2
                pnW = pn[W2_]
                bk1a, bk1b = 2 * W2_, 2 * W2_ + 1
                kw = 256 if w == 0 else 512
                kcol = f * 768 + (256 if w == 0 else wl * 256)
                qcol = f * 512 + wl * 256
                nmc = 880 + 4 * W2_
                rdc = 888 + 2 * W2_
                aeng = "vector" if W2_ == 0 else "gpsimd"
                for s in range(2):
                    bk = bk1a if s == 0 else bk1b
                    P.mm(self.pb[bk].ap()[:, 0:kw],
                         qfm.v(pb_, 64, qcol + s * 128, 128),
                         kfm.v(pb_, 64, kcol, kw), True, True,
                         reads=[qfm.R, kfm.R],
                         writes=[self.pbR[bk]],
                         tag=f"m1{i}_{w}_{h}_{s}")
                    mask = (c[f"mask_s{s}"][:, 256:512] if w == 0
                            else c[f"mask_s{s}"][:, 0:kw])
                    P.tt(self.pb[bk].ap()[:, 0:kw],
                         self.pb[bk].ap()[:, 0:kw],
                         mask, ALU.add,
                         reads=[self.pbR[bk], self.cR],
                         writes=[self.pbR[bk]], tag=f"mk{i}_{w}_{h}_{s}")
                    P.red(sfa[:, nmc + s:nmc + s + 1],
                          self.pb[bk].ap()[:, 0:kw],
                          ALU.max, negate=True,
                          reads=[self.pbR[bk]],
                          writes=[self.sfRp[2 + W2_]],
                          tag=f"nm{i}_{w}_{h}_{s}")
                    P.act(pnW.v(0, 128, s * 512, kw),
                          self.pb[bk].ap()[:, 0:kw], AF.Exp,
                          bias=sfa[:, nmc + s:nmc + s + 1],
                          accum=sfa[:, nmc + 2 + s:nmc + 3 + s],
                          reads=[self.pbR[bk], self.sfRp[2 + W2_]],
                          writes=[pnW.R, self.sfRp[2 + W2_]],
                          tag=f"pe{i}_{w}_{h}_{s}")
                    P.recip(sfa[:, rdc + s:rdc + s + 1],
                            sfa[:, nmc + 2 + s:nmc + 3 + s],
                            reads=[self.sfRp[2 + W2_]],
                            writes=[self.sfRp[2 + W2_]],
                            tag=f"rd{i}_{w}_{h}_{s}")
                    P.tt(pnW.v(0, 128, s * 512, kw),
                         pnW.v(0, 128, s * 512, kw),
                         sfa[:, rdc + s:rdc + s + 1].broadcast_to(
                             (128, kw)),
                         ALU.mult, engine=aeng,
                         reads=[pnW.R, self.sfRp[2 + W2_]],
                         writes=[pnW.R], tag=f"pw{i}_{w}_{h}_{s}")

            def emit_B1(j):
                h, wl = insts[j]
                w = q4 * 2 + wl
                W2_ = j % 2
                pnW, pTW = pn[W2_], pT[W2_]
                nk = 2 if w == 0 else 4
                tbk = self.tb[W2_].ap()
                for s in range(2):
                    for ck in range(nk):
                        P.trans(tbk[:, ck * 256 + s * 128:
                                    ck * 256 + s * 128 + 128],
                                pnW.v(0, 128, s * 512 + ck * 128, 128),
                                c["ident_bf"][:, :],
                                reads=[pnW.R, self.cR],
                                writes=[self.tbR[W2_]],
                                tag=f"pt{i}_{w}_{h}_{s}_{ck}")
                P.act(pTW.v(0, 128, 0, nk * 256),
                      tbk[:, 0:nk * 256], AF.Copy,
                      reads=[self.tbR[W2_]], writes=[pTW.R],
                      tag=f"pv{i}_{w}_{h}")

            def emit_B2(j):
                h, wl = insts[j]
                f = h // 2
                pb_ = (h % 2) * 64
                w = q4 * 2 + wl
                W2_ = j % 2
                pTW = pT[W2_]
                bkv = 4 + W2_
                nk = 2 if w == 0 else 4
                for ck in range(nk):
                    P.mm(self.pb[bkv].ap()[pb_:pb_ + 64, 0:256],
                         vtm.v(0, 128, (wl * 2 + ck + (2 if w == 0 else 0))
                               * 1024 + h * 64, 64),
                         pTW.v(0, 128, ck * 256, 256),
                         ck == 0, ck == nk - 1,
                         reads=[vtm.R, pTW.R], writes=[self.pbR[bkv]],
                         tag=f"av{i}_{w}_{h}_{ck}")
                P.act(atout.v(pb_, 64, f * 512 + wl * 256, 256),
                      self.pb[bkv].ap()[pb_:pb_ + 64, 0:256],
                      AF.Copy, reads=[self.pbR[bkv]], writes=[atout.R],
                      tag=f"ao{i}_{w}_{h}")

            emit_A(0)
            emit_A(1)
            for j in range(len(insts)):
                emit_B1(j)
                if j + 2 < len(insts):
                    emit_A(j + 2)
                emit_B2(j)
            for reg in pn + pT:
                reg.free()
            wstg = self.ar.alloc(NG * 128, f"wos{i}_{q4}")
            wob = self.ab.alloc(NG * 128, f"wobb{i}_{q4}")
            for mo in range(NG):
                P.dma("sync",
                      wstg.v(0, 128, 0, 1024).rearrange(
                          "p (g m) -> p g m", m=128),
                      hs["wo"][i][:, mo * 128:(mo + 1) * 128].rearrange(
                          "(g p) m -> p g m", p=128).bitcast(F32R),
                      writes=[wstg.R], tag=f"wo{i}_{q4}_{mo}")
                P.copy(wob.v(), wstg.v().bitcast(F32), reads=[wstg.R],
                       writes=[wob.R], tag=f"woc{i}_{q4}_{mo}")
                ps, psR = self.pb[0].ap()[:, :], self.pbR[0]
                for k in range(NG):
                    P.mm(ps, wob.v(0, 128, k * 128, 128),
                         atout.v(0, 128, k * 512, 512), k == 0, k == NG - 1,
                         reads=[wob.R, atout.R], writes=[psR],
                         tag=f"woy{i}_{q4}_{mo}_{k}")
                P.tt(self.xi_data(mo, t0, 512), ps,
                     self.xi_data(mo, t0, 512).bitcast(F32), ALU.add,
                     reads=[psR, self.xiR[mo]], writes=[self.xiR[mo]],
                     tag=f"wor{i}_{q4}_{mo}")
            for reg in (atout, wstg, wob):
                reg.free()
        for reg in [vtm, qfm, kfm] + rA + rB + list(wst):
            reg.free()

    def _l0f_ffn(self, i, hs, pl):
        P, c = self.P, self.c
        NM1 = 22
        w1s = [self.ar.alloc(NG * 128, f"w1s{i}_{j}") for j in range(4)]

        def w1_load(m):
            wdt = 128 if m < NM1 - 1 else DI - 128 * (NM1 - 1)
            sA, sG = w1s[2 * (m % 2)], w1s[2 * (m % 2) + 1]
            for (st_, cofs) in ((sA, m * 128), (sG, DI + m * 128)):
                P.dma("sync",
                      st_.v(0, 128, 0, NG * 128).rearrange(
                          "p (g f) -> p g f", f=128)[:, :, 0:wdt],
                      hs["w1"][i][:, cofs:cofs + wdt].rearrange(
                          "(g p) f -> p g f", p=128).bitcast(F32R),
                      writes=[st_.R], tag=f"w1l{i}_{m}")
        w1_load(0)
        w1_load(1)
        w2s = self.ar.alloc(2816, f"w2s{i}")
        w2bb = self.ab.alloc(2816, f"w2bb{i}")

        def w2_stage(mo):
            P.dma("gpsimd",
                  w2s.v(0, 128, 0, 2816).rearrange("p (g f) -> p g f", f=128),
                  hs["w2"][i][:, mo * 128:(mo + 1) * 128].rearrange(
                      "(g p) f -> p g f", p=128).bitcast(F32R),
                  writes=[w2s.R], tag=f"w2l{i}_{mo}")
            P.copy(w2bb.v(), w2s.v().bitcast(F32),
                   reads=[w2s.R], writes=[w2bb.R], engine="gpsimd",
                   tag=f"w2c{i}_{mo}")
            P.dma("gpsimd", self._w2st[mo], w2bb.v(),
                  reads=[w2bb.R], writes=[self._w2stR], tag=f"w2o{i}_{mo}")
        self.xnb = self.ab.alloc(NG * SEQ, f"xnb{i}")
        self.xnbR = self.xnb.subs(f"xnb{i}_", NG)
        tf2 = [self.ab.alloc(512, f"ft2{i}_{j}", dt=F32) for j in range(2)]
        def srcv(g, c0, n):
            return (self.xi_data(g, c0, n) if g < 4
                    else self.xi_shift(g, c0, n))
        def cb(g, ch, tile, tileR, t1, t1R, b_b, bbR):
            t = tf2[g % 2]
            P.tt(t.v(), t1, b_b, ALU.add, reads=[t1R, bbR],
                 writes=[t.R], tag=f"fln2{i}_{ch}_{g}")
            P.ts(self.xnb_view(g, ch * 512, 512), t.v(),
                 pl.v(0, 128, g, 1), pl.v(0, 128, 8 + g, 1),
                 op0=ALU.mult, op1=ALU.add,
                 reads=[t.R, pl.R], writes=[self.xnbR[g]],
                 tag=f"flnx{i}_{ch}_{g}")
        self.emit_ln(self.ResidentSrc(srcv, self.xiR), cb, f"fln{i}")
        for t in tf2:
            t.free()
        w1b = [self.ab.alloc(NG * 128, f"w1b{i}_{j}") for j in range(4)]
        gsc = self.ab.alloc(512, f"gsc{i}", dt=F32)
        glt = [self.ab.alloc(512, f"glt{i}_{j}") for j in range(2)]
        for m in range(NM1):
            wdt = 128 if m < NM1 - 1 else DI - 128 * (NM1 - 1)
            sA, sG = w1s[2 * (m % 2)], w1s[2 * (m % 2) + 1]
            bA, bG = w1b[2 * (m % 2)], w1b[2 * (m % 2) + 1]
            P.copy(bA.v(), sA.v().bitcast(F32), reads=[sA.R], writes=[bA.R],
                   engine="gpsimd", tag=f"w1cA{i}_{m}")
            P.copy(bG.v(), sG.v().bitcast(F32), reads=[sG.R], writes=[bG.R],
                   engine="gpsimd", tag=f"w1cG{i}_{m}")
            if m + 2 < NM1:
                w1_load(m + 2)
            if m % 2 == 1 and 3 <= m < 19:
                w2_stage((m - 3) // 2)
            for ch in range(NCH):
                psA, psAR = self.pb[ch % 2].ap(), self.pbR[ch % 2]
                psG, psGR = self.pb[2 + ch % 2].ap(), self.pbR[2 + ch % 2]
                for k in range(NG):
                    P.mm(psA[0:wdt, :], bA.v(0, 128, k * 128, wdt),
                         self.xnb_view(k, ch * 512, 512), k == 0, k == NG - 1,
                         reads=[bA.R, self.xnbR[k]], writes=[psAR],
                         tag=f"w1A{i}_{m}_{ch}_{k}")
                for k in range(NG):
                    P.mm(psG[0:wdt, :], bG.v(0, 128, k * 128, wdt),
                         self.xnb_view(k, ch * 512, 512), k == 0, k == NG - 1,
                         reads=[bG.R, self.xnbR[k]], writes=[psGR],
                         tag=f"w1G{i}_{m}_{ch}_{k}")
                P.act(gsc.v(0, wdt), psG[0:wdt, :], AF.Gelu,
                      bias=pl.v(0, wdt, 38 + m, 1),
                      reads=[psGR, pl.R], writes=[gsc.R],
                      tag=f"gel{i}_{m}_{ch}")
                g2 = glt[ch % 2]
                P.stt(g2.v(0, wdt), psA[0:wdt, :], pl.v(0, wdt, 16 + m, 1),
                      gsc.v(0, wdt), ALU.add, ALU.mult,
                      reads=[psAR, gsc.R, pl.R], writes=[g2.R],
                      tag=f"glu{i}_{m}_{ch}")
                P.dma("gpsimd",
                      self._glu3[0:wdt, m, ch * 512:(ch + 1) * 512],
                      g2.v(0, wdt), reads=[g2.R], writes=[self._gluR],
                      tag=f"gls{i}_{m}_{ch}")
        for reg in w1s + w1b + [gsc] + glt + [w2s, w2bb]:
            reg.free()
        self.xnb.free()
        w2f = [self.ab.alloc(2816, f"w2f{i}_{j}") for j in range(2)]
        gch = [self.ab.alloc(NM1 * 512, f"gch{i}_{j}") for j in range(2)]

        def gch_load(ch):
            g = gch[ch % 2]
            P.dma("gpsimd",
                  g.v(0, 128, 0, NM1 * 512).rearrange(
                      "p (g t) -> p g t", t=512),
                  self._glu3[:, 0:NM1, ch * 512:(ch + 1) * 512],
                  reads=[self._gluR], writes=[g.R], tag=f"w2g{i}_{ch}")

        def w2f_load(ch, mo):
            f = w2f[mo % 2]
            P.dma("scalar", f.v(), self._w2st[mo],
                  reads=[self._w2stR], writes=[f.R], tag=f"w2fl{i}_{ch}_{mo}")
        gch_load(0)
        gch_load(1)
        w2f_load(0, 0)
        for ch in range(NCH):
            g = gch[ch % 2]
            for mo in range(NG):
                f = w2f[mo % 2]
                if mo + 1 < NG:
                    w2f_load(ch, mo + 1)
                elif ch + 1 < NCH:
                    w2f_load(ch + 1, 0)
                ps, psR = self.pb[4 + mo % 2].ap(), self.pbR[4 + mo % 2]
                for k in range(NM1):
                    kw = 128 if k < NM1 - 1 else DI - 128 * (NM1 - 1)
                    P.mm(ps[:, :], f.v(0, kw, k * 128, 128),
                         g.v(0, kw, k * 512, 512), k == 0, k == NM1 - 1,
                         reads=[f.R, g.R], writes=[psR],
                         tag=f"w2m{i}_{ch}_{mo}_{k}")
                P.stt(self.xi_data(mo, ch * 512, 512), ps[:, :],
                      pl.v(0, 128, 60 + mo, 1),
                      self.xi_data(mo, ch * 512, 512).bitcast(F32),
                      ALU.add, ALU.add,
                      reads=[psR, pl.R, self.xiR[mo]], writes=[self.xiR[mo]],
                      tag=f"w2r{i}_{ch}_{mo}")
            if ch + 2 < NCH:
                gch_load(ch + 2)
        for reg in w2f + gch:
            reg.free()

    def phase_l0attn(self, wqkvT_h, woutT_h, attnC_h):
        P, c = self.P, self.c
        NQ = 4
        aC3 = attnC_h.rearrange("(g p) t -> p g t", p=128)
        aCR = R("attnC")
        Sp = [self.ab.alloc(8 * 64, f"l0S{i}", dt=F32) for i in range(2)]
        Zp = [self.ab.alloc(8, f"l0Z{i}", dt=F32) for i in range(2)]
        Sbf = self.ab.alloc(8 * 64, "l0Sbf")
        Zbf = self.ab.alloc(8, "l0Zbf")
        wst = [self.ar.alloc(NG * 256, f"l0wst{i}") for i in range(2)]
        sc_f = [self.ab.alloc(512, f"l0el{i}", dt=F32) for i in range(2)]
        for q4 in range(NQ):
            t0 = q4 * 512
            qtm = self.ab.alloc(4 * 1024, "l0qtm")
            ktm = self.ab.alloc(4 * 1024, "l0ktm")
            vtm = self.ab.alloc(4 * 1024, "l0vtm")
            for wc in range(12):
                w = wst[wc % 2]
                P.dma("sync",
                      w.v(0, 128, 0, NG * 256).rearrange(
                          "p (g f) -> p g f", f=256),
                      wqkvT_h[:, wc * 256:(wc + 1) * 256].rearrange(
                          "(g p) f -> p g f", p=128).bitcast(F32R),
                      writes=[w.R], tag=f"l0w{q4}_{wc}")
                for tt in range(4):
                    ps0 = self.pb[tt % 2].ap()[:, (wc % 2) * 256:
                                               (wc % 2) * 256 + 256]
                    psR = self.pbR[tt % 2]
                    for k in range(NG):
                        P.mm(ps0, self.xi_data(k, t0 + tt * 128, 128),
                             w.v(0, 128, k * 256, 256), k == 0, k == NG - 1,
                             reads=[self.xiR[k], w.R], writes=[psR],
                             tag=f"l0qkv{q4}_{wc}_{tt}_{k}")
                    dst = (qtm, ktm, vtm)[wc // 4]
                    dv = dst.v(0, 128, tt * 1024 + (wc % 4) * 256, 256)
                    if wc < 8:
                        s1, s2 = sc_f[0], sc_f[1]
                        P.ts(s1.v(0, 128, 0, 256), ps0, 0.0, None,
                             op0=ALU.min,
                             reads=[psR], writes=[s1.R],
                             tag=f"l0e1{q4}_{wc}_{tt}")
                        P.act(s2.v(0, 128, 0, 256), s1.v(0, 128, 0, 256),
                              AF.Exp, reads=[s1.R],
                              writes=[s2.R], tag=f"l0e2{q4}_{wc}_{tt}")
                        P.stt(dv, ps0, 0.0, s2.v(0, 128, 0, 256),
                              ALU.max, ALU.add,
                              reads=[psR, s2.R], writes=[dst.R],
                              tag=f"l0e3{q4}_{wc}_{tt}")
                    else:
                        P.act(dv, ps0, AF.Copy, reads=[psR], writes=[dst.R],
                              tag=f"l0ev{q4}_{wc}_{tt}")
            qfm = self.ab.alloc(NG * 512, "l0qfm")
            kfm = self.ab.alloc(NG * 512, "l0kfm")
            tbank = self.tb[0].ap()
            for src, dst in ((qtm, qfm), (ktm, kfm)):
                for f in range(NG):
                    for tt in range(4):
                        tb_ap = tbank[:, (tt % 4) * 128:(tt % 4) * 128 + 128]
                        P.trans(tb_ap,
                                src.v(0, 128, tt * 1024 + f * 128, 128),
                                c["ident_bf"][:, :],
                                reads=[src.R, self.cR], writes=[self.tbR[0]],
                                tag=f"l0T{q4}_{f}_{tt}")
                        P.copy(dst.v(0, 128, f * 512 + tt * 128, 128), tb_ap,
                               reads=[self.tbR[0]], writes=[dst.R],
                               tag=f"l0Te{q4}_{f}_{tt}")
            qtm.free()
            atn = self.ab.alloc(NG * 512, "l0atn")
            asc = [self.ab.alloc(512, f"l0asc{j}") for j in range(2)]
            atT = [self.ab.alloc(512, f"l0atT{j}") for j in range(2)]
            sfa0 = self.sf.ap()
            rdi = [sfa0[0:1, 0:256], sfa0[0:1, 512:768]]
            rdo = [sfa0[0:1, 256:512], sfa0[0:1, 768:1024]]
            rb = [self.ab.alloc(256, f"l0rb{j}", dt=F32) for j in range(2)]
            for h in range(HEADS):
                f = h // 2
                pb_ = (h % 2) * 64
                scol = (h // 2) * 64
                for cc in range(2):
                    gc = q4 * 2 + cc
                    W2_ = (h * 2 + cc) % 2
                    BSC, BND, BRS = W2_, 2 + W2_, 4 + W2_
                    ascW, atTW, rbW = asc[W2_], atT[W2_], rb[W2_]
                    rdiW, rdoW = rdi[W2_], rdo[W2_]
                    qv = lambda s_, n_: qfm.v(pb_, 64,
                                              f * 512 + cc * 256 + s_, n_)
                    kv = kfm.v(pb_, 64, f * 512 + cc * 256, 256)
                    for s in range(2):
                        P.mm(self.pb[BSC].ap()[:, s * 256:(s + 1) * 256],
                             qv(s * 128, 128), kv, True, True,
                             reads=[qfm.R, kfm.R], writes=[self.pbR[BSC]],
                             tag=f"l0sc{q4}_{h}_{cc}_{s}")
                    for s in range(2):
                        P.tt(ascW.v(0, 128, s * 256, 256),
                             self.pb[BSC].ap()[:, s * 256:(s + 1) * 256],
                             c[f"cmask_s{s}"][:, :], ALU.mult,
                             reads=[self.pbR[BSC], self.cR], writes=[ascW.R],
                             tag=f"l0ms{q4}_{h}_{cc}_{s}")
                    tbankp = self.tb[W2_].ap()
                    for s in range(2):
                        for ks in range(2):
                            tb_ap = tbankp[:, 512 + ks * 256 + s * 128:
                                           512 + ks * 256 + s * 128 + 128]
                            P.trans(tb_ap,
                                    ascW.v(0, 128, s * 256 + ks * 128, 128),
                                    c["ident_bf"][:, :],
                                    reads=[ascW.R, self.cR],
                                    writes=[self.tbR[W2_]],
                                    tag=f"l0aT{q4}_{h}_{cc}_{s}{ks}")
                            P.copy(atTW.v(0, 128, ks * 256 + s * 128, 128),
                                   tb_ap, reads=[self.tbR[W2_]],
                                   writes=[atTW.R],
                                   tag=f"l0aTe{q4}_{h}_{cc}_{s}{ks}")
                    nmm = 3 if gc > 0 else 2
                    for ks in range(2):
                        P.mm(self.pb[BND].ap()[pb_:pb_ + 64, 0:256],
                             vtm.v(0, 128, (2 * cc + ks) * 1024 + h * 64, 64),
                             atTW.v(0, 128, ks * 256, 256),
                             ks == 0, ks == nmm - 1 and gc == 0,
                             reads=[vtm.R, atTW.R], writes=[self.pbR[BND]],
                             tag=f"l0nm{q4}_{h}_{cc}_{ks}")
                    if gc > 0:
                        P.mm(self.pb[BND].ap()[pb_:pb_ + 64, 0:256],
                             Sbf.v(pb_, 64, scol, 64),
                             qv(0, 256), False, True,
                             reads=[Sbf.R, qfm.R], writes=[self.pbR[BND]],
                             tag=f"l0nS{q4}_{h}_{cc}")
                    for ks in range(2):
                        P.mm(self.pb[BND].ap()[0:1, 256:512],
                             c["ones_bf_col"][:, :],
                             atTW.v(0, 128, ks * 256, 256),
                             ks == 0, ks == nmm - 1 and gc == 0,
                             reads=[atTW.R, self.cR], writes=[self.pbR[BND]],
                             tag=f"l0dn{q4}_{h}_{cc}_{ks}")
                    if gc > 0:
                        P.mm(self.pb[BND].ap()[0:1, 256:512],
                             Zbf.v(pb_, 64, h // 2, 1),
                             qv(0, 256), False, True,
                             reads=[Zbf.R, qfm.R], writes=[self.pbR[BND]],
                             tag=f"l0dZ{q4}_{h}_{cc}")
                    P.ts(rdiW, self.pb[BND].ap()[0:1, 256:512], 1e-6, None,
                         op0=ALU.max, reads=[self.pbR[BND]],
                         writes=[self.sfRp[W2_]],
                         tag=f"l0rm{q4}_{h}_{cc}")
                    P.recip(rdoW, rdiW, reads=[self.sfRp[W2_]],
                            writes=[self.sfRp[W2_]],
                            tag=f"l0rr{q4}_{h}_{cc}")
                    P.mm(self.pb[BRS].ap()[pb_:pb_ + 64, 0:256],
                         c["ones_f32_row64"][:, :], rdoW, True, True,
                         reads=[self.sfRp[W2_], self.cR],
                         writes=[self.pbR[BRS]],
                         tag=f"l0rb{q4}_{h}_{cc}")
                    P.act(rbW.v(pb_, 64),
                          self.pb[BRS].ap()[pb_:pb_ + 64, 0:256],
                          AF.Copy,
                          reads=[self.pbR[BRS]], writes=[rbW.R],
                          tag=f"l0rbe{q4}_{h}_{cc}")
                    P.tt(atn.v(pb_, 64, f * 512 + cc * 256, 256),
                         self.pb[BND].ap()[pb_:pb_ + 64, 0:256],
                         rbW.v(pb_, 64), ALU.mult,
                         reads=[self.pbR[BND], rbW.R], writes=[atn.R],
                         tag=f"l0o{q4}_{h}_{cc}")
                    for ks in range(2):
                        P.mm(self.pb[BRS].ap()[pb_:pb_ + 64, 256:320],
                             ktm.v(0, 128, (2 * cc + ks) * 1024 + h * 64, 64),
                             vtm.v(0, 128, (2 * cc + ks) * 1024 + h * 64, 64),
                             ks == 0, ks == 1,
                             reads=[ktm.R, vtm.R], writes=[self.pbR[BRS]],
                             tag=f"l0Su{q4}_{h}_{cc}_{ks}")
                    for ks in range(2):
                        P.mm(self.pb[BRS].ap()[pb_:pb_ + 64, 320:321],
                             ktm.v(0, 128, (2 * cc + ks) * 1024 + h * 64, 64),
                             c["ones_bf_col"][:, :], ks == 0, ks == 1,
                             reads=[ktm.R, self.cR], writes=[self.pbR[BRS]],
                             tag=f"l0Zu{q4}_{h}_{cc}_{ks}")
                    Sr, Sw = Sp[gc % 2], Sp[1 - gc % 2]
                    Zr, Zw = Zp[gc % 2], Zp[1 - gc % 2]
                    if gc == 0:
                        P.copy(Sw.v(pb_, 64, scol, 64),
                               self.pb[BRS].ap()[pb_:pb_ + 64, 256:320],
                               reads=[self.pbR[BRS]], writes=[Sw.R],
                               tag=f"l0Sc{q4}_{h}_{cc}")
                        P.copy(Zw.v(pb_, 64, h // 2, 1),
                               self.pb[BRS].ap()[pb_:pb_ + 64, 320:321],
                               reads=[self.pbR[BRS]], writes=[Zw.R],
                               tag=f"l0Zc{q4}_{h}_{cc}")
                    else:
                        P.tt(Sw.v(pb_, 64, scol, 64),
                             self.pb[BRS].ap()[pb_:pb_ + 64, 256:320],
                             Sr.v(pb_, 64, scol, 64), ALU.add,
                             reads=[self.pbR[BRS], Sr.R], writes=[Sw.R],
                             tag=f"l0Sa{q4}_{h}_{cc}")
                        P.tt(Zw.v(pb_, 64, h // 2, 1),
                             self.pb[BRS].ap()[pb_:pb_ + 64, 320:321],
                             Zr.v(pb_, 64, h // 2, 1), ALU.add,
                             reads=[self.pbR[BRS], Zr.R], writes=[Zw.R],
                             tag=f"l0Za{q4}_{h}_{cc}")
                    P.act(Sbf.v(pb_, 64, scol, 64), Sw.v(pb_, 64, scol, 64),
                          AF.Copy, scale=0.125, reads=[Sw.R], writes=[Sbf.R],
                          tag=f"l0Sb{q4}_{h}_{cc}")
                    P.act(Zbf.v(pb_, 64, h // 2, 1), Zw.v(pb_, 64, h // 2, 1),
                          AF.Copy, scale=0.125, reads=[Zw.R], writes=[Zbf.R],
                          tag=f"l0Zb{q4}_{h}_{cc}")
            wstg = self.ar.alloc(NG * 128, "l0wostg")
            wob = self.ab.alloc(NG * 128, "l0wob")
            wsc = self.ab.alloc(512, "l0wsc", dt=F32)
            for mo in range(NG):
                P.dma("sync",
                      wstg.v(0, 128, 0, 1024).rearrange(
                          "p (g m) -> p g m", m=128),
                      woutT_h[:, mo * 128:(mo + 1) * 128].rearrange(
                          "(g p) m -> p g m", p=128).bitcast(F32R),
                      writes=[wstg.R], tag=f"l0wo{q4}_{mo}")
                P.copy(wob.v(), wstg.v().bitcast(F32), reads=[wstg.R],
                       writes=[wob.R], tag=f"l0woc{q4}_{mo}")
                ps, psR = self.pb[0].ap()[:, :], self.pbR[0]
                for k in range(NG):
                    P.mm(ps, wob.v(0, 128, k * 128, 128),
                         atn.v(0, 128, k * 512, 512), k == 0, k == NG - 1,
                         reads=[wob.R, atn.R], writes=[psR],
                         tag=f"l0y{q4}_{mo}_{k}")
                P.act(wsc.v(), ps, AF.Copy, reads=[psR], writes=[wsc.R],
                      tag=f"l0ye{q4}_{mo}")
                P.dma("sync", aC3[:, mo, t0:t0 + 512], wsc.v(),
                      reads=[wsc.R], writes=[aCR], tag=f"l0ys{q4}_{mo}")
            for reg in [ktm, vtm, qfm, kfm, atn, wstg, wob, wsc] + asc + \
                    atT + rb:
                reg.free()
        for reg in Sp + Zp + [Sbf, Zbf] + wst + sc_f:
            reg.free()
        return aCR

    def phase_l1attn(self, ap_arr, bwT_h, cwT_h):
        P, c = self.P, self.c
        NJ = 4
        stg = self.ar.alloc(NG * 512, "l1bwstg")
        bwb = self.ab.alloc(NG * 512, "l1bwb")
        P.dma("sync", stg.v(0, 128, 0, NG * 512).rearrange(
                  "p (g n) -> p g n", n=512),
              bwT_h.rearrange("(g p) n -> p g n", p=128).bitcast(F32R),
              writes=[stg.R], tag="bwld")
        P.copy(bwb.v(), stg.v().bitcast(F32), reads=[stg.R], writes=[bwb.R],
               tag="bwcv")
        stg.free()
        arow, arowR = self.small("l1_ap", ap_arr, F32)
        abc = self.ab.alloc(256, "l1abc", dt=F32)
        wsc = self.ab.alloc(256, "l1wsc", dt=F32)
        wbuf = self.ab.alloc(32, "l1wbuf", dt=F32)
        psA, psAR = self.pb[0].ap()[:, 0:256], self.pbR[0]
        P.mm(psA, c["ones_f32_row"][:, :], arow[:, :], True, True,
             reads=[arowR, self.cR], writes=[psAR], tag="l1abc")
        P.act(abc.v(), psA, AF.Copy, reads=[psAR], writes=[abc.R], tag="l1abce")
        for jt in range(NJ):
            P.tt(wsc.v(), c["l1_M"][:, jt * 256:(jt + 1) * 256], abc.v(),
                 ALU.mult, reads=[abc.R, self.cR], writes=[wsc.R],
                 tag=f"l1w{jt}")
            P.red(wbuf.v(0, 128, jt, 1), wsc.v(), ALU.add,
                  reads=[wsc.R], writes=[wbuf.R], tag=f"l1wr{jt}")
        DBG = getattr(self, "dbg_l1a", False)
        sfv = lambda c0, n=4: self.sf.ap()[:, c0:c0 + n]
        sfR = self.sfR
        W, T3, RP, RR, OMN, CTH, STH = 0, 4, 8, 12, 16, 20, 24
        CT, ST, A1, B1, C5, S5 = 28, 29, 30, 31, 32, 33
        CARP, CARQ = 34, 38
        P.copy(sfv(W), wbuf.v(0, 128, 0, 4), reads=[wbuf.R], writes=[sfR],
               tag="l1wcp")
        P.tt(sfv(T3), sfv(W), sfv(W), ALU.mult, reads=[sfR], writes=[sfR],
             tag="l1w2")
        P.ts(sfv(RP), sfv(T3), 1.0, None, op0=ALU.add, reads=[sfR],
             writes=[sfR], tag="l1w3")
        P.recip(sfv(RR), sfv(RP), reads=[sfR], writes=[sfR], tag="l1w4")
        P.ts(sfv(OMN), sfv(T3), -1.0, 1.0, op0=ALU.mult, op1=ALU.add,
             reads=[sfR], writes=[sfR], tag="l1w5")
        P.tt(sfv(CTH), sfv(OMN), sfv(RR), ALU.mult, reads=[sfR], writes=[sfR],
             tag="l1w6")
        P.stt(sfv(STH), sfv(W), -2.0, sfv(RR), ALU.mult, ALU.mult,
              reads=[sfR], writes=[sfR], tag="l1w7")
        if DBG:
            self.debug_out("dbg_wbuf", sfv(0, 48), [sfR])
        h_st = self.dint("l1h_st", (512, SEQ), BF16)
        h3 = h_st.rearrange("(g p) t -> p g t", p=128)
        hstR = R("l1h_st")
        tf = self.tabf.ap()
        tabs = [0, 1, 2, 3]
        tabRs = [R(f"tabf{i}") for i in range(4)]
        scrR = [R("tscr0"), R("tscr1")]
        tv = lambda b, c0, n: tf[:, b * 512 + c0:b * 512 + c0 + n]
        sc1 = lambda c0, n: tf[:, 2048 + c0:2048 + c0 + n]
        sc2 = lambda c0, n: tf[:, 2304 + c0:2304 + c0 + n]
        Pf = self.ab.alloc(512, "l1Pf", dt=F32)
        Qf = self.ab.alloc(512, "l1Qf", dt=F32)
        cumP = self.ab.alloc(512, "l1cumP", dt=F32)
        cumQ = self.ab.alloc(512, "l1cumQ", dt=F32)
        hstg = [self.ab.alloc(512, f"l1hs{i}") for i in range(2)]
        for jt in range(NJ):
            cb_, sb_ = tabs[0], tabs[1]
            cR_, sR_ = tabRs[cb_], tabRs[sb_]
            P.ts(tv(cb_, 0, 1), sfv(CTH, 1), 0.0, 1.0, op0=ALU.mult,
                 op1=ALU.add, reads=[sfR], writes=[cR_], tag=f"l1tc0_{jt}")
            P.ts(tv(sb_, 0, 1), sfv(CTH, 1), 0.0, None, op0=ALU.mult,
                 reads=[sfR], writes=[sR_], tag=f"l1ts0_{jt}")
            P.copy(tv(cb_, 1, 1), sfv(CTH + jt, 1), reads=[sfR], writes=[cR_],
                   tag=f"l1tc1_{jt}")
            P.copy(tv(sb_, 1, 1), sfv(STH + jt, 1), reads=[sfR], writes=[sR_],
                   tag=f"l1ts1_{jt}")
            T = 1
            while T < 512:
                if T == 1:
                    P.copy(sfv(CT, 1), tv(cb_, 1, 1), reads=[cR_],
                           writes=[sfR], tag=f"l1cT{jt}_{T}")
                    P.copy(sfv(ST, 1), tv(sb_, 1, 1), reads=[sR_],
                           writes=[sfR], tag=f"l1sT{jt}_{T}")
                else:
                    P.tt(sfv(A1, 1), tv(cb_, T - 1, 1), tv(cb_, 1, 1),
                         ALU.mult, reads=[cR_], writes=[sfR],
                         tag=f"l1cTa{jt}_{T}")
                    P.tt(sfv(B1, 1), tv(sb_, T - 1, 1), tv(sb_, 1, 1),
                         ALU.mult, reads=[sR_], writes=[sfR],
                         tag=f"l1cTb{jt}_{T}")
                    P.tt(sfv(CT, 1), sfv(A1, 1), sfv(B1, 1), ALU.subtract,
                         reads=[sfR], writes=[sfR], tag=f"l1cTc{jt}_{T}")
                    P.tt(sfv(A1, 1), tv(sb_, T - 1, 1), tv(cb_, 1, 1),
                         ALU.mult, reads=[cR_, sR_], writes=[sfR],
                         tag=f"l1sTa{jt}_{T}")
                    P.tt(sfv(B1, 1), tv(cb_, T - 1, 1), tv(sb_, 1, 1),
                         ALU.mult, reads=[cR_, sR_], writes=[sfR],
                         tag=f"l1sTb{jt}_{T}")
                    P.tt(sfv(ST, 1), sfv(A1, 1), sfv(B1, 1), ALU.add,
                         reads=[sfR], writes=[sfR], tag=f"l1sTc{jt}_{T}")
                P.ts(sc1(0, T), tv(cb_, 0, T), sfv(CT, 1), None, op0=ALU.mult,
                     reads=[cR_, sfR], writes=[scrR[0]], tag=f"l1e1_{jt}_{T}")
                P.ts(sc2(0, T), tv(sb_, 0, T), sfv(ST, 1), None, op0=ALU.mult,
                     reads=[sR_, sfR], writes=[scrR[1]], tag=f"l1e2_{jt}_{T}")
                P.tt(tv(cb_, T, T), sc1(0, T), sc2(0, T), ALU.subtract,
                     reads=[scrR[0], scrR[1]], writes=[cR_],
                     tag=f"l1e3_{jt}_{T}")
                P.ts(sc1(0, T), tv(sb_, 0, T), sfv(CT, 1), None, op0=ALU.mult,
                     reads=[sR_, sfR], writes=[scrR[0]], tag=f"l1e4_{jt}_{T}")
                P.ts(sc2(0, T), tv(cb_, 0, T), sfv(ST, 1), None, op0=ALU.mult,
                     reads=[cR_, sfR], writes=[scrR[1]], tag=f"l1e5_{jt}_{T}")
                P.tt(tv(sb_, T, T), sc1(0, T), sc2(0, T), ALU.add,
                     reads=[scrR[0], scrR[1]], writes=[sR_],
                     tag=f"l1e6_{jt}_{T}")
                T *= 2
            P.tt(sfv(A1, 1), tv(cb_, 511, 1), tv(cb_, 1, 1), ALU.mult,
                 reads=[cR_], writes=[sfR], tag=f"l1r1_{jt}")
            P.tt(sfv(B1, 1), tv(sb_, 511, 1), tv(sb_, 1, 1), ALU.mult,
                 reads=[sR_], writes=[sfR], tag=f"l1r2_{jt}")
            P.tt(sfv(C5, 1), sfv(A1, 1), sfv(B1, 1), ALU.subtract,
                 reads=[sfR], writes=[sfR], tag=f"l1r3_{jt}")
            P.tt(sfv(A1, 1), tv(sb_, 511, 1), tv(cb_, 1, 1), ALU.mult,
                 reads=[cR_, sR_], writes=[sfR], tag=f"l1r4_{jt}")
            P.tt(sfv(B1, 1), tv(cb_, 511, 1), tv(sb_, 1, 1), ALU.mult,
                 reads=[cR_, sR_], writes=[sfR], tag=f"l1r5_{jt}")
            P.tt(sfv(S5, 1), sfv(A1, 1), sfv(B1, 1), ALU.add,
                 reads=[sfR], writes=[sfR], tag=f"l1r6_{jt}")
            if DBG and jt == 0:
                self.debug_out("dbg_cos0", tv(cb_, 0, 512), [cR_])
                self.debug_out("dbg_sin0", tv(sb_, 0, 512), [sR_])
            for ch in range(NCH):
                cb_, sb_ = tabs[0], tabs[1]
                cn_, sn_ = tabs[2], tabs[3]
                cR_, sR_ = tabRs[cb_], tabRs[sb_]
                cnR_, snR_ = tabRs[cn_], tabRs[sn_]
                ps, psR = self.pb[2 + ch % 2].ap()[:, :], self.pbR[2 + ch % 2]
                for k in range(NG):
                    P.mm(ps, bwb.v(0, 128, k * 512 + jt * 128, 128),
                         self.xnb_view(k, ch * 512, 512), k == 0, k == NG - 1,
                         reads=[bwb.R, self.xnbR[k]], writes=[psR],
                         tag=f"l1bu{jt}_{ch}_{k}")
                P.tt(Pf.v(), ps, tv(cb_, 0, 512), ALU.mult,
                     reads=[psR, cR_], writes=[Pf.R], tag=f"l1P{jt}_{ch}")
                P.tt(Qf.v(), ps, tv(sb_, 0, 512), ALU.mult,
                     reads=[psR, sR_], writes=[Qf.R], tag=f"l1Q{jt}_{ch}")
                if DBG and jt == 0 and ch == 0:
                    self.debug_out("dbg_Pf", Pf.v(), [Pf.R])
                initP = 0.0 if ch == 0 else sfv(CARP + jt, 1)
                initQ = 0.0 if ch == 0 else sfv(CARQ + jt, 1)
                P.scan(cumP.v(), c["ones_f32_big"][:, :], Pf.v(), initP,
                       reads=[Pf.R, sfR, self.cR], writes=[cumP.R],
                       tag=f"l1sP{jt}_{ch}")
                P.scan(cumQ.v(), c["ones_f32_big"][:, :], Qf.v(), initQ,
                       reads=[Qf.R, sfR, self.cR], writes=[cumQ.R],
                       tag=f"l1sQ{jt}_{ch}")
                P.copy(sfv(CARP + jt, 1), cumP.v(0, 128, 511, 1),
                       reads=[cumP.R], writes=[sfR], tag=f"l1cP{jt}_{ch}")
                P.copy(sfv(CARQ + jt, 1), cumQ.v(0, 128, 511, 1),
                       reads=[cumQ.R], writes=[sfR], tag=f"l1cQ{jt}_{ch}")
                P.tt(Pf.v(), cumP.v(), tv(cb_, 0, 512), ALU.mult,
                     reads=[cumP.R, cR_], writes=[Pf.R], tag=f"l1h1{jt}_{ch}")
                P.tt(Qf.v(), cumQ.v(), tv(sb_, 0, 512), ALU.mult,
                     reads=[cumQ.R, sR_], writes=[Qf.R], tag=f"l1h2{jt}_{ch}")
                hs = hstg[ch % 2]
                P.tt(hs.v(), Pf.v(), Qf.v(), ALU.add,
                     reads=[Pf.R, Qf.R], writes=[hs.R], tag=f"l1h3{jt}_{ch}")
                P.dma("sync", h3[:, jt, ch * 512:(ch + 1) * 512], hs.v(),
                      reads=[hs.R], writes=[hstR], tag=f"l1hst{jt}_{ch}")
                if ch < NCH - 1:
                    for half in range(2):
                        o = half * 256
                        P.ts(sc1(0, 256), tv(cb_, o, 256), sfv(C5, 1), None,
                             op0=ALU.mult, reads=[cR_, sfR], writes=[scrR[0]],
                             tag=f"l1ra{jt}_{ch}_{half}")
                        P.ts(sc2(0, 256), tv(sb_, o, 256), sfv(S5, 1), None,
                             op0=ALU.mult, reads=[sR_, sfR], writes=[scrR[1]],
                             tag=f"l1rb{jt}_{ch}_{half}")
                        P.tt(tv(cn_, o, 256), sc1(0, 256), sc2(0, 256),
                             ALU.subtract, reads=[scrR[0], scrR[1]],
                             writes=[cnR_], tag=f"l1rc{jt}_{ch}_{half}")
                        P.ts(sc1(0, 256), tv(sb_, o, 256), sfv(C5, 1), None,
                             op0=ALU.mult, reads=[sR_, sfR], writes=[scrR[0]],
                             tag=f"l1rd{jt}_{ch}_{half}")
                        P.ts(sc2(0, 256), tv(cb_, o, 256), sfv(S5, 1), None,
                             op0=ALU.mult, reads=[cR_, sfR], writes=[scrR[1]],
                             tag=f"l1re{jt}_{ch}_{half}")
                        P.tt(tv(sn_, o, 256), sc1(0, 256), sc2(0, 256),
                             ALU.add, reads=[scrR[0], scrR[1]], writes=[snR_],
                             tag=f"l1rf{jt}_{ch}_{half}")
                    tabs[0], tabs[1], tabs[2], tabs[3] = \
                        tabs[2], tabs[3], tabs[0], tabs[1]
        for reg in [Pf, Qf, cumP, cumQ, abc, wsc, wbuf, bwb] + hstg:
            reg.free()
        stg2 = self.ar.alloc(4 * 1024, "l1cwstg")
        cwb = self.ab.alloc(4 * 1024, "l1cwb")
        P.dma("sync", stg2.v(0, 128, 0, 4096).rearrange(
                  "p (g n) -> p g n", n=1024),
              cwT_h.rearrange("(g p) n -> p g n", p=128).bitcast(F32R),
              writes=[stg2.R], tag="cwld")
        P.copy(cwb.v(), stg2.v().bitcast(F32), reads=[stg2.R], writes=[cwb.R],
               tag="cwcv")
        stg2.free()
        hch = [self.ab.alloc(4 * 512, f"l1hch{i}") for i in range(2)]
        for ch in range(NCH):
            hc = hch[ch % 2]
            P.dma("sync", hc.v(0, 128, 0, 2048).rearrange(
                      "p (g t) -> p g t", t=512),
                  h3[:, :, ch * 512:(ch + 1) * 512],
                  reads=[hstR], writes=[hc.R], tag=f"l1hld{ch}")
            for mo in range(NG):
                ps, psR = self.pb[4 + mo % 2].ap()[:, :], self.pbR[4 + mo % 2]
                for k in range(NJ):
                    P.mm(ps, cwb.v(0, 128, k * 1024 + mo * 128, 128),
                         hc.v(0, 128, k * 512, 512), k == 0, k == NJ - 1,
                         reads=[cwb.R, hc.R], writes=[psR],
                         tag=f"l1y{ch}_{mo}_{k}")
                P.tt(self.x_view(mo, ch * 512, 512), ps,
                     self.x_view(mo, ch * 512, 512).bitcast(F32), ALU.add,
                     reads=[psR, self.xR[mo]], writes=[self.xR[mo]],
                     tag=f"l1yev{ch}_{mo}")
        for reg in hch + [cwb]:
            reg.free()

    def phase_l1ffn(self, wgT_h, wuT_h, wdT_h):
        P, c = self.P, self.c
        glu_st = self.dint("l1glu_st", (HID_L1, SEQ), BF16)
        glu3 = glu_st.rearrange("(g p) t -> p g t", p=128)
        wd_bf = self.dint("l1wd_bf", (NG, 128, HID_L1), BF16)
        NM = HID_L1 // 128
        ist = [self.ab.alloc(HID_L1, "wdist0", dt=I32)]
        istR = [ist[0].R]
        wcv = [self.ab.alloc(HID_L1, "wdcv0")]
        wcvR = [wcv[0].R]
        for mo in range(NG):
            P.dma("sync", ist[0].v().rearrange("p (g m) -> p g m", m=128),
                  wdT_h[:, mo * 128:(mo + 1) * 128].rearrange(
                      "(g p) m -> p g m", p=128),
                  writes=[istR[0]], tag=f"wdld{mo}")
            P.copy(wcv[0].v(), ist[0].v(), engine="gpsimd",
                   reads=[istR[0]], writes=[wcvR[0]], tag=f"wdcv{mo}")
            P.dma("sync", wd_bf[mo], wcv[0].v(), reads=[wcvR[0]],
                  tag=f"wdst{mo}")
        for r in ist + wcv:
            r.free()
        xg = [self.ab.alloc(NG * 128, f"l1wg{i}", dt=I32) for i in range(2)]
        xgR = [b.R for b in xg]
        wgb = [self.ab.alloc(NG * 128, f"l1wgb{i}") for i in range(2)]
        wgbR = [b.R for b in wgb]
        xu = [self.ab.alloc(NG * 128, f"l1wu{i}", dt=I32) for i in range(2)]
        xuR = [b.R for b in xu]
        wub = [self.ab.alloc(NG * 128, f"l1wub{i}") for i in range(2)]
        wubR = [b.R for b in wub]
        sg = self.ab.alloc(512, "l1sg")
        sgR = sg.R
        glt = [self.ab.alloc(512, f"l1glt{i}") for i in range(2)]
        gltR = [b.R for b in glt]
        for m in range(NM):
            i2 = m % 2
            P.dma("sync", xg[i2].v().rearrange("p (g m) -> p g m", m=128),
                  wgT_h[:, m * 128:(m + 1) * 128].rearrange(
                      "(g p) m -> p g m", p=128),
                  writes=[xgR[i2]], tag=f"wgld{m}")
            P.copy(wgb[i2].v(), xg[i2].v(), engine="gpsimd",
                   reads=[xgR[i2]], writes=[wgbR[i2]], tag=f"wgcv{m}")
            P.dma("sync", xu[i2].v().rearrange("p (g m) -> p g m", m=128),
                  wuT_h[:, m * 128:(m + 1) * 128].rearrange(
                      "(g p) m -> p g m", p=128),
                  writes=[xuR[i2]], tag=f"wuld{m}")
            P.copy(wub[i2].v(), xu[i2].v(), engine="gpsimd",
                   reads=[xuR[i2]], writes=[wubR[i2]], tag=f"wucv{m}")
            for ch in range(NCH):
                ps0, ps0R = self.pb[ch % 2].ap()[:, :], self.pbR[ch % 2]
                ps1, ps1R = self.pb[2 + ch % 2].ap()[:, :], self.pbR[2 + ch % 2]
                for k in range(NG):
                    P.mm(ps0, wgb[i2].v(0, 128, k * 128, 128),
                         self.xnb_view(k, ch * 512, 512), k == 0, k == NG - 1,
                         reads=[wgbR[i2], self.xnbR[k]], writes=[ps0R],
                         tag=f"l1g{m}_{ch}_{k}")
                for k in range(NG):
                    P.mm(ps1, wub[i2].v(0, 128, k * 128, 128),
                         self.xnb_view(k, ch * 512, 512), k == 0, k == NG - 1,
                         reads=[wubR[i2], self.xnbR[k]], writes=[ps1R],
                         tag=f"l1u{m}_{ch}_{k}")
                P.act(sg.v(), ps0, AF.Silu, scale=INV127,
                      reads=[ps0R], writes=[sgR], tag=f"l1sg{m}_{ch}")
                g2 = glt[ch % 2]
                P.stt(g2.v(), ps1, INV127, sg.v(), ALU.mult, ALU.mult,
                      reads=[ps1R, sgR], writes=[gltR[ch % 2]],
                      tag=f"l1glu{m}_{ch}")
                P.dma("sync", glu3[:, m, ch * 512:(ch + 1) * 512], g2.v(),
                      reads=[gltR[ch % 2]], tag=f"l1gst{m}_{ch}")
        for r in xg + wgb + xu + wub + [sg] + glt:
            r.free()
        self.xnb.free()
        gch = self.ab.alloc(NM * 512, "l1gch")
        gchR = gch.R
        wdb = [self.ab.alloc(HID_L1, f"l1wdb{i}") for i in range(2)]
        wdbR = [b.R for b in wdb]
        for ch in range(NCH):
            P.dma("sync", gch.v(0, 128, 0, NM * 512).rearrange(
                      "p (g t) -> p g t", t=512),
                  glu3[:, :, ch * 512:(ch + 1) * 512],
                  writes=[gchR], tag=f"l1gld{ch}")
            for mo in range(NG):
                i2 = mo % 2
                P.dma("sync", wdb[i2].v(), wd_bf[mo], writes=[wdbR[i2]],
                      tag=f"l1wdb{ch}_{mo}")
                ps, psR = self.pb[4 + mo % 2].ap()[:, :], self.pbR[4 + mo % 2]
                for k in range(NM):
                    P.mm(ps, wdb[i2].v(0, 128, k * 128, 128),
                         gch.v(0, 128, k * 512, 512), k == 0, k == NM - 1,
                         reads=[wdbR[i2], gchR], writes=[psR],
                         tag=f"l1d{ch}_{mo}_{k}")
                P.stt(self.x_view(mo, ch * 512, 512), ps, INV127,
                      self.x_view(mo, ch * 512, 512).bitcast(F32),
                      ALU.mult, ALU.add,
                      reads=[psR, self.xR[mo]], writes=[self.xR[mo]],
                      tag=f"l1dev{ch}_{mo}")
        gch.free()
        for r in wdb:
            r.free()

    def debug_out(self, name, ap, reads, shape=None, dt=F32):
        if shape is None:
            shape = ap.shape
        h = self.dout(name, shape, dt)
        self.P.dma("sync", h, ap if dt != F32 or True else ap, reads=reads,
                   tag="dbg_" + name)

    def small(self, name, arr, dt):
        arr = np.ascontiguousarray(arr)
        self.shared_inputs[name] = arr
        hin = self.din(name, arr.shape, F32 if dt == F32R else
                       ({np.dtype(np.float32): F32,
                         np.dtype(np.int32): I32}[arr.dtype] if dt is None else dt))
        t = self.st.enter_context(
            self.nc.sbuf_tensor("s_" + name, list(arr.shape), dt))
        res = R(name)
        src = hin.bitcast(F32R) if dt == F32R else hin
        self.P.dma("sync", t.ap()[:, :], src, writes=[res], tag="ld_" + name)
        return t.ap(), res

    def store_xi_as(self, dst_ap, tag="store_xi"):
        d = dst_ap.rearrange("(g p) t -> p g t", p=128)
        src = self.xi.v(0, 128, 0, NG * (SEQ + 1)).rearrange(
            "p (g t) -> p g t", t=SEQ + 1)
        return self.P.dma("sync", d, src[:, :, 1:SEQ + 1].bitcast(F32),
                          reads=list(self.xiR), tag=tag)

    def run(self, per_core_inputs, out_names):
        self.P.final_fence()
        self.P.emit(self.st)
        self.st.close()
        in_maps = []
        for cidx in range(N_CORES):
            m = dict(self.shared_inputs)
            m.update(per_core_inputs[cidx])
            in_maps.append(m)
        res = run_bass_kernel_spmd(self.nc, in_maps,
                                   core_ids=list(range(N_CORES)))
        return [{k: r[k] for k in out_names} for r in res.results]



def build_test_l1ffn(g):
    b = Build(["test"]).setup()
    xin = b.din("x_in", (DM, SEQ), F32)
    xout = b.dout("x_out", (DM, SEQ), F32)
    wgT_h = b.win("l1_wgT", g["l1_wgT"])
    wuT_h = b.win("l1_wuT", g["l1_wuT"])
    wdT_h = b.win("l1_wdT", g["l1_wdT"])
    d_ap, dR = b.small("l1_D", g["l1_D"], F32)
    b.alloc_x()
    b.load_x(xin)
    b.phase_ln1(d_ap, dR)
    b.phase_l1ffn(wgT_h, wuT_h, wdT_h)
    b.store_x(xout)
    return b

def _l0ffn_handles(b, g):
    return {
        "wqkv": b.win("wqkv", g["wqkv"]),
        "wo": b.win("wo", g["wo"]),
        "w1": b.win("w1", g["w1"]),
        "w2": b.win("w2", g["w2"]),
        "qs": b.win("qs", g["qs"]),
        "ks": b.win("ks", g["ks"]),
        "lng": b.win("lng", g["ln_g"]),
        "lnb": b.win("lnb", g["ln_b"]),
        "b1a": b.win("b1a", g["b1a"]),
        "b1g": b.win("b1g", g["b1g"]),
        "b2": b.win("b2", g["b2"]),
    }

def build_test_l0ffn(g, nlayers=1):
    b = Build(["l0ffn"]).setup()
    xnin = b.din("xn_in", (DM, SEQ), F32)
    xout = b.dout("x_out", (DM, SEQ), F32)
    hs = _l0ffn_handles(b, g)
    b.alloc_xi()
    s = xnin.rearrange("(g p) t -> p g t", p=128)
    b.P.dma("sync",
            b.xi.v(0, 128, 0, NG * (SEQ + 1)).rearrange(
                "p (g t) -> p g t", t=SEQ + 1)[:, :, 1:SEQ + 1],
            s.bitcast(F32R), writes=list(b.xiR), tag="load_xn")
    glu_st = b.dint("glu_st", (2816, SEQ), BF16)
    b._glu3 = glu_st.rearrange("(g p) t -> p g t", p=128)
    b._gluR = R("glu_st")
    b._w2st = b.dint("w2bf_st", (NG, 128, 2816), BF16)
    b._w2stR = R("w2bf_st")
    for i in range(nlayers):
        pl, qsb, ftab = b._l0f_params(i, hs)
        b._l0f_attn(i, hs, ftab)
        ftab.free()
        qsb.free()
        b._l0f_ffn(i, hs, pl)
        pl.free()
    b.store_xi_as(xout)
    return b

def build_test_l0attn(g):
    b = Build(["l0head"]).setup()
    xnin = b.din("xn_in", (DM, SEQ), F32)
    attnC = b.dout("attnC", (DM, SEQ), F32)
    wq = b.win("l0a_wqkvT", g["l0a_wqkvT"])
    wo = b.win("l0a_woutT", g["l0a_woutT"])
    b.alloc_xi()
    s = xnin.rearrange("(g p) t -> p g t", p=128)
    b.P.dma("sync",
            b.xi.v(0, 128, 0, NG * (SEQ + 1)).rearrange(
                "p (g t) -> p g t", t=SEQ + 1)[:, :, 1:SEQ + 1],
            s.bitcast(F32R), writes=list(b.xiR), tag="load_xn")
    b.phase_l0attn(wq, wo, attnC)
    return b

def build_test_l1attn(g):
    b = Build(["l1attn"]).setup()
    xin = b.din("x_in", (DM, SEQ), F32)
    xout = b.dout("x_out", (DM, SEQ), F32)
    bwT_h = b.win("l1_bwT", g["l1_bwT"])
    cwT_h = b.win("l1_cwT", g["l1_cwT"])
    d_ap, dR = b.small("l1_D", g["l1_D"], F32)
    b.alloc_x()
    b.load_x(xin)
    b.phase_ln1(d_ap, dR)
    b.phase_l1attn(g["l1_ap"], bwT_h, cwT_h)
    b.store_x(xout)
    return b

def build_test_ln():
    b = Build(["test"]).setup()
    xin = b.din("x_in", (DM, SEQ), F32)
    xn_out = b.dout("xn_out", (DM, SEQ), F32)
    b.alloc_xi()
    b.phase_ln0(xin)
    b.store_xi_as(xn_out)
    return b


FUSED = True

def _build_l0head(g):
    b = Build(["l0head"]).setup()
    xin = b.din("x_in", (DM, SEQ), F32)
    xn0o = b.dout("xn0", (DM, SEQ), F32)
    attnCo = b.dout("attnC_o", (DM, SEQ), F32)
    wq = b.win("l0a_wqkvT", g["l0a_wqkvT"])
    wo = b.win("l0a_woutT", g["l0a_woutT"])
    b.alloc_xi()
    b.phase_ln0(xin)
    b.phase_l0attn(wq, wo, attnCo)
    b.store_xi_as(xn0o)
    return b

def _build_l0ffn(g):
    b = Build(["l0ffn"]).setup()
    xnin = b.din("xn0", (DM, SEQ), F32)
    xout = b.dout("xiF", (DM, SEQ), F32)
    hs = _l0ffn_handles(b, g)
    b.alloc_xi()
    s = xnin.rearrange("(g p) t -> p g t", p=128)
    b.P.dma("sync",
            b.xi.v(0, 128, 0, NG * (SEQ + 1)).rearrange(
                "p (g t) -> p g t", t=SEQ + 1)[:, :, 1:SEQ + 1],
            s.bitcast(F32R), writes=list(b.xiR), tag="load_xn")
    b.phase_l0ffn(hs)
    b.store_xi_as(xout)
    return b

def _build_l1(g):
    b = Build(["l1attn"]).setup()
    xin = b.din("x_in", (DM, SEQ), F32)
    aC = b.din("attnC_o", (DM, SEQ), F32)
    xiF = b.din("xiF", (DM, SEQ), F32)
    yout = b.dout("y_out", (DM, SEQ), F32)
    bw = b.win("l1_bwT", g["l1_bwT"])
    cw = b.win("l1_cwT", g["l1_cwT"])
    wg = b.win("l1_wgT", g["l1_wgT"])
    wu = b.win("l1_wuT", g["l1_wuT"])
    wd = b.win("l1_wdT", g["l1_wdT"])
    b.alloc_x()
    b.phase_combine_split(xin, aC, xiF)
    d_ap, dR = b.small("l1_D", g["l1_D"], F32)
    b.phase_ln1(d_ap, dR)
    b.phase_l1attn(g["l1_ap"], bw, cw)
    b.phase_l1ffn(wg, wu, wd)
    b.store_x(yout)
    return b

def _build_fused(g):
    b = Build(["l0head", "l0ffn", "l1attn"]).setup()
    xin = b.din("x_in", (DM, SEQ), F32)
    yout = b.dout("y_out", (DM, SEQ), F32)
    wq = b.win("l0a_wqkvT", g["l0a_wqkvT"])
    wo = b.win("l0a_woutT", g["l0a_woutT"])
    hs = _l0ffn_handles(b, g)
    bw = b.win("l1_bwT", g["l1_bwT"])
    cw = b.win("l1_cwT", g["l1_cwT"])
    wg = b.win("l1_wgT", g["l1_wgT"])
    wu = b.win("l1_wuT", g["l1_wuT"])
    wd = b.win("l1_wdT", g["l1_wdT"])
    attnC = b.dint("attnC", (DM, SEQ), F32)
    b.alloc_xi()
    b.phase_ln0(xin)
    aCR = b.phase_l0attn(wq, wo, attnC)
    b.phase_l0ffn(hs)
    b.phase_combine_fused(xin, attnC.rearrange("(g p) t -> p g t", p=128),
                          aCR)
    d_ap, dR = b.small("l1_D", g["l1_D"], F32)
    b.phase_ln1(d_ap, dR)
    b.phase_l1attn(g["l1_ap"], bw, cw)
    b.phase_l1ffn(wg, wu, wd)
    b.store_x(yout)
    return b

def kernel(**inputs):
    g, xs = host_prepare(inputs)
    per_core = [{"x_in": xs[c]} for c in range(N_CORES)]
    if FUSED:
        b = _build_fused(g)
        rs = b.run(per_core, ["y_out"])
        return np.stack([np.ascontiguousarray(r["y_out"].T)
                         for r in rs]).astype(np.float32)
    b1 = _build_l0head(g)
    r1 = b1.run(per_core, ["xn0", "attnC_o"])
    b2 = _build_l0ffn(g)
    r2 = b2.run([{"xn0": r1[c]["xn0"]} for c in range(N_CORES)], ["xiF"])
    b3 = _build_l1(g)
    r3 = b3.run([{"x_in": xs[c], "attnC_o": r1[c]["attnC_o"],
                  "xiF": r2[c]["xiF"]} for c in range(N_CORES)], ["y_out"])
    return np.stack([np.ascontiguousarray(r3[c]["y_out"].T)
                     for c in range(N_CORES)]).astype(np.float32)


# version 26 (final)
# speedup vs baseline: 1.0497x; 1.0469x over previous
import contextlib
import numpy as np
import ml_dtypes
import concourse.bass as bass
import concourse.mybir as mybir
from concourse.bass_utils import run_bass_kernel_spmd

F32 = mybir.dt.float32
F32R = mybir.dt.float32r
BF16 = mybir.dt.bfloat16
I32 = mybir.dt.int32
AF = mybir.ActivationFunctionType
ALU = mybir.AluOpType
AX = mybir.AxisListType
BF16NP = ml_dtypes.bfloat16

N_CORES = 8
SEQ = 2048
DM = 1024
NG = DM // 128
NCH = SEQ // 512
HEADS = 16
DH = 64
WIN = 256
DI = 2730
HID_L1 = 3072
INV127 = 1.0 / 127.0
DEPTH_FFN = 6


class R:
    __slots__ = ("name", "birth")
    def __init__(self, name, birth=None):
        self.name = name
        self.birth = birth
    def __repr__(self):
        return f"R({self.name})"

class _Op:
    __slots__ = ("idx", "engine", "fn", "is_dma", "dma_idx", "deps",
                 "milestone", "mnum", "tag", "sync_all", "small_out", "cost",
                 "pin")

COMPUTE_ENGINES = ("tensor", "vector", "scalar", "gpsimd")
ALL_ENGINES = ("sync", "tensor", "vector", "scalar", "gpsimd")

SERIAL_DEBUG = False
SIM_SAFE = False
RESCHED = True

class Prog:
    def __init__(self, nc):
        self.nc = nc
        self.ops = []
        self._lw = {}
        self._rd = {}
        self._dma_count = dict.fromkeys(ALL_ENGINES, 0)
        self.pinning = True

    def add(self, engine, fn, reads=(), writes=(), dma=False, tag="",
            extra_deps=(), sync_all=False, small_out=False, cost=300.0):
        idx = len(self.ops)
        deps = set(extra_deps)
        if SERIAL_DEBUG and idx > 0:
            deps.add(idx - 1)
        for r in tuple(reads) + tuple(writes):
            if getattr(r, "birth", None):
                for br in r.birth:
                    w = self._lw.get(br)
                    if w is not None:
                        deps.add(w)
                    for rr in self._rd.get(br, ()):
                        deps.add(rr)
                r.birth = None
        for r in reads:
            w = self._lw.get(r)
            if w is not None:
                deps.add(w)
            self._rd.setdefault(r, []).append(idx)
        for r in writes:
            w = self._lw.get(r)
            if w is not None:
                deps.add(w)
            for rr in self._rd.get(r, ()):
                deps.add(rr)
            self._lw[r] = idx
            self._rd[r] = []
        deps.discard(idx)
        op = _Op()
        op.idx, op.engine, op.fn, op.is_dma = idx, engine, fn, dma
        op.dma_idx = None
        if dma:
            op.dma_idx = self._dma_count[engine]
            self._dma_count[engine] += 1
        op.deps, op.milestone, op.mnum, op.tag = deps, False, 0, tag
        op.sync_all = sync_all
        op.small_out = small_out
        op.cost = cost
        op.pin = self.pinning
        self.ops.append(op)
        return idx

    @staticmethod
    def _fsz(ap):
        try:
            return ap.free_size()
        except Exception:
            return 128

    def dma(self, queue, out, in_, reads=(), writes=(), tag="", **kw):
        try:
            b = self._fsz(out) * 128 * _DT_BYTES.get(out.dtype, 4)
        except Exception:
            b = 256 * 1024
        return self.add(queue,
                        lambda e, o=out, i=in_, kw=kw: e.dma_start(out=o, in_=i, **kw),
                        reads=reads, writes=writes, dma=True, tag=tag,
                        cost=b * 0.008 + 900.0)

    def mm(self, ps, lhsT, rhs, start, stop, reads=(), writes=(), tag=""):
        n = self._fsz(rhs)
        return self.add("tensor",
                        lambda e, a=ps, b=lhsT, c=rhs, s=start, t=stop:
                            e.matmul(a, b, c, start=s, stop=t),
                        reads=reads, writes=writes, tag=tag,
                        cost=0.45 * n + 120.0)

    def trans(self, ps, in_, ident, reads=(), writes=(), tag=""):
        return self.add("tensor",
                        lambda e, a=ps, b=in_, c=ident: e.transpose(a, b, c),
                        reads=reads, writes=writes, tag=tag, cost=280.0)

    def act(self, out, in_, func, bias=None, scale=None, accum=None,
            reads=(), writes=(), tag=""):
        kw = {}
        if bias is not None:
            kw["bias"] = bias
        if scale is not None:
            kw["scale"] = scale
        if accum is not None:
            kw["accum_out"] = accum
        sa = self._isap(bias) or self._isap(scale)
        return self.add("scalar",
                        lambda e, o=out, i=in_, f=func, kw=kw:
                            e.activation(out=o, in_=i, func=f, **kw),
                        reads=reads, writes=writes, tag=tag, sync_all=sa,
                        small_out=self._small(out) or accum is not None,
                        cost=0.8 * self._fsz(in_) + 260.0)

    def tt(self, out, in0, in1, op, engine="vector", reads=(), writes=(), tag=""):
        return self.add(engine,
                        lambda e, o=out, a=in0, b=in1, p=op:
                            e.tensor_tensor(out=o, in0=a, in1=b, op=p),
                        reads=reads, writes=writes, tag=tag,
                        small_out=self._small(out),
                        cost=1.1 * self._fsz(out) + 90.0)

    @staticmethod
    def _isap(v):
        return v is not None and not isinstance(v, (int, float))

    @staticmethod
    def _small(out):
        try:
            return out.free_size() < 128
        except Exception:
            return True

    def ts(self, out, in0, s1, s2=None, op0=ALU.mult, op1=ALU.add,
           engine="vector", reads=(), writes=(), tag=""):
        sa = self._isap(s1) or self._isap(s2)
        if s2 is None:
            return self.add(engine,
                            lambda e, o=out, a=in0, x=s1, p0=op0:
                                e.tensor_single_scalar(out=o, in_=a, scalar=x,
                                                       op=p0),
                            reads=reads, writes=writes, tag=tag, sync_all=sa,
                            small_out=self._small(out),
                            cost=1.1 * self._fsz(out) + 90.0)
        return self.add(engine,
                        lambda e, o=out, a=in0, x=s1, y=s2, p0=op0, p1=op1:
                            e.tensor_scalar(out=o, in0=a, scalar1=x, scalar2=y,
                                            op0=p0, op1=p1),
                        reads=reads, writes=writes, tag=tag, sync_all=sa,
                        small_out=self._small(out),
                        cost=1.1 * self._fsz(out) + 90.0)

    def stt(self, out, in0, scalar, in1, op0, op1, accum=None,
            engine="vector", reads=(), writes=(), tag=""):
        return self.add(engine,
                        lambda e, o=out, a=in0, s=scalar, b=in1, p0=op0, p1=op1,
                               ac=accum:
                            e.scalar_tensor_tensor(out=o, in0=a, scalar=s, in1=b,
                                                   op0=p0, op1=p1, accum_out=ac),
                        reads=reads, writes=writes, tag=tag,
                        sync_all=self._isap(scalar),
                        small_out=self._small(out) or accum is not None,
                        cost=1.1 * self._fsz(out) + 90.0)

    def red(self, out, in_, op=ALU.add, negate=False, reads=(), writes=(), tag=""):
        return self.add("vector",
                        lambda e, o=out, i=in_, p=op, n=negate:
                            e.tensor_reduce(out=o, in_=i, axis=AX.X, op=p, negate=n),
                        reads=reads, writes=writes, tag=tag, small_out=True,
                        cost=1.1 * self._fsz(in_) + 90.0)

    def copy(self, out, in_, engine="vector", reads=(), writes=(), tag=""):
        return self.add(engine,
                        lambda e, o=out, i=in_: e.tensor_copy(out=o, in_=i),
                        reads=reads, writes=writes, tag=tag,
                        small_out=self._small(out),
                        cost=1.1 * self._fsz(out) + 90.0)

    def recip(self, out, in_, reads=(), writes=(), tag=""):
        return self.add("vector",
                        lambda e, o=out, i=in_: e.reciprocal(out=o, in_=i),
                        reads=reads, writes=writes, tag=tag, small_out=True,
                        cost=120.0)

    def scan(self, out, d0, d1, initial, reads=(), writes=(), tag=""):
        return self.add("vector",
                        lambda e, o=out, a=d0, b=d1, it=initial:
                            e.tensor_tensor_scan(out=o, data0=a, data1=b,
                                                 initial=it, op0=ALU.mult,
                                                 op1=ALU.add),
                        reads=reads, writes=writes, tag=tag,
                        sync_all=self._isap(initial),
                        small_out=self._small(out),
                        cost=1.3 * self._fsz(out) + 120.0)

    def final_fence(self):
        alldma = {op.idx for op in self.ops if op.is_dma}
        self.add("sync", lambda e: e.nop(), extra_deps=alldma,
                 tag="final_fence")

    def _order(self):
        import heapq
        ops = self.ops
        n = len(ops)
        succ = [[] for _ in range(n)]
        indeg = [0] * n
        for op in ops:
            for d in op.deps:
                succ[d].append(op.idx)
                indeg[op.idx] += 1
        lastpin = {}
        for op in ops:
            if op.pin:
                lp = lastpin.get(op.engine)
                if lp is not None and lp not in op.deps:
                    succ[lp].append(op.idx)
                    indeg[op.idx] += 1
                lastpin[op.engine] = op.idx
        pr = [0.0] * n
        for i in range(n - 1, -1, -1):
            m = 0.0
            for s in succ[i]:
                if pr[s] > m:
                    m = pr[s]
            pr[i] = ops[i].cost + m
        md = [0.0] * n
        ready = {e: [] for e in ALL_ENGINES}
        clock = dict.fromkeys(ALL_ENGINES, 0.0)
        for i in range(n):
            if indeg[i] == 0:
                heapq.heappush(ready[ops[i].engine], (-pr[i], i))
        order = []
        ndone = 0
        while ndone < n:
            e_best, c_best = None, None
            for e in ALL_ENGINES:
                if ready[e] and (c_best is None or clock[e] < c_best):
                    e_best, c_best = e, clock[e]
            h = ready[e_best]
            popped = []
            pick = None
            for _ in range(6):
                if not h:
                    break
                it = heapq.heappop(h)
                if md[it[1]] <= c_best:
                    pick = it
                    break
                popped.append(it)
            if pick is None:
                pick = min(popped, key=lambda it: (md[it[1]], it[0]))
                popped.remove(pick)
            for it in popped:
                heapq.heappush(h, it)
            i = pick[1]
            start = max(c_best, md[i])
            fin = start + ops[i].cost
            clock[e_best] = fin
            order.append(ops[i])
            ndone += 1
            for s in succ[i]:
                if md[s] < fin:
                    md[s] = fin
                indeg[s] -= 1
                if indeg[s] == 0:
                    heapq.heappush(ready[ops[s].engine], (-pr[s], s))
        return order

    def emit(self, st):
        nc = self.nc
        order = self._order() if RESCHED else self.ops
        if RESCHED:
            dcnt = dict.fromkeys(ALL_ENGINES, 0)
            for op in order:
                if op.is_dma:
                    op.dma_idx = dcnt[op.engine]
                    dcnt[op.engine] += 1
        for op in self.ops:
            for d in op.deps:
                dep = self.ops[d]
                if (not dep.is_dma) and (SIM_SAFE or op.sync_all
                                         or dep.small_out
                                         or dep.engine != op.engine):
                    dep.milestone = True
        counts = dict.fromkeys(ALL_ENGINES, 0)
        for op in order:
            if op.milestone and not op.is_dma:
                counts[op.engine] += 1
                op.mnum = counts[op.engine]
        engines_used = []
        for e in ALL_ENGINES:
            if any(op.engine == e for op in self.ops):
                engines_used.append(e)
        NPOOL = 12
        esem, qpool = {}, {}
        for e in engines_used:
            if any((not o.is_dma) and o.milestone and o.engine == e
                   for o in self.ops):
                esem[e] = st.enter_context(nc.semaphore("es_" + e))
            ndma = self._dma_count[e]
            if ndma:
                qpool[e] = [st.enter_context(nc.semaphore(f"qs_{e}{i}"))
                            for i in range(min(NPOOL, ndma))]
        def dma_sem_val(dep):
            pool = qpool[dep.engine]
            k = len(pool)
            return pool[dep.dma_idx % k], 16 * (dep.dma_idx // k + 1)
        sched = {e: [] for e in engines_used}
        hw = {}
        for op in order:
            waits = {}
            if op.is_dma:
                pool = qpool[op.engine]
                k = len(pool)
                cyc = op.dma_idx // k
                if cyc > 0:
                    waits[pool[op.dma_idx % k]] = 16 * cyc
            for d in sorted(op.deps):
                dep = self.ops[d]
                if dep.is_dma:
                    sem, val = dma_sem_val(dep)
                elif (dep.engine == op.engine and not SIM_SAFE
                      and not op.sync_all and not dep.small_out):
                    continue
                else:
                    sem, val = esem[dep.engine], dep.mnum
                if waits.get(sem, -1) < val:
                    waits[sem] = val
            final = []
            for sem, val in waits.items():
                key = (op.engine, sem)
                if hw.get(key, 0) >= val:
                    continue
                hw[key] = val
                final.append((sem, val))
            inc = None
            if op.is_dma:
                sem, val = dma_sem_val(op)
                inc = (sem, 16)
            elif op.milestone:
                inc = (esem[op.engine], 1)
            sched[op.engine].append((final, op.fn, inc, op.tag))
        block = st.enter_context(nc.Block())
        def runner(entries):
            def run(eng):
                for waits, fn, inc, tag in entries:
                    for sem, val in waits:
                        eng.wait_ge(sem, val)
                    ins = fn(eng)
                    if inc is not None:
                        if ins is None:
                            ins = eng.nop()
                        ins.then_inc(inc[0], inc[1])
            return run
        for e in engines_used:
            getattr(block, e)(runner(sched[e]))


class Region:
    __slots__ = ("arena", "off", "n", "name", "dt", "ratio", "res_list", "R")
    def __init__(self, arena, off, n, name, dt, ratio, birth):
        self.arena, self.off, self.n, self.name = arena, off, n, name
        self.dt, self.ratio = dt, ratio
        self.R = R(name, birth=tuple(birth))
        self.res_list = [self.R]
    def sub(self, name):
        r = R(name, birth=self.R.birth)
        self.res_list.append(r)
        return r
    def subs(self, name, n):
        return [self.sub(f"{name}{i}") for i in range(n)]
    def v(self, p0=0, np_=128, c0=0, nc_=None):
        r = self.ratio
        if nc_ is None:
            nc_ = self.n // r - c0
        assert 0 <= c0 and (c0 + nc_) * r <= self.n, \
            (self.name, c0, nc_, self.n, r)
        ap = self.arena.ap[p0:p0 + np_,
                           self.off + c0 * r:self.off + (c0 + nc_) * r]
        if self.dt is not None:
            ap = ap.bitcast(self.dt)
        return ap
    def free(self):
        self.arena.free_region(self)

_DT_BYTES = {F32: 4, F32R: 4, BF16: 2, I32: 4}

class Arena:
    def __init__(self, st, nc, name, dtype, n_elems):
        self.h = st.enter_context(nc.sbuf_tensor(name, [128, n_elems], dtype))
        self.ap = self.h.ap()
        self.name = name
        self.dtype = dtype
        self.n = n_elems
        self._free = [(0, n_elems)]
        self._tombs = []
        self.hiwater = 0

    def alloc(self, n, name="", dt=None):
        ratio = 1
        if dt is not None and dt != self.dtype:
            ratio = _DT_BYTES[dt] // _DT_BYTES[self.dtype]
            assert ratio >= 1
        else:
            dt = None
        nn = (n * ratio + 15) & ~15
        for i, (o, sz) in enumerate(self._free):
            if sz >= nn:
                if sz == nn:
                    self._free.pop(i)
                else:
                    self._free[i] = (o + nn, sz - nn)
                self.hiwater = max(self.hiwater, o + nn)
                birth = []
                keep = []
                for (to, tn, tres) in self._tombs:
                    if to < o + nn and o < to + tn:
                        birth.extend(tres)
                        if to < o:
                            keep.append((to, o - to, tres))
                        if to + tn > o + nn:
                            keep.append((o + nn, to + tn - (o + nn), tres))
                    else:
                        keep.append((to, tn, tres))
                self._tombs = keep
                return Region(self, o, nn, name, dt, ratio, birth)
        raise MemoryError(f"{self.name}: cannot alloc {nn} cols for {name} "
                          f"(free={self._free})")

    def free_region(self, region):
        self._tombs.append((region.off, region.n, tuple(region.res_list)))
        self._free.append((region.off, region.n))
        self._free.sort()
        merged = []
        for seg in self._free:
            if merged and merged[-1][0] + merged[-1][1] == seg[0]:
                merged[-1] = (merged[-1][0], merged[-1][1] + seg[1])
            else:
                merged.append(list(seg))
        self._free = [tuple(s) for s in merged]


def _t(a):
    return np.ascontiguousarray(np.asarray(a).T)

def _rs_feat(a):
    a = np.asarray(a)
    f = a.shape[-1]
    ncol = -(-f // 128)
    lead = a.shape[:-1]
    ar = a.reshape(-1, f)
    out = np.zeros((ar.shape[0], 128, ncol), a.dtype)
    for i in range(ar.shape[0]):
        pad = np.zeros(128 * ncol, a.dtype)
        pad[:f] = ar[i]
        out[i] = pad.reshape(ncol, 128).T
    return out.reshape(lead + (128, ncol))

def host_prepare(inputs):
    g = {}
    g["l0a_wqkvT"] = _t(inputs["L0_attn_W_qkv"]).astype(np.float32)
    g["l0a_woutT"] = _t(inputs["L0_attn_W_out"]).astype(np.float32)
    g["wqkv"] = np.ascontiguousarray(inputs["L0_ffn_Wqkv"], np.float32)
    g["wo"] = np.ascontiguousarray(inputs["L0_ffn_Wo"], np.float32)
    g["w1"] = np.ascontiguousarray(inputs["L0_ffn_W1"], np.float32)
    w2 = np.asarray(inputs["L0_ffn_W2"], np.float32)
    w2p = np.zeros((6, 2816, 1024), np.float32)
    w2p[:, :DI, :] = w2
    g["w2"] = w2p
    g["qs"] = np.ascontiguousarray(inputs["L0_ffn_q_scale"], np.float32).reshape(6, 1, 64)
    g["ks"] = np.ascontiguousarray(inputs["L0_ffn_k_scale"], np.float32).reshape(6, 1, 64)
    g["ln_g"] = _rs_feat(np.asarray(inputs["L0_ffn_ln_g"], np.float32))
    g["ln_b"] = _rs_feat(np.asarray(inputs["L0_ffn_ln_b"], np.float32))
    b1 = np.asarray(inputs["L0_ffn_b1"], np.float32)
    g["b1a"] = _rs_feat(b1[:, :DI])
    g["b1g"] = _rs_feat(b1[:, DI:])
    g["b2"] = _rs_feat(np.asarray(inputs["L0_ffn_b2"], np.float32))
    g["l1_ap"] = np.asarray(inputs["L1_attn_a_params"], np.float32).reshape(1, 256)
    g["l1_bwT"] = _t(inputs["L1_attn_B_w"]).astype(np.float32)
    g["l1_cwT"] = _t(inputs["L1_attn_C_w"]).astype(np.float32)
    g["l1_D"] = _rs_feat(np.asarray(inputs["L1_attn_D"], np.float32))
    g["l1_wgT"] = _t(inputs["L1_ffn_w_gate"]).astype(np.int32)
    g["l1_wuT"] = _t(inputs["L1_ffn_w_up"]).astype(np.int32)
    g["l1_wdT"] = _t(inputs["L1_ffn_w_down"]).astype(np.int32)
    x = np.asarray(inputs["L0_attn_x"], np.float32)
    xs = [np.ascontiguousarray(x[b].T) for b in range(N_CORES)]
    return g, xs


def make_consts():
    c = {}
    c["ident_bf"] = np.eye(128, dtype=np.float32)
    c["ident_f32"] = np.eye(128, dtype=np.float32)
    c["ones_r_col"] = np.ones((128, 1), np.float32)
    c["ones_r_row"] = np.ones((1, 128), np.float32)
    c["ones_bf_col"] = np.ones((128, 1), np.float32)
    c["ones_bf_row"] = np.ones((1, 128), np.float32)
    c["ones_f32_row64"] = np.ones((1, 64), np.float32)
    c["ones_f32_row"] = np.ones((1, 128), np.float32)
    c["ones_f32_big"] = np.ones((128, 512), np.float32)
    c["zeros_r"] = np.zeros((128, 16), np.float32)
    c["eps_col"] = np.full((128, 1), 1e-5, np.float32)
    half = np.arange(0, DH, 2, dtype=np.float64)
    inv_freq = 1.0 / 10000.0 ** (half / DH)
    t = np.arange(SEQ, dtype=np.float64)
    freqs = np.concatenate([t[:, None] * inv_freq[None, :]] * 2, -1)
    base_scale = (half + 0.4 * DH) / (1.4 * DH)
    power = (t - SEQ // 2) / (WIN // 2)
    scale = np.concatenate([base_scale[None, :] ** power[:, None]] * 2, -1)
    cos, sin = np.cos(freqs), np.sin(freqs)
    sgn = np.concatenate([-np.ones(32), np.ones(32)])[None, :]
    def tile_major(tab):
        return np.ascontiguousarray(
            tab.reshape(16, 128, 64).transpose(1, 0, 2).reshape(128, 16 * 64)
        ).astype(np.float32)
    c["rot_t1q"] = tile_major(cos * scale)
    c["rot_t2q"] = tile_major(sin * scale * sgn)
    c["rot_t1k"] = tile_major(cos / scale)
    c["rot_t2k"] = tile_major(sin / scale * sgn)
    jj = np.arange(512)[None, :]
    j2 = np.arange(256)[None, :]
    for s in range(2):
        ii = (np.arange(128) + 128 * s)[:, None]
        c[f"mask_s{s}"] = np.where((jj >= ii) & (jj <= ii + 256), 0.0,
                                   -1e30).astype(np.float32)
        c[f"mask0_s{s}"] = np.where(j2 <= ii, 0.0, -1e30).astype(np.float32)
    qq = np.arange(256)[None, :]
    for ch in range(4):
        j = (np.arange(128) + 128 * ch)[:, None]
        c[f"bmask_c{ch}"] = ((qq >= j - 256) & (qq <= j)).astype(np.float32)
    for ch in range(2):
        j = (np.arange(128) + 128 * ch)[:, None]
        c[f"bmask0_c{ch}"] = (qq >= j).astype(np.float32)
    cc = np.arange(256)[None, :]
    for s in range(2):
        ii = (np.arange(128) + 128 * s)[:, None]
        c[f"cmask_s{s}"] = np.where(cc <= ii, 0.125, 0.0).astype(np.float32)
    M = np.zeros((512, 256), np.float64)
    jv = np.arange(512)
    for m in range(1, 257):
        M[:, m - 1] += -np.sin(2 * np.pi * jv * m / 512)
    for m in range(257, 512):
        M[:, 512 - m] += np.sin(2 * np.pi * jv * m / 512)
    c["l1_M"] = np.ascontiguousarray(
        M.reshape(4, 128, 256).transpose(1, 0, 2).reshape(128, 4 * 256)
    ).astype(np.float32)
    return c

_CONSTS = make_consts()

_CONST_DT = {
    "ident_bf": BF16, "ident_f32": F32,
    "ones_r_col": F32R, "ones_r_row": F32R,
    "ones_bf_col": BF16, "ones_bf_row": BF16,
    "ones_f32_row64": F32, "ones_f32_row": F32, "ones_f32_big": F32,
    "zeros_r": F32R, "eps_col": F32,
    "rot_t1q": BF16, "rot_t2q": BF16, "rot_t1k": BF16, "rot_t2k": BF16,
    "mask_s0": BF16, "mask_s1": BF16, "mask0_s0": F32, "mask0_s1": F32,
    "bmask_c0": BF16, "bmask_c1": BF16, "bmask_c2": BF16, "bmask_c3": BF16,
    "bmask0_c0": BF16, "bmask0_c1": BF16,
    "cmask_s0": BF16, "cmask_s1": BF16,
    "l1_M": F32,
}


ARENA_R_ELEMS = 25700
ARENA_B_ELEMS = 36300

class Build:
    def __init__(self, phases):
        self.phases = phases
        self.nc = bass.Bass("TRN2", target_bir_lowering=False, debug=False)
        self.st = contextlib.ExitStack()
        self.P = Prog(self.nc)
        self.shared_inputs = {}
        self._uid = 0

    def uid(self):
        self._uid += 1
        return self._uid

    def din(self, name, shape, dt):
        return self.nc.dram_tensor(name, list(shape), dt, kind="ExternalInput").ap()

    def dout(self, name, shape, dt):
        return self.nc.dram_tensor(name, list(shape), dt, kind="ExternalOutput").ap()

    def dint(self, name, shape, dt):
        return self.nc.dram_tensor(name, list(shape), dt, kind="Internal").ap()

    def win(self, name, arr, dt=None):
        if dt is None:
            dt = {np.dtype(np.float32): F32, np.dtype(np.int32): I32}[arr.dtype]
        self.shared_inputs[name] = arr
        return self.din(name, arr.shape, dt)

    def setup(self):
        nc, st = self.nc, self.st
        self.ar = Arena(st, nc, "arena_r", F32R, ARENA_R_ELEMS)
        self.ab = Arena(st, nc, "arena_b", BF16, ARENA_B_ELEMS)
        self.sf = st.enter_context(nc.sbuf_tensor("sf_small", [128, 1024], F32))
        self.sfR = R("sf_small")
        self.sfRp = [R(f"sfp{j}") for j in range(4)]
        if "l1attn" in self.phases or "test" in self.phases:
            self.tabf = st.enter_context(
                nc.sbuf_tensor("tabf", [128, 2560], F32))
        self.pb = [st.enter_context(nc.psum_tensor(f"pb{i}", [128, 512], F32))
                   for i in range(6)]
        self.tb = [st.enter_context(nc.psum_tensor(f"tb{i}", [128, 1024], BF16))
                   for i in range(2)]
        self.pbR = [R(f"pb{i}") for i in range(6)]
        self.tbR = [R(f"tb{i}") for i in range(2)]
        self.c = {}
        self.cR = R("consts")
        for name in self.const_names():
            arr = _CONSTS[name]
            dt = _CONST_DT[name]
            npdt = {F32: np.float32, F32R: np.float32, BF16: BF16NP}[dt]
            self.shared_inputs["c_" + name] = np.ascontiguousarray(arr.astype(npdt))
            hin = self.din("c_" + name, arr.shape,
                           F32 if dt == F32R else dt)
            t = st.enter_context(nc.sbuf_tensor("sc_" + name, list(arr.shape), dt))
            src = hin.bitcast(F32R) if dt == F32R else hin
            self.P.dma("sync", t.ap()[:, :], src, writes=[self.cR],
                       tag="c_" + name)
            self.c[name] = t.ap()
        return self

    def const_names(self):
        names = ["ident_bf", "ident_f32", "ones_r_col", "ones_r_row",
                 "ones_bf_col", "ones_bf_row", "ones_f32_row64",
                 "ones_f32_row", "ones_f32_big", "zeros_r", "eps_col"]
        if "l0ffn" in self.phases:
            names += ["rot_t1q", "rot_t2q", "rot_t1k", "rot_t2k",
                      "mask_s0", "mask_s1",
                      "bmask_c0", "bmask_c1", "bmask_c2", "bmask_c3"]
        if "l0head" in self.phases:
            names += ["cmask_s0", "cmask_s1"]
        if "l1attn" in self.phases:
            names += ["l1_M"]
        return names

    def alloc_x(self):
        self.x = self.ar.alloc(NG * SEQ, "x")
        self.xR = self.x.subs("x", NG)

    def x_view(self, g, c0=0, n=SEQ, p0=0, np_=128):
        if getattr(self, "x_alias_xi", False):
            return self.xi_data(g, c0, n, p0, np_)
        return self.x.v(p0, np_, g * SEQ + c0, n)

    def load_x(self, src_ap):
        s = src_ap.rearrange("(g p) t -> p g t", p=128)
        self.P.dma("sync",
                   self.x.v(0, 128, 0, NG * SEQ).rearrange(
                       "p (g t) -> p g t", t=SEQ),
                   s.bitcast(F32R), writes=list(self.xR), tag="load_x")

    def store_x(self, dst_ap, tag="store_x"):
        if getattr(self, "x_alias_xi", False):
            return self.store_xi_as(dst_ap, tag=tag)
        d = dst_ap.rearrange("(g p) t -> p g t", p=128)
        return self.P.dma(
            "sync", d,
            self.x.v().rearrange("p (g t) -> p g t", t=SEQ).bitcast(F32),
            reads=list(self.xR), tag=tag)

    def phase_combine_split(self, x_h, a_h, f_h):
        P = self.P
        srcs = [h.rearrange("(g p) t -> p g t", p=128)
                for h in (x_h, a_h, f_h)]
        bufs = [self.ar.alloc(512, f"cmb{i}") for i in range(4)]
        for g_ in range(NG):
            for ch in range(NCH):
                i2 = ((g_ * NCH + ch) % 2) * 2
                bA, bB = bufs[i2], bufs[i2 + 1]
                sl = (slice(None), g_, slice(ch * 512, (ch + 1) * 512))
                P.dma("sync", bA.v(), srcs[0][sl].bitcast(F32R),
                      writes=[bA.R], tag=f"cmbx{g_}_{ch}")
                P.dma("sync", bB.v(), srcs[1][sl].bitcast(F32R),
                      writes=[bB.R], tag=f"cmba{g_}_{ch}")
                P.tt(self.x_view(g_, ch * 512, 512), bA.v().bitcast(F32),
                     bB.v().bitcast(F32), ALU.add,
                     reads=[bA.R, bB.R], writes=[self.xR[g_]],
                     tag=f"cmb1{g_}_{ch}")
                P.dma("sync", bA.v(), srcs[2][sl].bitcast(F32R),
                      writes=[bA.R], tag=f"cmbf{g_}_{ch}")
                P.tt(self.x_view(g_, ch * 512, 512),
                     self.x_view(g_, ch * 512, 512).bitcast(F32),
                     bA.v().bitcast(F32), ALU.add,
                     reads=[bA.R, self.xR[g_]], writes=[self.xR[g_]],
                     tag=f"cmb2{g_}_{ch}")
        for b_ in bufs:
            b_.free()

    def phase_combine_fused(self, x_h, aC3, aCR):
        P = self.P
        xsrc = x_h.rearrange("(g p) t -> p g t", p=128)
        bufs = [self.ar.alloc(512, f"cmf{i}") for i in range(4)]
        for g_ in range(NG):
            for ch in range(NCH):
                i2 = ((g_ * NCH + ch) % 2) * 2
                bA, bB = bufs[i2], bufs[i2 + 1]
                sl = (slice(None), g_, slice(ch * 512, (ch + 1) * 512))
                P.dma("sync", bA.v(), xsrc[sl].bitcast(F32R),
                      writes=[bA.R], tag=f"cmfx{g_}_{ch}")
                P.dma("sync", bB.v(), aC3[sl].bitcast(F32R),
                      reads=[aCR], writes=[bB.R], tag=f"cmfa{g_}_{ch}")
                P.tt(bA.v(), bA.v().bitcast(F32), bB.v().bitcast(F32),
                     ALU.add, reads=[bA.R, bB.R], writes=[bA.R],
                     tag=f"cmf1{g_}_{ch}")
                P.tt(self.xi_data(g_, ch * 512, 512),
                     self.xi_data(g_, ch * 512, 512).bitcast(F32),
                     bA.v().bitcast(F32), ALU.add,
                     reads=[bA.R, self.xiR[g_]], writes=[self.xiR[g_]],
                     tag=f"cmf2{g_}_{ch}")
        for b_ in bufs:
            b_.free()
        self.x_alias_xi = True
        self.xR = self.xiR

    def alloc_xi(self):
        self.xi = self.ar.alloc(NG * (SEQ + 1), "xi")
        self.xiR = self.xi.subs("xi", NG)
        z = self.c["zeros_r"]
        for g in range(NG):
            self.P.dma("sync", self.xi.v(0, 128, g * (SEQ + 1), 1), z[:, 0:1],
                       reads=[self.cR], writes=[self.xiR[g]], tag=f"xi_z{g}")

    def xi_data(self, g, c0=0, n=SEQ, p0=0, np_=128):
        return self.xi.v(p0, np_, g * (SEQ + 1) + 1 + c0, n)

    def xi_shift(self, g, c0=0, n=SEQ, p0=0, np_=128):
        return self.xi.v(p0, np_, g * (SEQ + 1) + c0, n)

    class ResidentSrc:
        def __init__(self, view_fn, res_list):
            self.view_fn, self.res = view_fn, res_list
        def chunk(self, b, ch):
            return [(self.view_fn(g, ch * 512, 512), self.res[g])
                    for g in range(NG)]
        def release(self):
            pass

    class HbmSrc:
        def __init__(self, b, dram_ap, name):
            self.ap3 = dram_ap.rearrange("(g p) t -> p g t", p=128)
            self.name = name
            self.bufs = [b.ar.alloc(512, f"{name}{g}") for g in range(NG)]
            self.res = [buf.R for buf in self.bufs]
        def chunk(self, b, ch):
            out = []
            for g in range(NG):
                b.P.dma("sync", self.bufs[g].v(),
                        self.ap3[:, g, ch * 512:(ch + 1) * 512].bitcast(F32R),
                        writes=[self.res[g]], tag=f"{self.name}{ch}_{g}")
                out.append((self.bufs[g].v(), self.res[g]))
            return out
        def release(self):
            for r in self.bufs:
                r.free()

    def emit_ln(self, src, cb, tag):
        P, c = self.P, self.c
        u = self.uid()
        sq = [self.ar.alloc(512, f"lnq{u}_{i}") for i in range(2)]
        sqR = [b.R for b in sq]
        rows = self.ab.alloc(5 * 512, f"lnrows{u}", dt=F32)
        rowR = rows.R
        SX, SQM, MU, T1, T2 = range(5)
        rv = lambda i: rows.v(0, 1, i * 512, 512)
        rstd = self.ar.alloc(512, f"lnrstd{u}")
        bprime = self.ar.alloc(512, f"lnbp{u}")
        rstdR = rstd.R
        bprime.res_list.append(rstdR)
        a_b = self.ab.alloc(512, f"lnab{u}", dt=F32)
        b_b = self.ab.alloc(512, f"lnbb{u}", dt=F32)
        t1b = [self.ab.alloc(512, f"lnt1{u}_{i}", dt=F32) for i in range(2)]
        abR, bbR = a_b.R, b_b.R
        t1R = [b.R for b in t1b]
        for ch in range(NCH):
            tiles = src.chunk(self, ch)
            psS, psSR = self.pb[0].ap()[0:1, :], self.pbR[0]
            psQ, psQR = self.pb[1].ap()[0:1, :], self.pbR[1]
            for g in range(NG):
                ap_, res_ = tiles[g]
                P.mm(psS, c["ones_r_col"][:, :], ap_, g == 0, g == NG - 1,
                     reads=[res_, self.cR], writes=[psSR],
                     tag=f"{tag}sx{ch}_{g}")
            for g in range(NG):
                ap_, res_ = tiles[g]
                b = sq[g % 2]
                P.act(b.v(), ap_, AF.Square, reads=[res_],
                      writes=[sqR[g % 2]], tag=f"{tag}q{ch}_{g}")
                P.mm(psQ, c["ones_r_col"][:, :], b.v(), g == 0, g == NG - 1,
                     reads=[sqR[g % 2], self.cR], writes=[psQR],
                     tag=f"{tag}qm{ch}_{g}")
            P.act(rv(SX), psS, AF.Copy, reads=[psSR], writes=[rowR],
                  tag=f"{tag}evS{ch}")
            P.act(rv(SQM), psQ, AF.Copy, reads=[psQR], writes=[rowR],
                  tag=f"{tag}evQ{ch}")
            P.ts(rv(MU), rv(SX), 1.0 / DM, None, reads=[rowR], writes=[rowR],
                 tag=f"{tag}mu{ch}")
            P.stt(rv(T1), rv(MU), -1.0, rv(MU), ALU.mult, ALU.mult,
                  reads=[rowR], writes=[rowR], tag=f"{tag}mu2{ch}")
            P.stt(rv(T2), rv(SQM), 1.0 / DM, rv(T1), ALU.mult, ALU.add,
                  reads=[rowR], writes=[rowR], tag=f"{tag}var{ch}")
            P.act(rv(T1), rv(T2), AF.Ln, bias=c["eps_col"][0:1, :],
                  reads=[rowR, self.cR], writes=[rowR], tag=f"{tag}ln{ch}")
            P.act(rstd.v(0, 1), rv(T1), AF.Exp, scale=-0.5,
                  reads=[rowR], writes=[rstdR], tag=f"{tag}rstd{ch}")
            P.stt(bprime.v(0, 1), rv(MU), -1.0, rstd.v(0, 1).bitcast(F32),
                  ALU.mult, ALU.mult,
                  reads=[rowR, rstdR], writes=[rstdR], tag=f"{tag}bp{ch}")
            psA, psAR = self.pb[2].ap()[:, :], self.pbR[2]
            psB, psBR = self.pb[3].ap()[:, :], self.pbR[3]
            P.mm(psA, c["ones_r_row"][:, :], rstd.v(0, 1), True, True,
                 reads=[rstdR, self.cR], writes=[psAR], tag=f"{tag}bcA{ch}")
            P.mm(psB, c["ones_r_row"][:, :], bprime.v(0, 1), True, True,
                 reads=[rstdR, self.cR], writes=[psBR], tag=f"{tag}bcB{ch}")
            P.act(a_b.v(), psA, AF.Copy, reads=[psAR], writes=[abR],
                  tag=f"{tag}abe{ch}")
            P.act(b_b.v(), psB, AF.Copy, reads=[psBR], writes=[bbR],
                  tag=f"{tag}bbe{ch}")
            for g in range(NG):
                ap_, res_ = tiles[g]
                t1 = t1b[g % 2]
                P.tt(t1.v(), ap_.bitcast(F32), a_b.v(), ALU.mult,
                     reads=[res_, abR], writes=[t1R[g % 2]],
                     tag=f"{tag}t1_{ch}_{g}")
                cb(g, ch, ap_, res_, t1.v(), t1R[g % 2], b_b.v(), bbR)
        for reg in (rows, rstd, bprime, a_b, b_b, t1b[0], t1b[1],
                    sq[0], sq[1]):
            reg.free()
        src.release()

    def phase_ln0(self, x_dram):
        src = self.HbmSrc(self, x_dram, "x0s")
        def cb(g, ch, tile, tileR, t1, t1R, b_b, bbR):
            self.P.tt(self.xi_data(g, ch * 512, 512), t1, b_b, ALU.add,
                      reads=[t1R, bbR], writes=[self.xiR[g]],
                      tag=f"ln0o{ch}_{g}")
        self.emit_ln(src, cb, "ln0")

    def phase_ln1(self, d_rs_ap, dR):
        self.xnb = self.ab.alloc(NG * SEQ, "xnb")
        self.xnbR = self.xnb.subs("xnb", NG)
        t2 = [self.ab.alloc(512, f"ln1t2_{i}", dt=F32) for i in range(2)]
        t2R = [b.R for b in t2]
        def cb(g, ch, tile, tileR, t1, t1R, b_b, bbR):
            P = self.P
            t = t2[g % 2]
            P.tt(t.v(), t1, b_b, ALU.add, reads=[t1R, bbR],
                 writes=[t2R[g % 2]], tag=f"ln1t2_{ch}_{g}")
            P.copy(self.xnb_view(g, ch * 512, 512), t.v(),
                   reads=[t2R[g % 2]], writes=[self.xnbR[g]],
                   tag=f"ln1xn_{ch}_{g}")
            P.stt(self.x_view(g, ch * 512, 512), t.v(), d_rs_ap[:, g:g + 1],
                  self.x_view(g, ch * 512, 512).bitcast(F32),
                  ALU.mult, ALU.add,
                  reads=[t2R[g % 2], dR, self.xR[g]], writes=[self.xR[g]],
                  tag=f"ln1D_{ch}_{g}")
        self.emit_ln(self.ResidentSrc(self.x_view, self.xR), cb, "ln1")

    def xnb_view(self, g, c0=0, n=SEQ, p0=0, np_=128):
        return self.xnb.v(p0, np_, g * SEQ + c0, n)

    def phase_l0ffn(self, hs):
        glu_st = self.dint("glu_st", (2816, SEQ), BF16)
        self._glu3 = glu_st.rearrange("(g p) t -> p g t", p=128)
        self._gluR = R("glu_st")
        w2st = self.dint("w2bf_st", (NG, 128, 2816), BF16)
        self._w2st = w2st
        self._w2stR = R("w2bf_st")
        for i in range(DEPTH_FFN):
            pl, qsb, ftab = self._l0f_params(i, hs)
            self._l0f_attn(i, hs, ftab)
            ftab.free()
            qsb.free()
            self.P.pinning = False
            self._l0f_ffn(i, hs, pl)
            self.P.pinning = True
            pl.free()

    def _l0f_params(self, i, hs):
        P, c = self.P, self.c
        pl = self.ab.alloc(68, f"pl{i}", dt=F32)
        for name, c0, n in (("lng", 0, 8), ("lnb", 8, 16), ("b1a", 16, 38),
                            ("b1g", 38, 60), ("b2", 60, 68)):
            P.dma("sync", pl.v(0, 128, c0, n - c0), hs[name][i],
                  writes=[pl.R], tag=f"pl_{name}{i}")
        sfa = self.sf.ap()
        P.dma("sync", sfa[0:1, 576:640], hs["qs"][i], writes=[self.sfR],
              tag=f"pl_qs{i}")
        P.dma("sync", sfa[0:1, 640:704], hs["ks"][i], writes=[self.sfR],
              tag=f"pl_ks{i}")
        qsb = self.ab.alloc(256, f"qsb{i}")
        for idx, (src0, dst0) in enumerate(((576, 0), (640, 128))):
            psq, psqR = self.pb[0].ap()[:, 0:64], self.pbR[0]
            P.mm(psq, c["ones_f32_row"][:, :], sfa[0:1, src0:src0 + 64],
                 True, True, reads=[self.sfR, self.cR], writes=[psqR],
                 tag=f"qsbc{i}_{idx}")
            P.act(qsb.v(0, 128, dst0, 64), psq, AF.Copy,
                  reads=[psqR], writes=[qsb.R], tag=f"qsbe{i}_{idx}")
            P.act(qsb.v(0, 128, dst0 + 64, 32), self.pb[0].ap()[:, 32:64],
                  AF.Copy, reads=[psqR], writes=[qsb.R], tag=f"qsw1{i}_{idx}")
            P.act(qsb.v(0, 128, dst0 + 96, 32), self.pb[0].ap()[:, 0:32],
                  AF.Copy, reads=[psqR], writes=[qsb.R], tag=f"qsw2{i}_{idx}")
        ftab = self.ab.alloc(4 * 1024, f"ftab{i}")
        for o, (tab, qcol) in enumerate((("rot_t1q", 0), ("rot_t2q", 64),
                                         ("rot_t1k", 128), ("rot_t2k", 192))):
            P.tt(ftab.v(0, 128, o * 1024, 1024).rearrange(
                     "p (a c) -> p a c", c=64),
                 c[tab][:, :].rearrange("p (a c) -> p a c", c=64),
                 qsb.v(0, 128, qcol, 64).rearrange(
                     "p (a c) -> p a c", a=1).broadcast_to((128, 16, 64)),
                 ALU.mult, reads=[qsb.R, self.cR], writes=[ftab.R],
                 tag=f"ftab{i}_{o}")
        return pl, qsb, ftab

    def _l0f_attn(self, i, hs, ftab):
        P, c = self.P, self.c
        vtm = self.ab.alloc(6 * 1024, f"vtm{i}")
        qfm = self.ab.alloc(NG * 512, f"qfm{i}")
        kfm = self.ab.alloc(NG * 768, f"kfm{i}")
        wst = [self.ar.alloc(NG * 512, f"l0fw{i}_{j}") for j in range(2)]
        rA = [self.ab.alloc(1024, f"rA{i}_{j}") for j in range(2)]
        rB = [self.ab.alloc(1024, f"rB{i}_{j}") for j in range(2)]
        tbank = self.tb[0].ap()
        sfa = self.sf.ap()
        rdn = sfa[0:1, 64:320]
        rdn2 = sfa[0:1, 320:576]
        for q4 in range(4):
            t0 = q4 * 512
            qtm = self.ab.alloc(4 * 1024, f"qtm{i}_{q4}")
            ktm = self.ab.alloc(4 * 1024, f"ktm{i}_{q4}")
            if q4 > 0:
                P.copy(vtm.v(0, 128, 0, 2048), vtm.v(0, 128, 4096, 2048),
                       reads=[vtm.R], writes=[vtm.R], tag=f"vh{i}_{q4}")
                for f in range(NG):
                    P.copy(kfm.v(0, 128, f * 768, 256),
                           kfm.v(0, 128, f * 768 + 512, 256),
                           reads=[kfm.R], writes=[kfm.R],
                           tag=f"kh{i}_{q4}_{f}")
            for wc in range(6):
                w = wst[wc % 2]
                P.dma("sync",
                      w.v(0, 128, 0, NG * 512).rearrange(
                          "p (g f) -> p g f", f=512),
                      hs["wqkv"][i][:, wc * 512:(wc + 1) * 512].rearrange(
                          "(g p) f -> p g f", p=128).bitcast(F32R),
                      writes=[w.R], tag=f"fw{i}_{q4}_{wc}")
                for tt in range(4):
                    ps, psR = self.pb[tt % 2].ap()[:, :], self.pbR[tt % 2]
                    for k in range(NG):
                        lhsT = (self.xi_data(k, t0 + tt * 128, 128) if k < 4
                                else self.xi_shift(k, t0 + tt * 128, 128))
                        P.mm(ps, lhsT, w.v(0, 128, k * 512, 512),
                             k == 0, k == NG - 1,
                             reads=[self.xiR[k], w.R], writes=[psR],
                             tag=f"fqkv{i}_{q4}_{wc}_{tt}_{k}")
                    dst = (qtm, ktm, vtm)[wc // 2]
                    base = (tt * 1024 if wc < 4 else (2 + tt) * 1024)
                    dv = dst.v(0, 128, base + (wc % 2) * 512, 512)
                    P.act(dv, ps, AF.Copy, reads=[psR], writes=[dst.R],
                          tag=f"fev{i}_{q4}_{wc}_{tt}")
            for qk in range(2):
                src = qtm if qk == 0 else ktm
                dstfm = qfm if qk == 0 else kfm
                fb = ftab.v
                f1c = 0 if qk == 0 else 2048
                f2c = 1024 if qk == 0 else 3072
                for tt in range(4):
                    tg = q4 * 4 + tt
                    tpar = (qk * 4 + tt) % 2
                    teng = "vector" if tpar == 0 else "gpsimd"
                    rAv, rBv = rA[tpar], rB[tpar]
                    sb0 = 704 + 96 * tpar
                    t3 = lambda ap: ap.rearrange("p (a c) -> p a c", c=64)
                    sv = src.v(0, 128, tt * 1024, 1024)
                    P.act(rAv.v(), sv, AF.Square, reads=[src.R],
                          writes=[rAv.R], tag=f"rsq{i}_{q4}_{qk}_{tt}")
                    P.red(sfa[:, sb0:sb0 + 16].rearrange("p (a c) -> p a c", c=1),
                          t3(rAv.v()), ALU.add, reads=[rAv.R],
                          writes=[self.sfRp[tpar]],
                          tag=f"rss{i}_{q4}_{qk}_{tt}")
                    P.act(sfa[:, sb0 + 16:sb0 + 32], sfa[:, sb0:sb0 + 16],
                          AF.Sqrt,
                          reads=[self.sfRp[tpar]], writes=[self.sfRp[tpar]],
                          tag=f"rst{i}_{q4}_{qk}_{tt}")
                    P.ts(sfa[:, sb0 + 32:sb0 + 48],
                         sfa[:, sb0 + 16:sb0 + 32], 1e-12, None,
                         op0=ALU.max, reads=[self.sfRp[tpar]],
                         writes=[self.sfRp[tpar]],
                         tag=f"rmx{i}_{q4}_{qk}_{tt}")
                    P.recip(sfa[:, sb0 + 48:sb0 + 64],
                            sfa[:, sb0 + 32:sb0 + 48],
                            reads=[self.sfRp[tpar]],
                            writes=[self.sfRp[tpar]],
                            tag=f"rrc{i}_{q4}_{qk}_{tt}")
                    if qk == 0:
                        P.ts(sfa[:, sb0 + 64:sb0 + 80],
                             sfa[:, sb0 + 48:sb0 + 64], 8.0, None,
                             reads=[self.sfRp[tpar]],
                             writes=[self.sfRp[tpar]],
                             tag=f"rq8{i}_{q4}_{tt}")
                    P.tt(rAv.v().rearrange("p (a c) -> p a c", c=64),
                         t3(sv),
                         ftab.v(0, 128, f1c + tg * 64, 64).rearrange(
                             "p (a c) -> p a c", a=1).broadcast_to(
                                 (128, 16, 64)),
                         ALU.mult, engine=teng,
                         reads=[src.R, ftab.R], writes=[rAv.R],
                         tag=f"rA{i}_{q4}_{qk}_{tt}")
                    sv3 = src.v(0, 128, tt * 1024, 1024).rearrange(
                        "p (a h c) -> p a h c", h=2, c=32)
                    rB3 = rBv.v().rearrange("p (a h c) -> p a h c", h=2, c=32)
                    P.tt(rB3[:, :, 0, :], sv3[:, :, 1, :],
                         ftab.v(0, 128, f2c + tg * 64, 32).rearrange(
                             "p (a c) -> p a c", a=1).broadcast_to(
                                 (128, 16, 32)),
                         ALU.mult, engine=teng,
                         reads=[src.R, ftab.R], writes=[rBv.R],
                         tag=f"rB1{i}_{q4}_{qk}_{tt}")
                    P.tt(rB3[:, :, 1, :], sv3[:, :, 0, :],
                         ftab.v(0, 128, f2c + tg * 64 + 32, 32).rearrange(
                             "p (a c) -> p a c", a=1).broadcast_to(
                                 (128, 16, 32)),
                         ALU.mult, engine=teng,
                         reads=[src.R, ftab.R], writes=[rBv.R],
                         tag=f"rB2{i}_{q4}_{qk}_{tt}")
                    P.tt(rAv.v(), rAv.v(), rBv.v(), ALU.add, engine=teng,
                         reads=[rAv.R, rBv.R], writes=[rAv.R],
                         tag=f"rC{i}_{q4}_{qk}_{tt}")
                    rncol = (sb0 + 64) if qk == 0 else (sb0 + 48)
                    P.tt(rBv.v().rearrange("p (a c) -> p a c", c=64),
                         t3(rAv.v()),
                         sfa[:, rncol:rncol + 16].rearrange(
                             "p (a c) -> p a c", c=1).broadcast_to(
                                 (128, 16, 64)),
                         ALU.mult, engine=teng,
                         reads=[rAv.R, self.sfRp[tpar]],
                         writes=[rBv.R],
                         tag=f"rD{i}_{q4}_{qk}_{tt}")
                    tbankp = self.tb[tpar].ap()
                    for f in range(NG):
                        tb_ap = tbankp[:, (f % 8) * 128:(f % 8) * 128 + 128]
                        P.trans(tb_ap, rBv.v(0, 128, f * 128, 128),
                                c["ident_bf"][:, :],
                                reads=[rBv.R, self.cR],
                                writes=[self.tbR[tpar]],
                                tag=f"rT{i}_{q4}_{qk}_{tt}_{f}")
                        dcol = (f * 512 + tt * 128 if qk == 0
                                else f * 768 + 256 + tt * 128)
                        P.copy(dstfm.v(0, 128, dcol, 128), tb_ap,
                               reads=[self.tbR[tpar]], writes=[dstfm.R],
                               tag=f"rTe{i}_{q4}_{qk}_{tt}_{f}")
            qtm.free()
            ktm.free()
            atout = self.ab.alloc(NG * 512, f"ato{i}_{q4}")
            pn = [self.ab.alloc(1024, f"pn{i}_{q4}_{j}") for j in range(2)]
            pT = [self.ab.alloc(4 * 256, f"pT{i}_{q4}_{j}") for j in range(2)]
            insts = [(h, wl) for h in range(HEADS) for wl in range(2)]

            def emit_A(j):
                h, wl = insts[j]
                f = h // 2
                pb_ = (h % 2) * 64
                w = q4 * 2 + wl
                W2_ = j % 2
                pnW = pn[W2_]
                bk1a, bk1b = 2 * W2_, 2 * W2_ + 1
                kw = 256 if w == 0 else 512
                kcol = f * 768 + (256 if w == 0 else wl * 256)
                qcol = f * 512 + wl * 256
                nmc = 880 + 4 * W2_
                rdc = 888 + 2 * W2_
                aeng = "vector" if W2_ == 0 else "gpsimd"
                for s in range(2):
                    bk = bk1a if s == 0 else bk1b
                    P.mm(self.pb[bk].ap()[:, 0:kw],
                         qfm.v(pb_, 64, qcol + s * 128, 128),
                         kfm.v(pb_, 64, kcol, kw), True, True,
                         reads=[qfm.R, kfm.R],
                         writes=[self.pbR[bk]],
                         tag=f"m1{i}_{w}_{h}_{s}")
                    mask = (c[f"mask_s{s}"][:, 256:512] if w == 0
                            else c[f"mask_s{s}"][:, 0:kw])
                    P.tt(self.pb[bk].ap()[:, 0:kw],
                         self.pb[bk].ap()[:, 0:kw],
                         mask, ALU.add,
                         reads=[self.pbR[bk], self.cR],
                         writes=[self.pbR[bk]], tag=f"mk{i}_{w}_{h}_{s}")
                    P.red(sfa[:, nmc + s:nmc + s + 1],
                          self.pb[bk].ap()[:, 0:kw],
                          ALU.max, negate=True,
                          reads=[self.pbR[bk]],
                          writes=[self.sfRp[2 + W2_]],
                          tag=f"nm{i}_{w}_{h}_{s}")
                    P.act(pnW.v(0, 128, s * 512, kw),
                          self.pb[bk].ap()[:, 0:kw], AF.Exp,
                          bias=sfa[:, nmc + s:nmc + s + 1],
                          accum=sfa[:, nmc + 2 + s:nmc + 3 + s],
                          reads=[self.pbR[bk], self.sfRp[2 + W2_]],
                          writes=[pnW.R, self.sfRp[2 + W2_]],
                          tag=f"pe{i}_{w}_{h}_{s}")
                    P.recip(sfa[:, rdc + s:rdc + s + 1],
                            sfa[:, nmc + 2 + s:nmc + 3 + s],
                            reads=[self.sfRp[2 + W2_]],
                            writes=[self.sfRp[2 + W2_]],
                            tag=f"rd{i}_{w}_{h}_{s}")
                    P.tt(pnW.v(0, 128, s * 512, kw),
                         pnW.v(0, 128, s * 512, kw),
                         sfa[:, rdc + s:rdc + s + 1].broadcast_to(
                             (128, kw)),
                         ALU.mult, engine=aeng,
                         reads=[pnW.R, self.sfRp[2 + W2_]],
                         writes=[pnW.R], tag=f"pw{i}_{w}_{h}_{s}")

            def emit_B1(j):
                h, wl = insts[j]
                w = q4 * 2 + wl
                W2_ = j % 2
                pnW, pTW = pn[W2_], pT[W2_]
                nk = 2 if w == 0 else 4
                tbk = self.tb[W2_].ap()
                for s in range(2):
                    for ck in range(nk):
                        P.trans(tbk[:, ck * 256 + s * 128:
                                    ck * 256 + s * 128 + 128],
                                pnW.v(0, 128, s * 512 + ck * 128, 128),
                                c["ident_bf"][:, :],
                                reads=[pnW.R, self.cR],
                                writes=[self.tbR[W2_]],
                                tag=f"pt{i}_{w}_{h}_{s}_{ck}")
                P.act(pTW.v(0, 128, 0, nk * 256),
                      tbk[:, 0:nk * 256], AF.Copy,
                      reads=[self.tbR[W2_]], writes=[pTW.R],
                      tag=f"pv{i}_{w}_{h}")

            def emit_B2(j):
                h, wl = insts[j]
                f = h // 2
                pb_ = (h % 2) * 64
                w = q4 * 2 + wl
                W2_ = j % 2
                pTW = pT[W2_]
                bkv = 4 + W2_
                nk = 2 if w == 0 else 4
                for ck in range(nk):
                    P.mm(self.pb[bkv].ap()[pb_:pb_ + 64, 0:256],
                         vtm.v(0, 128, (wl * 2 + ck + (2 if w == 0 else 0))
                               * 1024 + h * 64, 64),
                         pTW.v(0, 128, ck * 256, 256),
                         ck == 0, ck == nk - 1,
                         reads=[vtm.R, pTW.R], writes=[self.pbR[bkv]],
                         tag=f"av{i}_{w}_{h}_{ck}")
                P.act(atout.v(pb_, 64, f * 512 + wl * 256, 256),
                      self.pb[bkv].ap()[pb_:pb_ + 64, 0:256],
                      AF.Copy, reads=[self.pbR[bkv]], writes=[atout.R],
                      tag=f"ao{i}_{w}_{h}")

            emit_A(0)
            emit_A(1)
            for j in range(len(insts)):
                emit_B1(j)
                if j + 2 < len(insts):
                    emit_A(j + 2)
                emit_B2(j)
            for reg in pn + pT:
                reg.free()
            wstg = self.ar.alloc(NG * 128, f"wos{i}_{q4}")
            wob = self.ab.alloc(NG * 128, f"wobb{i}_{q4}")
            for mo in range(NG):
                P.dma("sync",
                      wstg.v(0, 128, 0, 1024).rearrange(
                          "p (g m) -> p g m", m=128),
                      hs["wo"][i][:, mo * 128:(mo + 1) * 128].rearrange(
                          "(g p) m -> p g m", p=128).bitcast(F32R),
                      writes=[wstg.R], tag=f"wo{i}_{q4}_{mo}")
                P.copy(wob.v(), wstg.v().bitcast(F32), reads=[wstg.R],
                       writes=[wob.R], tag=f"woc{i}_{q4}_{mo}")
                ps, psR = self.pb[0].ap()[:, :], self.pbR[0]
                for k in range(NG):
                    P.mm(ps, wob.v(0, 128, k * 128, 128),
                         atout.v(0, 128, k * 512, 512), k == 0, k == NG - 1,
                         reads=[wob.R, atout.R], writes=[psR],
                         tag=f"woy{i}_{q4}_{mo}_{k}")
                P.tt(self.xi_data(mo, t0, 512), ps,
                     self.xi_data(mo, t0, 512).bitcast(F32), ALU.add,
                     reads=[psR, self.xiR[mo]], writes=[self.xiR[mo]],
                     tag=f"wor{i}_{q4}_{mo}")
            for reg in (atout, wstg, wob):
                reg.free()
        for reg in [vtm, qfm, kfm] + rA + rB + list(wst):
            reg.free()

    def _l0f_ffn(self, i, hs, pl):
        P, c = self.P, self.c
        NM1 = 22
        w1s = [self.ar.alloc(NG * 128, f"w1s{i}_{j}") for j in range(4)]

        def w1_load(m):
            wdt = 128 if m < NM1 - 1 else DI - 128 * (NM1 - 1)
            sA, sG = w1s[2 * (m % 2)], w1s[2 * (m % 2) + 1]
            for (st_, cofs) in ((sA, m * 128), (sG, DI + m * 128)):
                P.dma("sync",
                      st_.v(0, 128, 0, NG * 128).rearrange(
                          "p (g f) -> p g f", f=128)[:, :, 0:wdt],
                      hs["w1"][i][:, cofs:cofs + wdt].rearrange(
                          "(g p) f -> p g f", p=128).bitcast(F32R),
                      writes=[st_.R], tag=f"w1l{i}_{m}")
        w1_load(0)
        w1_load(1)
        w2s = self.ar.alloc(2816, f"w2s{i}")
        w2bb = self.ab.alloc(2816, f"w2bb{i}")

        def w2_stage(mo):
            P.dma("gpsimd",
                  w2s.v(0, 128, 0, 2816).rearrange("p (g f) -> p g f", f=128),
                  hs["w2"][i][:, mo * 128:(mo + 1) * 128].rearrange(
                      "(g p) f -> p g f", p=128).bitcast(F32R),
                  writes=[w2s.R], tag=f"w2l{i}_{mo}")
            P.copy(w2bb.v(), w2s.v().bitcast(F32),
                   reads=[w2s.R], writes=[w2bb.R], engine="gpsimd",
                   tag=f"w2c{i}_{mo}")
            P.dma("gpsimd", self._w2st[mo], w2bb.v(),
                  reads=[w2bb.R], writes=[self._w2stR], tag=f"w2o{i}_{mo}")
        self.xnb = self.ab.alloc(NG * SEQ, f"xnb{i}")
        self.xnbR = self.xnb.subs(f"xnb{i}_", NG)
        tf2 = [self.ab.alloc(512, f"ft2{i}_{j}", dt=F32) for j in range(2)]
        def srcv(g, c0, n):
            return (self.xi_data(g, c0, n) if g < 4
                    else self.xi_shift(g, c0, n))
        def cb(g, ch, tile, tileR, t1, t1R, b_b, bbR):
            t = tf2[g % 2]
            P.tt(t.v(), t1, b_b, ALU.add, reads=[t1R, bbR],
                 writes=[t.R], tag=f"fln2{i}_{ch}_{g}")
            P.ts(self.xnb_view(g, ch * 512, 512), t.v(),
                 pl.v(0, 128, g, 1), pl.v(0, 128, 8 + g, 1),
                 op0=ALU.mult, op1=ALU.add,
                 reads=[t.R, pl.R], writes=[self.xnbR[g]],
                 tag=f"flnx{i}_{ch}_{g}")
        self.emit_ln(self.ResidentSrc(srcv, self.xiR), cb, f"fln{i}")
        for t in tf2:
            t.free()
        w1b = [self.ab.alloc(NG * 128, f"w1b{i}_{j}") for j in range(4)]
        gsc = self.ab.alloc(512, f"gsc{i}", dt=F32)
        glt = [self.ab.alloc(512, f"glt{i}_{j}") for j in range(2)]
        for m in range(NM1):
            wdt = 128 if m < NM1 - 1 else DI - 128 * (NM1 - 1)
            sA, sG = w1s[2 * (m % 2)], w1s[2 * (m % 2) + 1]
            bA, bG = w1b[2 * (m % 2)], w1b[2 * (m % 2) + 1]
            P.copy(bA.v(), sA.v().bitcast(F32), reads=[sA.R], writes=[bA.R],
                   engine="gpsimd", tag=f"w1cA{i}_{m}")
            P.copy(bG.v(), sG.v().bitcast(F32), reads=[sG.R], writes=[bG.R],
                   engine="gpsimd", tag=f"w1cG{i}_{m}")
            if m + 2 < NM1:
                w1_load(m + 2)
            if m % 2 == 1 and 3 <= m < 19:
                w2_stage((m - 3) // 2)
            for ch in range(NCH):
                psA, psAR = self.pb[ch % 2].ap(), self.pbR[ch % 2]
                psG, psGR = self.pb[2 + ch % 2].ap(), self.pbR[2 + ch % 2]
                for k in range(NG):
                    P.mm(psA[0:wdt, :], bA.v(0, 128, k * 128, wdt),
                         self.xnb_view(k, ch * 512, 512), k == 0, k == NG - 1,
                         reads=[bA.R, self.xnbR[k]], writes=[psAR],
                         tag=f"w1A{i}_{m}_{ch}_{k}")
                for k in range(NG):
                    P.mm(psG[0:wdt, :], bG.v(0, 128, k * 128, wdt),
                         self.xnb_view(k, ch * 512, 512), k == 0, k == NG - 1,
                         reads=[bG.R, self.xnbR[k]], writes=[psGR],
                         tag=f"w1G{i}_{m}_{ch}_{k}")
                P.act(gsc.v(0, wdt), psG[0:wdt, :], AF.Gelu,
                      bias=pl.v(0, wdt, 38 + m, 1),
                      reads=[psGR, pl.R], writes=[gsc.R],
                      tag=f"gel{i}_{m}_{ch}")
                g2 = glt[ch % 2]
                P.stt(g2.v(0, wdt), psA[0:wdt, :], pl.v(0, wdt, 16 + m, 1),
                      gsc.v(0, wdt), ALU.add, ALU.mult,
                      reads=[psAR, gsc.R, pl.R], writes=[g2.R],
                      tag=f"glu{i}_{m}_{ch}")
                P.dma("gpsimd",
                      self._glu3[0:wdt, m, ch * 512:(ch + 1) * 512],
                      g2.v(0, wdt), reads=[g2.R], writes=[self._gluR],
                      tag=f"gls{i}_{m}_{ch}")
        for reg in w1s + w1b + [gsc] + glt + [w2s, w2bb]:
            reg.free()
        self.xnb.free()
        w2f = [self.ab.alloc(2816, f"w2f{i}_{j}") for j in range(2)]
        gch = [self.ab.alloc(NM1 * 512, f"gch{i}_{j}") for j in range(2)]

        def gch_load(ch):
            g = gch[ch % 2]
            P.dma("gpsimd",
                  g.v(0, 128, 0, NM1 * 512).rearrange(
                      "p (g t) -> p g t", t=512),
                  self._glu3[:, 0:NM1, ch * 512:(ch + 1) * 512],
                  reads=[self._gluR], writes=[g.R], tag=f"w2g{i}_{ch}")

        def w2f_load(ch, mo):
            f = w2f[mo % 2]
            P.dma("scalar", f.v(), self._w2st[mo],
                  reads=[self._w2stR], writes=[f.R], tag=f"w2fl{i}_{ch}_{mo}")
        gch_load(0)
        gch_load(1)
        w2f_load(0, 0)
        for ch in range(NCH):
            g = gch[ch % 2]
            for mo in range(NG):
                f = w2f[mo % 2]
                if mo + 1 < NG:
                    w2f_load(ch, mo + 1)
                elif ch + 1 < NCH:
                    w2f_load(ch + 1, 0)
                ps, psR = self.pb[4 + mo % 2].ap(), self.pbR[4 + mo % 2]
                for k in range(NM1):
                    kw = 128 if k < NM1 - 1 else DI - 128 * (NM1 - 1)
                    P.mm(ps[:, :], f.v(0, kw, k * 128, 128),
                         g.v(0, kw, k * 512, 512), k == 0, k == NM1 - 1,
                         reads=[f.R, g.R], writes=[psR],
                         tag=f"w2m{i}_{ch}_{mo}_{k}")
                P.stt(self.xi_data(mo, ch * 512, 512), ps[:, :],
                      pl.v(0, 128, 60 + mo, 1),
                      self.xi_data(mo, ch * 512, 512).bitcast(F32),
                      ALU.add, ALU.add,
                      reads=[psR, pl.R, self.xiR[mo]], writes=[self.xiR[mo]],
                      tag=f"w2r{i}_{ch}_{mo}")
            if ch + 2 < NCH:
                gch_load(ch + 2)
        for reg in w2f + gch:
            reg.free()

    def phase_l0attn(self, wqkvT_h, woutT_h, attnC_h):
        P, c = self.P, self.c
        NQ = 4
        aC3 = attnC_h.rearrange("(g p) t -> p g t", p=128)
        aCR = R("attnC")
        Sp = [self.ab.alloc(8 * 64, f"l0S{i}", dt=F32) for i in range(2)]
        Zp = [self.ab.alloc(8, f"l0Z{i}", dt=F32) for i in range(2)]
        Sbf = self.ab.alloc(8 * 64, "l0Sbf")
        Zbf = self.ab.alloc(8, "l0Zbf")
        wst = [self.ar.alloc(NG * 256, f"l0wst{i}") for i in range(2)]
        sc_f = [self.ab.alloc(512, f"l0el{i}", dt=F32) for i in range(2)]
        for q4 in range(NQ):
            t0 = q4 * 512
            qtm = self.ab.alloc(4 * 1024, "l0qtm")
            ktm = self.ab.alloc(4 * 1024, "l0ktm")
            vtm = self.ab.alloc(4 * 1024, "l0vtm")
            for wc in range(12):
                w = wst[wc % 2]
                P.dma("sync",
                      w.v(0, 128, 0, NG * 256).rearrange(
                          "p (g f) -> p g f", f=256),
                      wqkvT_h[:, wc * 256:(wc + 1) * 256].rearrange(
                          "(g p) f -> p g f", p=128).bitcast(F32R),
                      writes=[w.R], tag=f"l0w{q4}_{wc}")
                for tt in range(4):
                    ps0 = self.pb[tt % 2].ap()[:, (wc % 2) * 256:
                                               (wc % 2) * 256 + 256]
                    psR = self.pbR[tt % 2]
                    for k in range(NG):
                        P.mm(ps0, self.xi_data(k, t0 + tt * 128, 128),
                             w.v(0, 128, k * 256, 256), k == 0, k == NG - 1,
                             reads=[self.xiR[k], w.R], writes=[psR],
                             tag=f"l0qkv{q4}_{wc}_{tt}_{k}")
                    dst = (qtm, ktm, vtm)[wc // 4]
                    dv = dst.v(0, 128, tt * 1024 + (wc % 4) * 256, 256)
                    if wc < 8:
                        s1, s2 = sc_f[0], sc_f[1]
                        P.ts(s1.v(0, 128, 0, 256), ps0, 0.0, None,
                             op0=ALU.min,
                             reads=[psR], writes=[s1.R],
                             tag=f"l0e1{q4}_{wc}_{tt}")
                        P.act(s2.v(0, 128, 0, 256), s1.v(0, 128, 0, 256),
                              AF.Exp, reads=[s1.R],
                              writes=[s2.R], tag=f"l0e2{q4}_{wc}_{tt}")
                        P.stt(dv, ps0, 0.0, s2.v(0, 128, 0, 256),
                              ALU.max, ALU.add,
                              reads=[psR, s2.R], writes=[dst.R],
                              tag=f"l0e3{q4}_{wc}_{tt}")
                    else:
                        P.act(dv, ps0, AF.Copy, reads=[psR], writes=[dst.R],
                              tag=f"l0ev{q4}_{wc}_{tt}")
            qfm = self.ab.alloc(NG * 512, "l0qfm")
            kfm = self.ab.alloc(NG * 512, "l0kfm")
            tbank = self.tb[0].ap()
            for src, dst in ((qtm, qfm), (ktm, kfm)):
                for f in range(NG):
                    for tt in range(4):
                        tb_ap = tbank[:, (tt % 4) * 128:(tt % 4) * 128 + 128]
                        P.trans(tb_ap,
                                src.v(0, 128, tt * 1024 + f * 128, 128),
                                c["ident_bf"][:, :],
                                reads=[src.R, self.cR], writes=[self.tbR[0]],
                                tag=f"l0T{q4}_{f}_{tt}")
                        P.copy(dst.v(0, 128, f * 512 + tt * 128, 128), tb_ap,
                               reads=[self.tbR[0]], writes=[dst.R],
                               tag=f"l0Te{q4}_{f}_{tt}")
            qtm.free()
            atn = self.ab.alloc(NG * 512, "l0atn")
            asc = [self.ab.alloc(512, f"l0asc{j}") for j in range(2)]
            atT = [self.ab.alloc(512, f"l0atT{j}") for j in range(2)]
            sfa0 = self.sf.ap()
            rdi = [sfa0[0:1, 0:256], sfa0[0:1, 512:768]]
            rdo = [sfa0[0:1, 256:512], sfa0[0:1, 768:1024]]
            rb = [self.ab.alloc(256, f"l0rb{j}", dt=F32) for j in range(2)]
            for h in range(HEADS):
                f = h // 2
                pb_ = (h % 2) * 64
                scol = (h // 2) * 64
                for cc in range(2):
                    gc = q4 * 2 + cc
                    W2_ = (h * 2 + cc) % 2
                    BSC, BND, BRS = W2_, 2 + W2_, 4 + W2_
                    ascW, atTW, rbW = asc[W2_], atT[W2_], rb[W2_]
                    rdiW, rdoW = rdi[W2_], rdo[W2_]
                    qv = lambda s_, n_: qfm.v(pb_, 64,
                                              f * 512 + cc * 256 + s_, n_)
                    kv = kfm.v(pb_, 64, f * 512 + cc * 256, 256)
                    for s in range(2):
                        P.mm(self.pb[BSC].ap()[:, s * 256:(s + 1) * 256],
                             qv(s * 128, 128), kv, True, True,
                             reads=[qfm.R, kfm.R], writes=[self.pbR[BSC]],
                             tag=f"l0sc{q4}_{h}_{cc}_{s}")
                    for s in range(2):
                        P.tt(ascW.v(0, 128, s * 256, 256),
                             self.pb[BSC].ap()[:, s * 256:(s + 1) * 256],
                             c[f"cmask_s{s}"][:, :], ALU.mult,
                             reads=[self.pbR[BSC], self.cR], writes=[ascW.R],
                             tag=f"l0ms{q4}_{h}_{cc}_{s}")
                    tbankp = self.tb[W2_].ap()
                    for s in range(2):
                        for ks in range(2):
                            tb_ap = tbankp[:, 512 + ks * 256 + s * 128:
                                           512 + ks * 256 + s * 128 + 128]
                            P.trans(tb_ap,
                                    ascW.v(0, 128, s * 256 + ks * 128, 128),
                                    c["ident_bf"][:, :],
                                    reads=[ascW.R, self.cR],
                                    writes=[self.tbR[W2_]],
                                    tag=f"l0aT{q4}_{h}_{cc}_{s}{ks}")
                            P.copy(atTW.v(0, 128, ks * 256 + s * 128, 128),
                                   tb_ap, reads=[self.tbR[W2_]],
                                   writes=[atTW.R],
                                   tag=f"l0aTe{q4}_{h}_{cc}_{s}{ks}")
                    nmm = 3 if gc > 0 else 2
                    for ks in range(2):
                        P.mm(self.pb[BND].ap()[pb_:pb_ + 64, 0:256],
                             vtm.v(0, 128, (2 * cc + ks) * 1024 + h * 64, 64),
                             atTW.v(0, 128, ks * 256, 256),
                             ks == 0, ks == nmm - 1 and gc == 0,
                             reads=[vtm.R, atTW.R], writes=[self.pbR[BND]],
                             tag=f"l0nm{q4}_{h}_{cc}_{ks}")
                    if gc > 0:
                        P.mm(self.pb[BND].ap()[pb_:pb_ + 64, 0:256],
                             Sbf.v(pb_, 64, scol, 64),
                             qv(0, 256), False, True,
                             reads=[Sbf.R, qfm.R], writes=[self.pbR[BND]],
                             tag=f"l0nS{q4}_{h}_{cc}")
                    for ks in range(2):
                        P.mm(self.pb[BND].ap()[0:1, 256:512],
                             c["ones_bf_col"][:, :],
                             atTW.v(0, 128, ks * 256, 256),
                             ks == 0, ks == nmm - 1 and gc == 0,
                             reads=[atTW.R, self.cR], writes=[self.pbR[BND]],
                             tag=f"l0dn{q4}_{h}_{cc}_{ks}")
                    if gc > 0:
                        P.mm(self.pb[BND].ap()[0:1, 256:512],
                             Zbf.v(pb_, 64, h // 2, 1),
                             qv(0, 256), False, True,
                             reads=[Zbf.R, qfm.R], writes=[self.pbR[BND]],
                             tag=f"l0dZ{q4}_{h}_{cc}")
                    P.ts(rdiW, self.pb[BND].ap()[0:1, 256:512], 1e-6, None,
                         op0=ALU.max, reads=[self.pbR[BND]],
                         writes=[self.sfRp[W2_]],
                         tag=f"l0rm{q4}_{h}_{cc}")
                    P.recip(rdoW, rdiW, reads=[self.sfRp[W2_]],
                            writes=[self.sfRp[W2_]],
                            tag=f"l0rr{q4}_{h}_{cc}")
                    P.mm(self.pb[BRS].ap()[pb_:pb_ + 64, 0:256],
                         c["ones_f32_row64"][:, :], rdoW, True, True,
                         reads=[self.sfRp[W2_], self.cR],
                         writes=[self.pbR[BRS]],
                         tag=f"l0rb{q4}_{h}_{cc}")
                    P.act(rbW.v(pb_, 64),
                          self.pb[BRS].ap()[pb_:pb_ + 64, 0:256],
                          AF.Copy,
                          reads=[self.pbR[BRS]], writes=[rbW.R],
                          tag=f"l0rbe{q4}_{h}_{cc}")
                    P.tt(atn.v(pb_, 64, f * 512 + cc * 256, 256),
                         self.pb[BND].ap()[pb_:pb_ + 64, 0:256],
                         rbW.v(pb_, 64), ALU.mult,
                         reads=[self.pbR[BND], rbW.R], writes=[atn.R],
                         tag=f"l0o{q4}_{h}_{cc}")
                    for ks in range(2):
                        P.mm(self.pb[BRS].ap()[pb_:pb_ + 64, 256:320],
                             ktm.v(0, 128, (2 * cc + ks) * 1024 + h * 64, 64),
                             vtm.v(0, 128, (2 * cc + ks) * 1024 + h * 64, 64),
                             ks == 0, ks == 1,
                             reads=[ktm.R, vtm.R], writes=[self.pbR[BRS]],
                             tag=f"l0Su{q4}_{h}_{cc}_{ks}")
                    for ks in range(2):
                        P.mm(self.pb[BRS].ap()[pb_:pb_ + 64, 320:321],
                             ktm.v(0, 128, (2 * cc + ks) * 1024 + h * 64, 64),
                             c["ones_bf_col"][:, :], ks == 0, ks == 1,
                             reads=[ktm.R, self.cR], writes=[self.pbR[BRS]],
                             tag=f"l0Zu{q4}_{h}_{cc}_{ks}")
                    Sr, Sw = Sp[gc % 2], Sp[1 - gc % 2]
                    Zr, Zw = Zp[gc % 2], Zp[1 - gc % 2]
                    if gc == 0:
                        P.copy(Sw.v(pb_, 64, scol, 64),
                               self.pb[BRS].ap()[pb_:pb_ + 64, 256:320],
                               reads=[self.pbR[BRS]], writes=[Sw.R],
                               tag=f"l0Sc{q4}_{h}_{cc}")
                        P.copy(Zw.v(pb_, 64, h // 2, 1),
                               self.pb[BRS].ap()[pb_:pb_ + 64, 320:321],
                               reads=[self.pbR[BRS]], writes=[Zw.R],
                               tag=f"l0Zc{q4}_{h}_{cc}")
                    else:
                        P.tt(Sw.v(pb_, 64, scol, 64),
                             self.pb[BRS].ap()[pb_:pb_ + 64, 256:320],
                             Sr.v(pb_, 64, scol, 64), ALU.add,
                             reads=[self.pbR[BRS], Sr.R], writes=[Sw.R],
                             tag=f"l0Sa{q4}_{h}_{cc}")
                        P.tt(Zw.v(pb_, 64, h // 2, 1),
                             self.pb[BRS].ap()[pb_:pb_ + 64, 320:321],
                             Zr.v(pb_, 64, h // 2, 1), ALU.add,
                             reads=[self.pbR[BRS], Zr.R], writes=[Zw.R],
                             tag=f"l0Za{q4}_{h}_{cc}")
                    P.act(Sbf.v(pb_, 64, scol, 64), Sw.v(pb_, 64, scol, 64),
                          AF.Copy, scale=0.125, reads=[Sw.R], writes=[Sbf.R],
                          tag=f"l0Sb{q4}_{h}_{cc}")
                    P.act(Zbf.v(pb_, 64, h // 2, 1), Zw.v(pb_, 64, h // 2, 1),
                          AF.Copy, scale=0.125, reads=[Zw.R], writes=[Zbf.R],
                          tag=f"l0Zb{q4}_{h}_{cc}")
            wstg = self.ar.alloc(NG * 128, "l0wostg")
            wob = self.ab.alloc(NG * 128, "l0wob")
            wsc = self.ab.alloc(512, "l0wsc", dt=F32)
            for mo in range(NG):
                P.dma("sync",
                      wstg.v(0, 128, 0, 1024).rearrange(
                          "p (g m) -> p g m", m=128),
                      woutT_h[:, mo * 128:(mo + 1) * 128].rearrange(
                          "(g p) m -> p g m", p=128).bitcast(F32R),
                      writes=[wstg.R], tag=f"l0wo{q4}_{mo}")
                P.copy(wob.v(), wstg.v().bitcast(F32), reads=[wstg.R],
                       writes=[wob.R], tag=f"l0woc{q4}_{mo}")
                ps, psR = self.pb[0].ap()[:, :], self.pbR[0]
                for k in range(NG):
                    P.mm(ps, wob.v(0, 128, k * 128, 128),
                         atn.v(0, 128, k * 512, 512), k == 0, k == NG - 1,
                         reads=[wob.R, atn.R], writes=[psR],
                         tag=f"l0y{q4}_{mo}_{k}")
                P.act(wsc.v(), ps, AF.Copy, reads=[psR], writes=[wsc.R],
                      tag=f"l0ye{q4}_{mo}")
                P.dma("sync", aC3[:, mo, t0:t0 + 512], wsc.v(),
                      reads=[wsc.R], writes=[aCR], tag=f"l0ys{q4}_{mo}")
            for reg in [ktm, vtm, qfm, kfm, atn, wstg, wob, wsc] + asc + \
                    atT + rb:
                reg.free()
        for reg in Sp + Zp + [Sbf, Zbf] + wst + sc_f:
            reg.free()
        return aCR

    def phase_l1attn(self, ap_arr, bwT_h, cwT_h):
        P, c = self.P, self.c
        NJ = 4
        stg = self.ar.alloc(NG * 512, "l1bwstg")
        bwb = self.ab.alloc(NG * 512, "l1bwb")
        P.dma("sync", stg.v(0, 128, 0, NG * 512).rearrange(
                  "p (g n) -> p g n", n=512),
              bwT_h.rearrange("(g p) n -> p g n", p=128).bitcast(F32R),
              writes=[stg.R], tag="bwld")
        P.copy(bwb.v(), stg.v().bitcast(F32), reads=[stg.R], writes=[bwb.R],
               tag="bwcv")
        stg.free()
        arow, arowR = self.small("l1_ap", ap_arr, F32)
        abc = self.ab.alloc(256, "l1abc", dt=F32)
        wsc = self.ab.alloc(256, "l1wsc", dt=F32)
        wbuf = self.ab.alloc(32, "l1wbuf", dt=F32)
        psA, psAR = self.pb[0].ap()[:, 0:256], self.pbR[0]
        P.mm(psA, c["ones_f32_row"][:, :], arow[:, :], True, True,
             reads=[arowR, self.cR], writes=[psAR], tag="l1abc")
        P.act(abc.v(), psA, AF.Copy, reads=[psAR], writes=[abc.R], tag="l1abce")
        for jt in range(NJ):
            P.tt(wsc.v(), c["l1_M"][:, jt * 256:(jt + 1) * 256], abc.v(),
                 ALU.mult, reads=[abc.R, self.cR], writes=[wsc.R],
                 tag=f"l1w{jt}")
            P.red(wbuf.v(0, 128, jt, 1), wsc.v(), ALU.add,
                  reads=[wsc.R], writes=[wbuf.R], tag=f"l1wr{jt}")
        DBG = getattr(self, "dbg_l1a", False)
        sfv = lambda c0, n=4: self.sf.ap()[:, c0:c0 + n]
        sfR = self.sfR
        W, T3, RP, RR, OMN, CTH, STH = 0, 4, 8, 12, 16, 20, 24
        CT, ST, A1, B1, C5, S5 = 28, 29, 30, 31, 32, 33
        CARP, CARQ = 34, 38
        P.copy(sfv(W), wbuf.v(0, 128, 0, 4), reads=[wbuf.R], writes=[sfR],
               tag="l1wcp")
        P.tt(sfv(T3), sfv(W), sfv(W), ALU.mult, reads=[sfR], writes=[sfR],
             tag="l1w2")
        P.ts(sfv(RP), sfv(T3), 1.0, None, op0=ALU.add, reads=[sfR],
             writes=[sfR], tag="l1w3")
        P.recip(sfv(RR), sfv(RP), reads=[sfR], writes=[sfR], tag="l1w4")
        P.ts(sfv(OMN), sfv(T3), -1.0, 1.0, op0=ALU.mult, op1=ALU.add,
             reads=[sfR], writes=[sfR], tag="l1w5")
        P.tt(sfv(CTH), sfv(OMN), sfv(RR), ALU.mult, reads=[sfR], writes=[sfR],
             tag="l1w6")
        P.stt(sfv(STH), sfv(W), -2.0, sfv(RR), ALU.mult, ALU.mult,
              reads=[sfR], writes=[sfR], tag="l1w7")
        if DBG:
            self.debug_out("dbg_wbuf", sfv(0, 48), [sfR])
        h_st = self.dint("l1h_st", (512, SEQ), BF16)
        h3 = h_st.rearrange("(g p) t -> p g t", p=128)
        hstR = R("l1h_st")
        tf = self.tabf.ap()
        tabs = [0, 1, 2, 3]
        tabRs = [R(f"tabf{i}") for i in range(4)]
        scrR = [R("tscr0"), R("tscr1")]
        tv = lambda b, c0, n: tf[:, b * 512 + c0:b * 512 + c0 + n]
        sc1 = lambda c0, n: tf[:, 2048 + c0:2048 + c0 + n]
        sc2 = lambda c0, n: tf[:, 2304 + c0:2304 + c0 + n]
        Pf = self.ab.alloc(512, "l1Pf", dt=F32)
        Qf = self.ab.alloc(512, "l1Qf", dt=F32)
        cumP = self.ab.alloc(512, "l1cumP", dt=F32)
        cumQ = self.ab.alloc(512, "l1cumQ", dt=F32)
        hstg = [self.ab.alloc(512, f"l1hs{i}") for i in range(2)]
        for jt in range(NJ):
            cb_, sb_ = tabs[0], tabs[1]
            cR_, sR_ = tabRs[cb_], tabRs[sb_]
            P.ts(tv(cb_, 0, 1), sfv(CTH, 1), 0.0, 1.0, op0=ALU.mult,
                 op1=ALU.add, reads=[sfR], writes=[cR_], tag=f"l1tc0_{jt}")
            P.ts(tv(sb_, 0, 1), sfv(CTH, 1), 0.0, None, op0=ALU.mult,
                 reads=[sfR], writes=[sR_], tag=f"l1ts0_{jt}")
            P.copy(tv(cb_, 1, 1), sfv(CTH + jt, 1), reads=[sfR], writes=[cR_],
                   tag=f"l1tc1_{jt}")
            P.copy(tv(sb_, 1, 1), sfv(STH + jt, 1), reads=[sfR], writes=[sR_],
                   tag=f"l1ts1_{jt}")
            T = 1
            while T < 512:
                if T == 1:
                    P.copy(sfv(CT, 1), tv(cb_, 1, 1), reads=[cR_],
                           writes=[sfR], tag=f"l1cT{jt}_{T}")
                    P.copy(sfv(ST, 1), tv(sb_, 1, 1), reads=[sR_],
                           writes=[sfR], tag=f"l1sT{jt}_{T}")
                else:
                    P.tt(sfv(A1, 1), tv(cb_, T - 1, 1), tv(cb_, 1, 1),
                         ALU.mult, reads=[cR_], writes=[sfR],
                         tag=f"l1cTa{jt}_{T}")
                    P.tt(sfv(B1, 1), tv(sb_, T - 1, 1), tv(sb_, 1, 1),
                         ALU.mult, reads=[sR_], writes=[sfR],
                         tag=f"l1cTb{jt}_{T}")
                    P.tt(sfv(CT, 1), sfv(A1, 1), sfv(B1, 1), ALU.subtract,
                         reads=[sfR], writes=[sfR], tag=f"l1cTc{jt}_{T}")
                    P.tt(sfv(A1, 1), tv(sb_, T - 1, 1), tv(cb_, 1, 1),
                         ALU.mult, reads=[cR_, sR_], writes=[sfR],
                         tag=f"l1sTa{jt}_{T}")
                    P.tt(sfv(B1, 1), tv(cb_, T - 1, 1), tv(sb_, 1, 1),
                         ALU.mult, reads=[cR_, sR_], writes=[sfR],
                         tag=f"l1sTb{jt}_{T}")
                    P.tt(sfv(ST, 1), sfv(A1, 1), sfv(B1, 1), ALU.add,
                         reads=[sfR], writes=[sfR], tag=f"l1sTc{jt}_{T}")
                P.ts(sc1(0, T), tv(cb_, 0, T), sfv(CT, 1), None, op0=ALU.mult,
                     reads=[cR_, sfR], writes=[scrR[0]], tag=f"l1e1_{jt}_{T}")
                P.ts(sc2(0, T), tv(sb_, 0, T), sfv(ST, 1), None, op0=ALU.mult,
                     reads=[sR_, sfR], writes=[scrR[1]], tag=f"l1e2_{jt}_{T}")
                P.tt(tv(cb_, T, T), sc1(0, T), sc2(0, T), ALU.subtract,
                     reads=[scrR[0], scrR[1]], writes=[cR_],
                     tag=f"l1e3_{jt}_{T}")
                P.ts(sc1(0, T), tv(sb_, 0, T), sfv(CT, 1), None, op0=ALU.mult,
                     reads=[sR_, sfR], writes=[scrR[0]], tag=f"l1e4_{jt}_{T}")
                P.ts(sc2(0, T), tv(cb_, 0, T), sfv(ST, 1), None, op0=ALU.mult,
                     reads=[cR_, sfR], writes=[scrR[1]], tag=f"l1e5_{jt}_{T}")
                P.tt(tv(sb_, T, T), sc1(0, T), sc2(0, T), ALU.add,
                     reads=[scrR[0], scrR[1]], writes=[sR_],
                     tag=f"l1e6_{jt}_{T}")
                T *= 2
            P.tt(sfv(A1, 1), tv(cb_, 511, 1), tv(cb_, 1, 1), ALU.mult,
                 reads=[cR_], writes=[sfR], tag=f"l1r1_{jt}")
            P.tt(sfv(B1, 1), tv(sb_, 511, 1), tv(sb_, 1, 1), ALU.mult,
                 reads=[sR_], writes=[sfR], tag=f"l1r2_{jt}")
            P.tt(sfv(C5, 1), sfv(A1, 1), sfv(B1, 1), ALU.subtract,
                 reads=[sfR], writes=[sfR], tag=f"l1r3_{jt}")
            P.tt(sfv(A1, 1), tv(sb_, 511, 1), tv(cb_, 1, 1), ALU.mult,
                 reads=[cR_, sR_], writes=[sfR], tag=f"l1r4_{jt}")
            P.tt(sfv(B1, 1), tv(cb_, 511, 1), tv(sb_, 1, 1), ALU.mult,
                 reads=[cR_, sR_], writes=[sfR], tag=f"l1r5_{jt}")
            P.tt(sfv(S5, 1), sfv(A1, 1), sfv(B1, 1), ALU.add,
                 reads=[sfR], writes=[sfR], tag=f"l1r6_{jt}")
            if DBG and jt == 0:
                self.debug_out("dbg_cos0", tv(cb_, 0, 512), [cR_])
                self.debug_out("dbg_sin0", tv(sb_, 0, 512), [sR_])
            for ch in range(NCH):
                cb_, sb_ = tabs[0], tabs[1]
                cn_, sn_ = tabs[2], tabs[3]
                cR_, sR_ = tabRs[cb_], tabRs[sb_]
                cnR_, snR_ = tabRs[cn_], tabRs[sn_]
                ps, psR = self.pb[2 + ch % 2].ap()[:, :], self.pbR[2 + ch % 2]
                for k in range(NG):
                    P.mm(ps, bwb.v(0, 128, k * 512 + jt * 128, 128),
                         self.xnb_view(k, ch * 512, 512), k == 0, k == NG - 1,
                         reads=[bwb.R, self.xnbR[k]], writes=[psR],
                         tag=f"l1bu{jt}_{ch}_{k}")
                P.tt(Pf.v(), ps, tv(cb_, 0, 512), ALU.mult,
                     reads=[psR, cR_], writes=[Pf.R], tag=f"l1P{jt}_{ch}")
                P.tt(Qf.v(), ps, tv(sb_, 0, 512), ALU.mult,
                     reads=[psR, sR_], writes=[Qf.R], tag=f"l1Q{jt}_{ch}")
                if DBG and jt == 0 and ch == 0:
                    self.debug_out("dbg_Pf", Pf.v(), [Pf.R])
                initP = 0.0 if ch == 0 else sfv(CARP + jt, 1)
                initQ = 0.0 if ch == 0 else sfv(CARQ + jt, 1)
                P.scan(cumP.v(), c["ones_f32_big"][:, :], Pf.v(), initP,
                       reads=[Pf.R, sfR, self.cR], writes=[cumP.R],
                       tag=f"l1sP{jt}_{ch}")
                P.scan(cumQ.v(), c["ones_f32_big"][:, :], Qf.v(), initQ,
                       reads=[Qf.R, sfR, self.cR], writes=[cumQ.R],
                       tag=f"l1sQ{jt}_{ch}")
                P.copy(sfv(CARP + jt, 1), cumP.v(0, 128, 511, 1),
                       reads=[cumP.R], writes=[sfR], tag=f"l1cP{jt}_{ch}")
                P.copy(sfv(CARQ + jt, 1), cumQ.v(0, 128, 511, 1),
                       reads=[cumQ.R], writes=[sfR], tag=f"l1cQ{jt}_{ch}")
                P.tt(Pf.v(), cumP.v(), tv(cb_, 0, 512), ALU.mult,
                     reads=[cumP.R, cR_], writes=[Pf.R], tag=f"l1h1{jt}_{ch}")
                P.tt(Qf.v(), cumQ.v(), tv(sb_, 0, 512), ALU.mult,
                     reads=[cumQ.R, sR_], writes=[Qf.R], tag=f"l1h2{jt}_{ch}")
                hs = hstg[ch % 2]
                P.tt(hs.v(), Pf.v(), Qf.v(), ALU.add,
                     reads=[Pf.R, Qf.R], writes=[hs.R], tag=f"l1h3{jt}_{ch}")
                P.dma("sync", h3[:, jt, ch * 512:(ch + 1) * 512], hs.v(),
                      reads=[hs.R], writes=[hstR], tag=f"l1hst{jt}_{ch}")
                if ch < NCH - 1:
                    for half in range(2):
                        o = half * 256
                        P.ts(sc1(0, 256), tv(cb_, o, 256), sfv(C5, 1), None,
                             op0=ALU.mult, reads=[cR_, sfR], writes=[scrR[0]],
                             tag=f"l1ra{jt}_{ch}_{half}")
                        P.ts(sc2(0, 256), tv(sb_, o, 256), sfv(S5, 1), None,
                             op0=ALU.mult, reads=[sR_, sfR], writes=[scrR[1]],
                             tag=f"l1rb{jt}_{ch}_{half}")
                        P.tt(tv(cn_, o, 256), sc1(0, 256), sc2(0, 256),
                             ALU.subtract, reads=[scrR[0], scrR[1]],
                             writes=[cnR_], tag=f"l1rc{jt}_{ch}_{half}")
                        P.ts(sc1(0, 256), tv(sb_, o, 256), sfv(C5, 1), None,
                             op0=ALU.mult, reads=[sR_, sfR], writes=[scrR[0]],
                             tag=f"l1rd{jt}_{ch}_{half}")
                        P.ts(sc2(0, 256), tv(cb_, o, 256), sfv(S5, 1), None,
                             op0=ALU.mult, reads=[cR_, sfR], writes=[scrR[1]],
                             tag=f"l1re{jt}_{ch}_{half}")
                        P.tt(tv(sn_, o, 256), sc1(0, 256), sc2(0, 256),
                             ALU.add, reads=[scrR[0], scrR[1]], writes=[snR_],
                             tag=f"l1rf{jt}_{ch}_{half}")
                    tabs[0], tabs[1], tabs[2], tabs[3] = \
                        tabs[2], tabs[3], tabs[0], tabs[1]
        for reg in [Pf, Qf, cumP, cumQ, abc, wsc, wbuf, bwb] + hstg:
            reg.free()
        stg2 = self.ar.alloc(4 * 1024, "l1cwstg")
        cwb = self.ab.alloc(4 * 1024, "l1cwb")
        P.dma("sync", stg2.v(0, 128, 0, 4096).rearrange(
                  "p (g n) -> p g n", n=1024),
              cwT_h.rearrange("(g p) n -> p g n", p=128).bitcast(F32R),
              writes=[stg2.R], tag="cwld")
        P.copy(cwb.v(), stg2.v().bitcast(F32), reads=[stg2.R], writes=[cwb.R],
               tag="cwcv")
        stg2.free()
        hch = [self.ab.alloc(4 * 512, f"l1hch{i}") for i in range(2)]
        for ch in range(NCH):
            hc = hch[ch % 2]
            P.dma("sync", hc.v(0, 128, 0, 2048).rearrange(
                      "p (g t) -> p g t", t=512),
                  h3[:, :, ch * 512:(ch + 1) * 512],
                  reads=[hstR], writes=[hc.R], tag=f"l1hld{ch}")
            for mo in range(NG):
                ps, psR = self.pb[4 + mo % 2].ap()[:, :], self.pbR[4 + mo % 2]
                for k in range(NJ):
                    P.mm(ps, cwb.v(0, 128, k * 1024 + mo * 128, 128),
                         hc.v(0, 128, k * 512, 512), k == 0, k == NJ - 1,
                         reads=[cwb.R, hc.R], writes=[psR],
                         tag=f"l1y{ch}_{mo}_{k}")
                P.tt(self.x_view(mo, ch * 512, 512), ps,
                     self.x_view(mo, ch * 512, 512).bitcast(F32), ALU.add,
                     reads=[psR, self.xR[mo]], writes=[self.xR[mo]],
                     tag=f"l1yev{ch}_{mo}")
        for reg in hch + [cwb]:
            reg.free()

    def phase_l1ffn(self, wgT_h, wuT_h, wdT_h):
        P, c = self.P, self.c
        glu_st = self.dint("l1glu_st", (HID_L1, SEQ), BF16)
        glu3 = glu_st.rearrange("(g p) t -> p g t", p=128)
        wd_bf = self.dint("l1wd_bf", (NG, 128, HID_L1), BF16)
        NM = HID_L1 // 128
        ist = [self.ab.alloc(HID_L1, "wdist0", dt=I32)]
        istR = [ist[0].R]
        wcv = [self.ab.alloc(HID_L1, "wdcv0")]
        wcvR = [wcv[0].R]
        for mo in range(NG):
            P.dma("sync", ist[0].v().rearrange("p (g m) -> p g m", m=128),
                  wdT_h[:, mo * 128:(mo + 1) * 128].rearrange(
                      "(g p) m -> p g m", p=128),
                  writes=[istR[0]], tag=f"wdld{mo}")
            P.copy(wcv[0].v(), ist[0].v(), engine="gpsimd",
                   reads=[istR[0]], writes=[wcvR[0]], tag=f"wdcv{mo}")
            P.dma("sync", wd_bf[mo], wcv[0].v(), reads=[wcvR[0]],
                  tag=f"wdst{mo}")
        for r in ist + wcv:
            r.free()
        xg = [self.ab.alloc(NG * 128, f"l1wg{i}", dt=I32) for i in range(2)]
        xgR = [b.R for b in xg]
        wgb = [self.ab.alloc(NG * 128, f"l1wgb{i}") for i in range(2)]
        wgbR = [b.R for b in wgb]
        xu = [self.ab.alloc(NG * 128, f"l1wu{i}", dt=I32) for i in range(2)]
        xuR = [b.R for b in xu]
        wub = [self.ab.alloc(NG * 128, f"l1wub{i}") for i in range(2)]
        wubR = [b.R for b in wub]
        sg = self.ab.alloc(512, "l1sg")
        sgR = sg.R
        glt = [self.ab.alloc(512, f"l1glt{i}") for i in range(2)]
        gltR = [b.R for b in glt]
        for m in range(NM):
            i2 = m % 2
            P.dma("sync", xg[i2].v().rearrange("p (g m) -> p g m", m=128),
                  wgT_h[:, m * 128:(m + 1) * 128].rearrange(
                      "(g p) m -> p g m", p=128),
                  writes=[xgR[i2]], tag=f"wgld{m}")
            P.copy(wgb[i2].v(), xg[i2].v(), engine="gpsimd",
                   reads=[xgR[i2]], writes=[wgbR[i2]], tag=f"wgcv{m}")
            P.dma("sync", xu[i2].v().rearrange("p (g m) -> p g m", m=128),
                  wuT_h[:, m * 128:(m + 1) * 128].rearrange(
                      "(g p) m -> p g m", p=128),
                  writes=[xuR[i2]], tag=f"wuld{m}")
            P.copy(wub[i2].v(), xu[i2].v(), engine="gpsimd",
                   reads=[xuR[i2]], writes=[wubR[i2]], tag=f"wucv{m}")
            for ch in range(NCH):
                ps0, ps0R = self.pb[ch % 2].ap()[:, :], self.pbR[ch % 2]
                ps1, ps1R = self.pb[2 + ch % 2].ap()[:, :], self.pbR[2 + ch % 2]
                for k in range(NG):
                    P.mm(ps0, wgb[i2].v(0, 128, k * 128, 128),
                         self.xnb_view(k, ch * 512, 512), k == 0, k == NG - 1,
                         reads=[wgbR[i2], self.xnbR[k]], writes=[ps0R],
                         tag=f"l1g{m}_{ch}_{k}")
                for k in range(NG):
                    P.mm(ps1, wub[i2].v(0, 128, k * 128, 128),
                         self.xnb_view(k, ch * 512, 512), k == 0, k == NG - 1,
                         reads=[wubR[i2], self.xnbR[k]], writes=[ps1R],
                         tag=f"l1u{m}_{ch}_{k}")
                P.act(sg.v(), ps0, AF.Silu, scale=INV127,
                      reads=[ps0R], writes=[sgR], tag=f"l1sg{m}_{ch}")
                g2 = glt[ch % 2]
                P.stt(g2.v(), ps1, INV127, sg.v(), ALU.mult, ALU.mult,
                      reads=[ps1R, sgR], writes=[gltR[ch % 2]],
                      tag=f"l1glu{m}_{ch}")
                P.dma("sync", glu3[:, m, ch * 512:(ch + 1) * 512], g2.v(),
                      reads=[gltR[ch % 2]], tag=f"l1gst{m}_{ch}")
        for r in xg + wgb + xu + wub + [sg] + glt:
            r.free()
        self.xnb.free()
        gch = self.ab.alloc(NM * 512, "l1gch")
        gchR = gch.R
        wdb = [self.ab.alloc(HID_L1, f"l1wdb{i}") for i in range(2)]
        wdbR = [b.R for b in wdb]
        for ch in range(NCH):
            P.dma("sync", gch.v(0, 128, 0, NM * 512).rearrange(
                      "p (g t) -> p g t", t=512),
                  glu3[:, :, ch * 512:(ch + 1) * 512],
                  writes=[gchR], tag=f"l1gld{ch}")
            for mo in range(NG):
                i2 = mo % 2
                P.dma("sync", wdb[i2].v(), wd_bf[mo], writes=[wdbR[i2]],
                      tag=f"l1wdb{ch}_{mo}")
                ps, psR = self.pb[4 + mo % 2].ap()[:, :], self.pbR[4 + mo % 2]
                for k in range(NM):
                    P.mm(ps, wdb[i2].v(0, 128, k * 128, 128),
                         gch.v(0, 128, k * 512, 512), k == 0, k == NM - 1,
                         reads=[wdbR[i2], gchR], writes=[psR],
                         tag=f"l1d{ch}_{mo}_{k}")
                P.stt(self.x_view(mo, ch * 512, 512), ps, INV127,
                      self.x_view(mo, ch * 512, 512).bitcast(F32),
                      ALU.mult, ALU.add,
                      reads=[psR, self.xR[mo]], writes=[self.xR[mo]],
                      tag=f"l1dev{ch}_{mo}")
        gch.free()
        for r in wdb:
            r.free()

    def debug_out(self, name, ap, reads, shape=None, dt=F32):
        if shape is None:
            shape = ap.shape
        h = self.dout(name, shape, dt)
        self.P.dma("sync", h, ap if dt != F32 or True else ap, reads=reads,
                   tag="dbg_" + name)

    def small(self, name, arr, dt):
        arr = np.ascontiguousarray(arr)
        self.shared_inputs[name] = arr
        hin = self.din(name, arr.shape, F32 if dt == F32R else
                       ({np.dtype(np.float32): F32,
                         np.dtype(np.int32): I32}[arr.dtype] if dt is None else dt))
        t = self.st.enter_context(
            self.nc.sbuf_tensor("s_" + name, list(arr.shape), dt))
        res = R(name)
        src = hin.bitcast(F32R) if dt == F32R else hin
        self.P.dma("sync", t.ap()[:, :], src, writes=[res], tag="ld_" + name)
        return t.ap(), res

    def store_xi_as(self, dst_ap, tag="store_xi"):
        d = dst_ap.rearrange("(g p) t -> p g t", p=128)
        src = self.xi.v(0, 128, 0, NG * (SEQ + 1)).rearrange(
            "p (g t) -> p g t", t=SEQ + 1)
        return self.P.dma("sync", d, src[:, :, 1:SEQ + 1].bitcast(F32),
                          reads=list(self.xiR), tag=tag)

    def run(self, per_core_inputs, out_names):
        self.P.final_fence()
        self.P.emit(self.st)
        self.st.close()
        in_maps = []
        for cidx in range(N_CORES):
            m = dict(self.shared_inputs)
            m.update(per_core_inputs[cidx])
            in_maps.append(m)
        res = run_bass_kernel_spmd(self.nc, in_maps,
                                   core_ids=list(range(N_CORES)))
        return [{k: r[k] for k in out_names} for r in res.results]



def build_test_l1ffn(g):
    b = Build(["test"]).setup()
    xin = b.din("x_in", (DM, SEQ), F32)
    xout = b.dout("x_out", (DM, SEQ), F32)
    wgT_h = b.win("l1_wgT", g["l1_wgT"])
    wuT_h = b.win("l1_wuT", g["l1_wuT"])
    wdT_h = b.win("l1_wdT", g["l1_wdT"])
    d_ap, dR = b.small("l1_D", g["l1_D"], F32)
    b.alloc_x()
    b.load_x(xin)
    b.phase_ln1(d_ap, dR)
    b.phase_l1ffn(wgT_h, wuT_h, wdT_h)
    b.store_x(xout)
    return b

def _l0ffn_handles(b, g):
    return {
        "wqkv": b.win("wqkv", g["wqkv"]),
        "wo": b.win("wo", g["wo"]),
        "w1": b.win("w1", g["w1"]),
        "w2": b.win("w2", g["w2"]),
        "qs": b.win("qs", g["qs"]),
        "ks": b.win("ks", g["ks"]),
        "lng": b.win("lng", g["ln_g"]),
        "lnb": b.win("lnb", g["ln_b"]),
        "b1a": b.win("b1a", g["b1a"]),
        "b1g": b.win("b1g", g["b1g"]),
        "b2": b.win("b2", g["b2"]),
    }

def build_test_l0ffn(g, nlayers=1):
    b = Build(["l0ffn"]).setup()
    xnin = b.din("xn_in", (DM, SEQ), F32)
    xout = b.dout("x_out", (DM, SEQ), F32)
    hs = _l0ffn_handles(b, g)
    b.alloc_xi()
    s = xnin.rearrange("(g p) t -> p g t", p=128)
    b.P.dma("sync",
            b.xi.v(0, 128, 0, NG * (SEQ + 1)).rearrange(
                "p (g t) -> p g t", t=SEQ + 1)[:, :, 1:SEQ + 1],
            s.bitcast(F32R), writes=list(b.xiR), tag="load_xn")
    glu_st = b.dint("glu_st", (2816, SEQ), BF16)
    b._glu3 = glu_st.rearrange("(g p) t -> p g t", p=128)
    b._gluR = R("glu_st")
    b._w2st = b.dint("w2bf_st", (NG, 128, 2816), BF16)
    b._w2stR = R("w2bf_st")
    for i in range(nlayers):
        pl, qsb, ftab = b._l0f_params(i, hs)
        b._l0f_attn(i, hs, ftab)
        ftab.free()
        qsb.free()
        b._l0f_ffn(i, hs, pl)
        pl.free()
    b.store_xi_as(xout)
    return b

def build_test_l0attn(g):
    b = Build(["l0head"]).setup()
    xnin = b.din("xn_in", (DM, SEQ), F32)
    attnC = b.dout("attnC", (DM, SEQ), F32)
    wq = b.win("l0a_wqkvT", g["l0a_wqkvT"])
    wo = b.win("l0a_woutT", g["l0a_woutT"])
    b.alloc_xi()
    s = xnin.rearrange("(g p) t -> p g t", p=128)
    b.P.dma("sync",
            b.xi.v(0, 128, 0, NG * (SEQ + 1)).rearrange(
                "p (g t) -> p g t", t=SEQ + 1)[:, :, 1:SEQ + 1],
            s.bitcast(F32R), writes=list(b.xiR), tag="load_xn")
    b.phase_l0attn(wq, wo, attnC)
    return b

def build_test_l1attn(g):
    b = Build(["l1attn"]).setup()
    xin = b.din("x_in", (DM, SEQ), F32)
    xout = b.dout("x_out", (DM, SEQ), F32)
    bwT_h = b.win("l1_bwT", g["l1_bwT"])
    cwT_h = b.win("l1_cwT", g["l1_cwT"])
    d_ap, dR = b.small("l1_D", g["l1_D"], F32)
    b.alloc_x()
    b.load_x(xin)
    b.phase_ln1(d_ap, dR)
    b.phase_l1attn(g["l1_ap"], bwT_h, cwT_h)
    b.store_x(xout)
    return b

def build_test_ln():
    b = Build(["test"]).setup()
    xin = b.din("x_in", (DM, SEQ), F32)
    xn_out = b.dout("xn_out", (DM, SEQ), F32)
    b.alloc_xi()
    b.phase_ln0(xin)
    b.store_xi_as(xn_out)
    return b


FUSED = True

def _build_l0head(g):
    b = Build(["l0head"]).setup()
    xin = b.din("x_in", (DM, SEQ), F32)
    xn0o = b.dout("xn0", (DM, SEQ), F32)
    attnCo = b.dout("attnC_o", (DM, SEQ), F32)
    wq = b.win("l0a_wqkvT", g["l0a_wqkvT"])
    wo = b.win("l0a_woutT", g["l0a_woutT"])
    b.alloc_xi()
    b.phase_ln0(xin)
    b.phase_l0attn(wq, wo, attnCo)
    b.store_xi_as(xn0o)
    return b

def _build_l0ffn(g):
    b = Build(["l0ffn"]).setup()
    xnin = b.din("xn0", (DM, SEQ), F32)
    xout = b.dout("xiF", (DM, SEQ), F32)
    hs = _l0ffn_handles(b, g)
    b.alloc_xi()
    s = xnin.rearrange("(g p) t -> p g t", p=128)
    b.P.dma("sync",
            b.xi.v(0, 128, 0, NG * (SEQ + 1)).rearrange(
                "p (g t) -> p g t", t=SEQ + 1)[:, :, 1:SEQ + 1],
            s.bitcast(F32R), writes=list(b.xiR), tag="load_xn")
    b.phase_l0ffn(hs)
    b.store_xi_as(xout)
    return b

def _build_l1(g):
    b = Build(["l1attn"]).setup()
    xin = b.din("x_in", (DM, SEQ), F32)
    aC = b.din("attnC_o", (DM, SEQ), F32)
    xiF = b.din("xiF", (DM, SEQ), F32)
    yout = b.dout("y_out", (DM, SEQ), F32)
    bw = b.win("l1_bwT", g["l1_bwT"])
    cw = b.win("l1_cwT", g["l1_cwT"])
    wg = b.win("l1_wgT", g["l1_wgT"])
    wu = b.win("l1_wuT", g["l1_wuT"])
    wd = b.win("l1_wdT", g["l1_wdT"])
    b.alloc_x()
    b.phase_combine_split(xin, aC, xiF)
    d_ap, dR = b.small("l1_D", g["l1_D"], F32)
    b.phase_ln1(d_ap, dR)
    b.phase_l1attn(g["l1_ap"], bw, cw)
    b.phase_l1ffn(wg, wu, wd)
    b.store_x(yout)
    return b

def _build_fused(g):
    b = Build(["l0head", "l0ffn", "l1attn"]).setup()
    xin = b.din("x_in", (DM, SEQ), F32)
    yout = b.dout("y_out", (DM, SEQ), F32)
    wq = b.win("l0a_wqkvT", g["l0a_wqkvT"])
    wo = b.win("l0a_woutT", g["l0a_woutT"])
    hs = _l0ffn_handles(b, g)
    bw = b.win("l1_bwT", g["l1_bwT"])
    cw = b.win("l1_cwT", g["l1_cwT"])
    wg = b.win("l1_wgT", g["l1_wgT"])
    wu = b.win("l1_wuT", g["l1_wuT"])
    wd = b.win("l1_wdT", g["l1_wdT"])
    attnC = b.dint("attnC", (DM, SEQ), F32)
    b.alloc_xi()
    b.phase_ln0(xin)
    aCR = b.phase_l0attn(wq, wo, attnC)
    b.phase_l0ffn(hs)
    b.phase_combine_fused(xin, attnC.rearrange("(g p) t -> p g t", p=128),
                          aCR)
    d_ap, dR = b.small("l1_D", g["l1_D"], F32)
    b.phase_ln1(d_ap, dR)
    b.phase_l1attn(g["l1_ap"], bw, cw)
    b.phase_l1ffn(wg, wu, wd)
    b.store_x(yout)
    return b

def kernel(**inputs):
    g, xs = host_prepare(inputs)
    per_core = [{"x_in": xs[c]} for c in range(N_CORES)]
    if FUSED:
        b = _build_fused(g)
        rs = b.run(per_core, ["y_out"])
        return np.stack([np.ascontiguousarray(r["y_out"].T)
                         for r in rs]).astype(np.float32)
    b1 = _build_l0head(g)
    r1 = b1.run(per_core, ["xn0", "attnC_o"])
    b2 = _build_l0ffn(g)
    r2 = b2.run([{"xn0": r1[c]["xn0"]} for c in range(N_CORES)], ["xiF"])
    b3 = _build_l1(g)
    r3 = b3.run([{"x_in": xs[c], "attnC_o": r1[c]["attnC_o"],
                  "xiF": r2[c]["xiF"]} for c in range(N_CORES)], ["y_out"])
    return np.stack([np.ascontiguousarray(r3[c]["y_out"].T)
                     for c in range(N_CORES)]).astype(np.float32)
